# version 15 (final)
# baseline (speedup 1.0000x reference)
import math
import numpy as np
import concourse.bass as bass
import concourse.mybir as mybir

F32 = mybir.dt.float32
BF16 = mybir.dt.bfloat16
I8 = mybir.dt.int8
AF = mybir.ActivationFunctionType
ALU = mybir.AluOpType
AX = mybir.AxisListType

E = 2048
S = 4096
NE = 16
NTS = 8
TS = 512
INTER = 8192
NIO = 64
EPS = 1e-5
WIN = 128
HOPS = 2
ALPHA = 0.7
KEEP_K = int(0.33 * S)
SCALE_D = 1.0 / math.sqrt(128.0)
NEGBIG = -30000.0


class Sem:
    def __init__(self, h):
        self.h = h
        self.n = 0

    def bump(self, k=1):
        self.n += k
        return self.n


class MK:

    ENGINES = ("tensor", "vector", "scalar", "gpsimd", "sync")

    def __init__(self, nc):
        self.nc = nc
        self.q = {e: [] for e in self.ENGINES}
        self.sems = {}
        self._ctx = []
        self._phase_ctx = []

    CHAINED = ("vector", "scalar", "gpsimd")

    def op(self, eng, method, *args, inc=None, **kwargs):
        if inc is not None and not isinstance(inc, tuple):
            inc = (inc, 1)
        if eng in self.CHAINED and method != "dma_start":
            assert inc is None
            ctr = self.sem("ctr_" + eng)
            if ctr.n > 0:
                self.q[eng].append(
                    lambda e, _s=ctr.h, _v=ctr.n: e.wait_ge(_s, _v))
            inc = (ctr, 1)

        def fn(e, _m=method, _a=args, _k=kwargs, _i=inc):
            ins = getattr(e, _m)(*_a, **_k)
            if _i is not None:
                ins.then_inc(_i[0].h, _i[1])
        self.q[eng].append(fn)
        if inc is not None:
            inc[0].bump(inc[1])
            return (inc[0], inc[0].n)
        return None

    def waitt(self, eng, tok):
        if tok is not None:
            self.wait(eng, tok[0], tok[1])

    def wait(self, eng, sem, val=None):
        v = sem.n if val is None else val
        if v > 0:
            self.q[eng].append(lambda e, _s=sem.h, _v=v: e.wait_ge(_s, _v))

    def tick(self, eng, sem, k=1):
        self.op(eng, "nop", inc=(sem, k))

    def dma(self, q, sem, out, in_):
        return self.op(q, "dma_start", out=out, in_=in_, inc=(sem, 16))

    def dma_p(self, q, out, in_):
        if not hasattr(self, "_dma_rr"):
            self._dma_rr = {}
        i = self._dma_rr[q] = (self._dma_rr.get(q, -1) + 1) % 8
        sem = self.sem(f"dpool_{q}{i}")
        if sem.n > 0:
            self.wait(q, sem, sem.n)
        return self.op(q, "dma_start", out=out, in_=in_, inc=(sem, 16))

    def barrier(self, engines=("tensor", "vector", "scalar")):
        bar = self.sem("bar")
        for e in engines:
            self.op(e, "nop", inc=(bar, 1))
        tgt = bar.n
        for e in engines:
            self.wait(e, bar, tgt)

    def play(self):
        nc = self.nc
        with nc.Block() as block:
            @block.tensor
            def _(eng):
                for fn in self.q["tensor"]:
                    fn(eng)

            @block.vector
            def _(eng):
                for fn in self.q["vector"]:
                    fn(eng)

            @block.scalar
            def _(eng):
                for fn in self.q["scalar"]:
                    fn(eng)

            @block.gpsimd
            def _(eng):
                for fn in self.q["gpsimd"]:
                    fn(eng)

            @block.sync
            def _(eng):
                for fn in self.q["sync"]:
                    fn(eng)

    def sem(self, name):
        if name not in self.sems:
            g = self.nc.semaphore(name)
            h = g.__enter__()
            self._ctx.append(g)
            self.sems[name] = Sem(h)
        return self.sems[name]

    def sbuf(self, name, shape, dtype, phase=False, scope=None):
        g = self.nc.sbuf_tensor(name, shape, dtype)
        t = g.__enter__()
        if scope is not None:
            scope.append(g)
        else:
            (self._phase_ctx if phase else self._ctx).append(g)
        return t

    def release(self, scope):
        for g in reversed(scope):
            g.__exit__(None, None, None)
        scope.clear()

    def psum(self, name, shape, dtype=F32, phase=True, scope=None):
        g = self.nc.psum_tensor(name, shape, dtype)
        t = g.__enter__()
        if scope is not None:
            scope.append(g)
        else:
            (self._phase_ctx if phase else self._ctx).append(g)
        return t

    def end_phase(self):
        for g in reversed(self._phase_ctx):
            g.__exit__(None, None, None)
        self._phase_ctx = []

    def close(self):
        self.end_phase()
        for g in reversed(self._ctx):
            g.__exit__(None, None, None)
        self._ctx = []



def build_consts(mk):
    c = {}
    c["ones_col"] = mk.sbuf("c_ones_col", [128, 1], BF16)
    c["ones_col_f32"] = mk.sbuf("c_ones_col32", [128, 1], F32)
    c["ones_row"] = mk.sbuf("c_ones_row", [1, 128], BF16)
    c["ones_row_f32"] = mk.sbuf("c_ones_row32", [1, 128], F32)
    c["ident"] = mk.sbuf("c_ident", [128, 128], BF16)
    c["ident32f"] = mk.sbuf("c_ident32f", [32, 32], F32)
    c["ident_f32"] = mk.sbuf("c_ident_f32", [128, 128], F32)
    c["tri_keep"] = mk.sbuf("c_tri_keep", [128, 128], BF16)
    c["low_strict"] = mk.sbuf("c_low_strict", [128, 128], BF16)

    mk.op("gpsimd", "memset", c["ones_col"][:], 1.0)
    mk.op("gpsimd", "memset", c["ones_col_f32"][:], 1.0)
    mk.op("gpsimd", "memset", c["ones_row"][:], 1.0)
    mk.op("gpsimd", "memset", c["ones_row_f32"][:], 1.0)
    mk.op("gpsimd", "memset", c["ident"][:], 0.0)
    mk.op("gpsimd", "memset", c["ident32f"][:], 0.0)
    mk.op("gpsimd", "memset", c["ident_f32"][:], 0.0)
    mk.op("gpsimd", "memset", c["tri_keep"][:], 0.0)
    mk.op("gpsimd", "memset", c["low_strict"][:], 0.0)
    mk.op("gpsimd", "affine_select", out=c["ident"][:], in_=c["ident"][:],
          compare_op=ALU.not_equal, fill=1.0, base=0,
          pattern=[[-1, 128]], channel_multiplier=1)
    mk.op("gpsimd", "affine_select", out=c["ident32f"][:], in_=c["ident32f"][:],
          compare_op=ALU.not_equal, fill=1.0, base=0,
          pattern=[[-1, 32]], channel_multiplier=1)
    mk.op("gpsimd", "affine_select", out=c["ident_f32"][:], in_=c["ident_f32"][:],
          compare_op=ALU.not_equal, fill=1.0, base=0,
          pattern=[[-1, 128]], channel_multiplier=1)
    mk.op("gpsimd", "affine_select", out=c["tri_keep"][:], in_=c["tri_keep"][:],
          compare_op=ALU.is_gt, fill=1.0, base=0,
          pattern=[[-1, 128]], channel_multiplier=1)
    tok = mk.op("gpsimd", "affine_select", out=c["low_strict"][:],
                in_=c["low_strict"][:], compare_op=ALU.is_ge, fill=1.0, base=0,
                pattern=[[1, 128]], channel_multiplier=-1)
    for e in ("tensor", "vector", "scalar", "sync"):
        mk.waitt(e, tok)
    return c



def ln_phase(mk, c, R_ap, A, tag, var_row=None, mbar=None, src_bf16=False,
             deps=(), row_toks=None, load_dep_half=None,
             cast_load=False):
    st = mk.sem(f"{tag}_t")
    for d in deps:
        if d is None:
            continue
        for e in ("tensor", "vector", "scalar", "gpsimd"):
            mk.waitt(e, d)

    sqb = mk.sbuf(f"{tag}_sqb", [128, S], BF16, phase=True)
    meanh = mk.sbuf(f"{tag}_meanh", [1, S // 2], F32, phase=True)
    scr = mk.sbuf(f"{tag}_scr", [1, 3 * TS], F32, phase=True)
    varh = mk.sbuf(f"{tag}_varh", [1, S // 2], F32, phase=True)
    tmp = mk.sbuf(f"{tag}_tmp", [128, 2 * TS], F32, phase=True)
    bcs = mk.sbuf(f"{tag}_bcs", [128, 2 * TS], F32, phase=True)
    if cast_load:
        rt2 = mk.sbuf(f"{tag}_rt2", [128, S], F32, phase=True)
    ps_sum = mk.psum(f"{tag}_ps_sum", [1, S // 2], F32)
    ps_bc = [mk.psum(f"{tag}_ps_bc{j}", [128, 2 * TS], F32)
             for j in range(2)]
    svr = mk.sem("sm_svr")

    cast_done = [None] * NE
    for i in range(NE):
        q_ = "gpsimd" if (cast_load and i % 2 == 0) else "sync"
        if load_dep_half is not None:
            mk.waitt(q_, load_dep_half[i // 8])
        if row_toks is not None and row_toks[i] is not None:
            mk.waitt(q_, row_toks[i])
        if cast_load and i % 2 == 1:
            if i >= 3:
                mk.waitt("sync", cast_done[i - 2])
            dtok = mk.dma_p("sync", rt2[:, :],
                            R_ap[i * 128:(i + 1) * 128, :])
            mk.waitt("vector", dtok)
            cast_done[i] = mk.op("vector", "tensor_copy",
                                 A[:, i * S:(i + 1) * S], rt2[:, :])
        else:
            cast_done[i] = mk.dma_p(q_, A[:, i * S:(i + 1) * S],
                                    R_ap[i * 128:(i + 1) * 128, :])
    for d in deps:
        if d is not None:
            mk.waitt("sync", d)

    xn_ts = [None] * NTS
    sq_done = {}
    mk.waitt("vector", cast_done[-1])
    for half in range(2):
        hts = range(half * 4, half * 4 + 4)
        if half == 1:
            mk.waitt("tensor", varh_tok)
        for i in range(NE):
            if half == 0:
                mk.waitt("tensor", cast_done[i])
            ch = A[:, i * S:(i + 1) * S]
            for j, t_ in enumerate(hts):
                last = (i == NE - 1 and j == 3)
                mk.op("tensor", "matmul", ps_sum[0:1, j * TS:(j + 1) * TS],
                      c["ones_col"][:], ch[:, t_ * TS:(t_ + 1) * TS],
                      start=(i == 0), stop=(i == NE - 1),
                      inc=(st if last else None))
        mk.wait("vector", st, st.n)
        mean_tok = mk.op("vector", "tensor_scalar", meanh[0:1, :],
                         ps_sum[0:1, :], 1.0 / E, None, op0=ALU.mult)
        S2 = S // 2
        for i in range(NE):
            ch = A[:, i * S + half * S2: i * S + (half + 1) * S2]
            gi = half * NE + i
            slot = sqb[:, (i % 2) * S2:(i % 2 + 1) * S2]
            if gi >= 2:
                mk.wait("vector", st, sq_done[gi - 2])
            sq_tok = mk.op("vector", "tensor_mul", slot, ch, ch)
            mk.waitt("tensor", sq_tok)
            if i == 0:
                mk.waitt("tensor", mean_tok)
            for j, t_ in enumerate(hts):
                mk.op("tensor", "matmul", ps_sum[0:1, j * TS:(j + 1) * TS],
                      c["ones_col"][:], slot[:, j * TS:(j + 1) * TS],
                      start=(i == 0), stop=(i == NE - 1),
                      inc=(st if j == 3 else None))
            sq_done[gi] = st.n
        mk.wait("vector", st, st.n)
        varh_tok = mk.op("vector", "tensor_scalar", varh[0:1, :],
                         ps_sum[0:1, :], 1.0 / E, None, op0=ALU.mult)
        if half == 0:
            bufs_early = varh_tok
        else:
            psum_free = varh_tok

        v_, s_, r_ = (scr[0:1, 0:TS], scr[0:1, TS:2 * TS],
                      scr[0:1, 2 * TS:3 * TS])
        for j, t_ in enumerate(hts):
            pb = ps_bc[t_ % 2]
            cm = bcs[:, 0:TS]
            cr = bcs[:, TS:2 * TS]
            tsl = slice(t_ * TS, (t_ + 1) * TS)
            x_ = tmp[0:1, 0:TS]
            mk.op("vector", "tensor_mul", x_, meanh[0:1, j * TS:(j + 1) * TS],
                  meanh[0:1, j * TS:(j + 1) * TS])
            vtok = mk.op("vector", "tensor_sub", v_,
                         varh[0:1, j * TS:(j + 1) * TS], x_)
            if var_row is not None:
                mk.waitt("sync", vtok)
                mk.dma("sync", svr, var_row[0:1, tsl], v_)
                mk.wait("vector", svr, svr.n)
            ve_tok = mk.op("vector", "tensor_scalar", v_, v_, EPS, None,
                           op0=ALU.add)
            mk.waitt("scalar", ve_tok)
            sq2 = mk.op("scalar", "activation", s_, v_, AF.Sqrt)
            mk.waitt("vector", sq2)
            mk.op("vector", "reciprocal", r_, s_)
            for it in range(2):
                mk.op("vector", "tensor_mul", x_, r_, r_)
                mk.op("vector", "tensor_mul", x_, x_, v_)
                mk.op("vector", "tensor_scalar", x_, x_, -0.5, 1.5,
                      op0=ALU.mult, op1=ALU.add)
                rows_tok = mk.op("vector", "tensor_mul", r_, r_, x_)
            mk.waitt("tensor", rows_tok)
            if t_ >= 2:
                mk.waitt("tensor", xn_ts[t_ - 2])
            mk.op("tensor", "matmul", pb[:, 0:TS], c["ones_row_f32"][:],
                  meanh[0:1, j * TS:(j + 1) * TS], start=True, stop=True,
                  skip_group_check=True)
            mk.op("tensor", "matmul", pb[:, TS:2 * TS], c["ones_row_f32"][:],
                  r_, start=True, stop=True, inc=st, skip_group_check=True)
            mk.wait("vector", st, st.n)
            mk.op("vector", "tensor_copy", bcs[:, :], pb[:, 0:2 * TS])
            for i in range(NE):
                last = (i == NE - 1)
                ach = A[:, i * S + t_ * TS: i * S + (t_ + 1) * TS]
                f32t = tmp[:, 0:TS]
                mk.op("vector", "tensor_sub", f32t, ach, cm)
                if mbar is None:
                    tok = mk.op("vector", "tensor_mul", ach, f32t, cr)
                else:
                    x32 = tmp[:, TS:2 * TS]
                    mk.op("vector", "tensor_mul", x32, f32t, cr)
                    mk.op("vector", "tensor_copy", ach, x32)
                    mk.op("vector", "reduce_sum", tmp[:, 0:1], x32, axis=AX.X)
                    if t_ == 0:
                        tok = mk.op("vector", "tensor_scalar", mbar[:, i:i + 1],
                                    tmp[:, 0:1], 1.0 / S, None, op0=ALU.mult)
                    else:
                        mk.op("vector", "tensor_scalar", tmp[:, 0:1],
                              tmp[:, 0:1], 1.0 / S, None, op0=ALU.mult)
                        tok = mk.op("vector", "tensor_add", mbar[:, i:i + 1],
                                    mbar[:, i:i + 1], tmp[:, 0:1])
            xn_ts[t_] = tok

    mk.end_phase()
    return {"xn_ts": xn_ts, "xn_done": xn_ts[-1], "psum_free": psum_free,
            "bufs_early": bufs_early, "mrow_tok": None}



def mm_prealloc(mk, tag, n_ei, ns, scope):
    return [mk.sbuf(f"{tag}_wb{s}", [128, 2 * n_ei * 128], BF16, scope=scope)
            for s in range(ns)]


def mm_pass(mk, c, tag, movs, wstreams, n_eo, evac, mov_tok=None, deps=(),
            mov_ts=None, psum_dep=None, wdep=(), prealloc=None):
    n_ei = len(movs)
    ns = len(wstreams)
    st = mk.sem("sm_t")
    wld = [[mk.sem(f"sm_w{s}_{j}") for j in range(2)] for s in range(ns)]
    wbufs = prealloc if prealloc is not None else [
        mk.sbuf(f"{tag}_wb{s}", [128, 2 * n_ei * 128], BF16, phase=True)
        for s in range(ns)]
    pss = [mk.psum(f"{tag}_ps{s}", [128, 2 * 1024], F32) for s in range(ns)]

    if mov_tok is not None:
        mk.waitt("tensor", mov_tok)
        if prealloc is None:
            mk.waitt("gpsimd", mov_tok)
    for d in wdep:
        if d is not None:
            mk.waitt("gpsimd", d)
    if psum_dep is not None:
        mk.waitt("tensor", psum_dep)
    for d in deps:
        if d is None:
            continue
        mk.waitt("tensor", d)
        if prealloc is None:
            mk.waitt("gpsimd", d)
    pe_groups_done = [0] * (n_eo + 1)
    evac_toks = {}
    wtoks = [None] * ns
    for eo in range(n_eo):
        for s in range(ns):
            if eo >= 2:
                mk.wait("gpsimd", st, pe_groups_done[eo - 2])
            src = wstreams[s]["ap"][:, eo * 128:(eo + 1) * 128].rearrange(
                "(a p) c -> p a c", p=128)
            dst = wbufs[s][:, (eo % 2) * n_ei * 128:
                           (eo % 2 + 1) * n_ei * 128].rearrange(
                "p (a c) -> p a c", c=128)
            wtoks[s] = mk.dma("gpsimd", wld[s][eo % 2], dst, src)
        for s in range(ns):
            mk.waitt("tensor", wtoks[s])
        for q in range(4):
            sl = q % 2
            if eo == 0 and mov_ts is not None:
                mk.waitt("tensor", mov_ts[2 * q + 1])
            for s in range(ns):
                prev = (eo, q - 2, s) if q >= 2 else (eo - 1, q + 2, s)
                if prev in evac_toks:
                    mk.waitt("tensor", evac_toks[prev])
                wb = wbufs[s][:, (eo % 2) * n_ei * 128:
                              (eo % 2 + 1) * n_ei * 128]
                ps = pss[s][:, sl * 1024:(sl + 1) * 1024]
                for ei in range(n_ei):
                    wc = wb[:, ei * 128:(ei + 1) * 128]
                    for u in range(2):
                        t_ = 2 * q + u
                        last = (ei == n_ei - 1 and u == 1 and s == ns - 1)
                        mk.op("tensor", "matmul",
                              ps[:, u * TS:(u + 1) * TS], wc,
                              movs[ei][:, t_ * TS:(t_ + 1) * TS],
                              start=(ei == 0), stop=(ei == n_ei - 1),
                              inc=(st if last else None))
            pe_groups_done[eo] = st.n
            toks = evac(eo, q, [pss[s][:, sl * 1024:(sl + 1) * 1024]
                                for s in range(ns)], (st, st.n))
            if not isinstance(toks, list):
                toks = [toks] * ns
            for s in range(ns):
                evac_toks[(eo, q, s)] = toks[s]
                last_tok = toks[s]
    mk.end_phase()
    return last_tok


def qkv_evac_factory(mk, c, tag, out_hbm, bias_col=None, vtrans=False,
                     n_eo=16, scope=None):
    stg = mk.sbuf(f"{tag}_stg", [128, S], BF16, phase=(scope is None),
                  scope=scope)
    sd = [mk.sem(f"sm_sd{j}") for j in range(2)]
    state = {"st": None, "tr": None}
    if vtrans:
        pst = [mk.psum(f"{tag}_pt{j}", [128, 128], BF16) for j in range(2)]
        vstg = mk.sbuf(f"{tag}_vstg", [128, S], BF16, phase=True)
        stv = mk.sem("sm_tv")

    def evac(eo, q, ps_list, pe_tok):
        ps = ps_list[0]
        out = stg[:, q * 1024:(q + 1) * 1024]
        mk.waitt("vector", pe_tok)
        if eo >= 1 and q == 0:
            if vtrans:
                mk.waitt("vector", state["tr"])
            else:
                mk.waitt("vector", state["st"])
        if bias_col is None:
            tok = mk.op("vector", "tensor_copy", out, ps)
        else:
            tok = mk.op("vector", "tensor_scalar", out, ps,
                        bias_col[:, eo:eo + 1], None, op0=ALU.add)
        if q < 3:
            return tok
        if not vtrans:
            mk.waitt("sync", tok)
            state["st"] = mk.dma("sync", sd[0],
                                 out_hbm[eo * 128:(eo + 1) * 128, :],
                                 stg[:, :])
            return tok
        mk.waitt("tensor", tok)
        if eo >= 1:
            mk.waitt("vector", state["st"])
        ctoks = [None, None]
        for blk in range(32):
            pj = blk % 2
            if blk >= 2:
                mk.waitt("tensor", ctoks[pj])
            mk.op("tensor", "transpose", pst[pj][:],
                  stg[:, blk * 128:(blk + 1) * 128],
                  c["ident"][:], inc=stv)
            mk.wait("vector", stv, stv.n)
            ctoks[pj] = mk.op("vector", "tensor_copy",
                              vstg[:, blk * 128:(blk + 1) * 128], pst[pj][:])
        mk.waitt("sync", ctoks[1])
        dst = out_hbm[:, eo * 128:(eo + 1) * 128].rearrange(
            "(b p) c -> p b c", p=128)
        state["st"] = mk.dma("sync", sd[1], dst, vstg[:, :].rearrange(
            "p (b c) -> p b c", c=128))
        state["tr"] = ctoks[1]
        return ctoks[1]

    def finish():
        mk.wait("sync", sd[0], sd[0].n)
        mk.wait("sync", sd[1], sd[1].n)
        return [(sd[0], sd[0].n), (sd[1], sd[1].n)]
    return evac, finish


def load_vec_cols(mk, c, tag, vec_ap, out_cols, n, ps_share=None):
    tmp = mk.sbuf(f"{tag}_lvtmp", [n, 128], F32)
    ps = ps_share if ps_share is not None else mk.psum(f"{tag}_lvp",
                                                       [128, n], F32)
    sdl = mk.sem(f"lv_{tag}_d")
    stl = mk.sem("sm_lvt")
    mk.dma("sync", sdl, tmp[:, :], vec_ap.rearrange("(a p) -> a p", p=128))
    mk.wait("tensor", sdl, sdl.n)
    mk.op("tensor", "matmul", ps[:, 0:n], tmp[:, :], c["ident_f32"][0:n, 0:n],
          start=True, stop=True, inc=stl)
    mk.wait("vector", stl, stl.n)
    return mk.op("vector", "tensor_copy", out_cols[:, 0:n], ps[:, 0:n])


def attn_core(mk, c, tag, QT_s, KT_s, V_s, A, nheads=16, kv_group=1,
              lm01_cols=None, lm_bias_cols=None, deps=()):
    st = mk.sem("sm_t")
    qld = [mk.sem(f"sm_ql{j}") for j in range(2)]
    kld = [mk.sem(f"sm_kl{j}") for j in range(2)]
    vld = [mk.sem(f"sm_vl{j}") for j in range(2)]
    NSL = 3

    qt = mk.sbuf(f"{tag}_qt", [128, 2 * S], BF16, phase=True)
    kt = mk.sbuf(f"{tag}_kt", [128, 2 * S], BF16, phase=True)
    vh = mk.sbuf(f"{tag}_vh", [128, 2 * S], BF16, phase=True)
    pt = mk.sbuf(f"{tag}_pt", [128, NSL * 1024], BF16, phase=True)
    sq = mk.sbuf(f"{tag}_sq", [128, S], BF16, phase=True)
    sm = mk.sbuf(f"{tag}_sm", [1, 16], F32, phase=True)
    nbc = mk.sbuf(f"{tag}_nbc", [128, 2], F32, phase=True)
    bh = mk.sbuf(f"{tag}_bh", [128, 32], F32, phase=True)
    tk = mk.sbuf(f"{tag}_tk", [128, 128], BF16, phase=True)
    rr = mk.sbuf(f"{tag}_rr", [1, TS], F32, phase=True)
    rdb = mk.sbuf(f"{tag}_rdb", [128, TS], F32, phase=True)

    ps_st = mk.psum(f"{tag}_ps_st", [128, NSL * 1024], F32)
    ps_ot = mk.psum(f"{tag}_ps_ot", [128, TS], F32)
    ps_den = mk.psum(f"{tag}_ps_den", [1, TS], F32)
    ps_bc = ps_st[:, 2 * 1024: 2 * 1024 + TS]
    ps_n4 = ps_st[0:1, 0:4 * TS]

    for d in deps:
        if d is None:
            continue
        mk.waitt("tensor", d)
        mk.waitt("vector", d)
        mk.waitt("scalar", d)
        mk.waitt("sync", d)

    q2, k2, mm_, tsc = sm[0:1, 0:1], sm[0:1, 1:2], sm[0:1, 2:3], sm[0:1, 3:4]
    pe_head_done = [0] * (nheads + 2)
    dve_head_done = [None] * (nheads + 2)
    qtoks = [None, None]
    ktoks = [None, None]
    vtoks = [None, None]
    last_ot_tok = None
    prev_act_tok = None

    def maxnorm2(dst, src_sb, sq_tok):
        mk.op("vector", "memset", dst[:, :], 0.0)
        tok = mk.op("vector", "tensor_mul", sq[:], src_sb, src_sb)
        for r_ in range(2):
            mk.waitt("tensor", tok)
            for t4 in range(4):
                t_ = r_ * 4 + t4
                mk.op("tensor", "matmul", ps_n4[0:1, t4 * TS:(t4 + 1) * TS],
                      c["ones_col"][:], sq[:, t_ * TS:(t_ + 1) * TS],
                      start=True, stop=True, skip_group_check=True,
                      inc=(st if t4 == 3 else None))
            mk.wait("vector", st, st.n)
            mk.op("vector", "reduce_max", tsc, ps_n4[0:1, :], axis=AX.X)
            tok = mk.op("vector", "tensor_max", dst, dst, tsc)
        return tok

    for h in range(nheads):
        kv = h // kv_group
        hs = h % 2
        ks = kv % 2
        if h >= 2:
            mk.wait("sync", st, pe_head_done[h - 2])
            mk.waitt("sync", dve_head_done[h - 2])
        qtoks[hs] = mk.dma("sync", qld[hs], qt[:, hs * S:(hs + 1) * S],
                           QT_s[h * 128:(h + 1) * 128, :])
        if kv_group == 1 or h % 2 == 0:
            ktoks[ks] = mk.dma("sync", kld[ks], kt[:, ks * S:(ks + 1) * S],
                               KT_s[kv * 128:(kv + 1) * 128, :])
            vtoks[ks] = mk.dma("sync", vld[ks], vh[:, ks * S:(ks + 1) * S],
                               V_s[:, kv * 128:(kv + 1) * 128].rearrange(
                                   "(b p) c -> p b c", p=128))
        qth = qt[:, hs * S:(hs + 1) * S]
        kth = kt[:, ks * S:(ks + 1) * S]
        vhh = vh[:, ks * S:(ks + 1) * S]
        if prev_act_tok is not None:
            mk.waitt("vector", prev_act_tok)
        mk.waitt("vector", qtoks[hs])
        sq_tok = maxnorm2(q2, qth, None)
        if kv_group == 1 or h % 2 == 0:
            mk.waitt("vector", ktoks[ks])
            sq_tok = maxnorm2(k2, kth, None)
        t_mm = mk.op("vector", "tensor_mul", mm_, q2, k2)
        mk.waitt("scalar", t_mm)
        mk.op("scalar", "activation", tsc, mm_, AF.Ln)
        sq2 = mk.op("scalar", "activation", mm_, tsc, AF.Exp, scale=0.5)
        mk.waitt("vector", sq2)
        neg_tok = mk.op("vector", "tensor_scalar", mm_, mm_,
                        -SCALE_D * 1.001, None, op0=ALU.mult)
        mk.waitt("tensor", neg_tok)
        mk.op("tensor", "matmul", ps_bc[:, 0:1], c["ones_row_f32"][:],
              mm_, start=True, stop=True, inc=st, skip_group_check=True)
        mk.wait("vector", st, st.n)
        nb_tok = mk.op("vector", "tensor_copy", nbc[:, 0:1], ps_bc[:, 0:1])
        if lm_bias_cols is not None:
            nb_tok = mk.op("vector", "tensor_scalar", bh[:], lm_bias_cols[:],
                           nbc[:, 0:1], None, op0=ALU.add)
        mk.waitt("scalar", nb_tok)

        mk.waitt("tensor", vtoks[ks])
        st_war = [nb_tok, None, None]
        pt_war = [0] * NSL
        ot_war = last_ot_tok

        def make_tail(h_, qs_, pv_done_val):
            st_ = {"rc": None}

            def stage_rc():
                mk.wait("vector", st, pv_done_val)
                st_["rc"] = mk.op("vector", "tensor_copy", rr[0:1, :],
                                  ps_den[0:1, :])

            def stage_rest():
                nonlocal last_ot_tok, ot_war
                mk.waitt("tensor", st_["rc"])
                mk.op("tensor", "matmul", ps_bc[:, :], c["ones_row_f32"][:],
                      rr[0:1, :], start=True, stop=True, inc=st,
                      skip_group_check=True)
                mk.wait("vector", st, st.n)
                rdb_tok = mk.op("vector", "reciprocal", rdb[:], ps_bc[:, :])
                last_ot_tok = mk.op(
                    "vector", "tensor_mul",
                    A[:, h_ * S + qs_ * TS: h_ * S + (qs_ + 1) * TS],
                    ps_ot[:, :], rdb[:])
                ot_war = last_ot_tok
                st_war[2] = rdb_tok
            return stage_rc, stage_rest

        pend_tail = None
        for qs in range(NTS):
            nkb = 4 * qs + 4
            nslab = (nkb + 1) // 2
            exp_toks = {}
            fix_toks = {}
            if pend_tail is not None:
                pend_tail[0]()
            for kb2 in range(nslab + 1):
                if kb2 < nslab:
                    sl = kb2 % NSL
                    kbs = [2 * kb2, 2 * kb2 + 1]
                    if st_war[sl] is not None:
                        mk.waitt("tensor", st_war[sl])
                    for j, kb in enumerate(kbs):
                        mk.op("tensor", "matmul",
                              ps_st[:, (sl * 2 + j) * TS:(sl * 2 + j + 1) * TS],
                              kth[:, kb * 128:(kb + 1) * 128],
                              qth[:, qs * TS:(qs + 1) * TS],
                              start=True, stop=True, skip_group_check=True,
                              inc=(st if j == len(kbs) - 1 else None))
                    st_done = st.n
                    mk.wait("scalar", st, st_done)
                    mk.wait("scalar", st, pt_war[sl])
                    win_lo = 4 * qs - 1
                    if lm_bias_cols is None:
                        e_tok = mk.op(
                            "scalar", "activation",
                            pt[:, sl * 1024:(sl + 1) * 1024],
                            ps_st[:, sl * 2 * TS:(sl * 2 + 2) * TS],
                            AF.Exp, bias=nbc[:, 0:1], scale=SCALE_D)
                        for kb in kbs:
                            exp_toks[kb] = e_tok
                    else:
                        for j, kb in enumerate(kbs):
                            bias = (nbc[:, 0:1] if kb >= win_lo
                                    else bh[:, kb:kb + 1])
                            exp_toks[kb] = mk.op(
                                "scalar", "activation",
                                pt[:, (sl * 2 + j) * TS:(sl * 2 + j + 1) * TS],
                                ps_st[:, (sl * 2 + j) * TS:(sl * 2 + j + 1) * TS],
                                AF.Exp, bias=bias, scale=SCALE_D)
                    st_war[sl] = exp_toks[kbs[-1]]
                    prev_act_tok = st_war[sl]
                    for j, kb in enumerate(kbs):
                        ptk = pt[:, (sl * 2 + j) * TS:(sl * 2 + j + 1) * TS]
                        jd = kb - 4 * qs
                        tok = None
                        if jd >= 0:
                            mk.waitt("vector", exp_toks[kb])
                            if jd > 0:
                                tok = mk.op("vector", "memset",
                                            ptk[:, 0:jd * 128], 0.0)
                            tok = mk.op("vector", "tensor_mul",
                                        ptk[:, jd * 128:(jd + 1) * 128],
                                        ptk[:, jd * 128:(jd + 1) * 128],
                                        c["tri_keep"][:])
                            if lm_bias_cols is not None:
                                if jd < 3:
                                    mk.op("vector", "tensor_scalar", tk[:],
                                          c["low_strict"][:],
                                          lm01_cols[:, kb:kb + 1], None,
                                          op0=ALU.max)
                                    tok = mk.op(
                                        "vector", "tensor_mul",
                                        ptk[:, (jd + 1) * 128:(jd + 2) * 128],
                                        ptk[:, (jd + 1) * 128:(jd + 2) * 128],
                                        tk[:])
                                if jd < 2:
                                    tok = mk.op(
                                        "vector", "tensor_scalar",
                                        ptk[:, (jd + 2) * 128:],
                                        ptk[:, (jd + 2) * 128:],
                                        lm01_cols[:, kb:kb + 1], None,
                                        op0=ALU.mult)
                        elif lm_bias_cols is not None and jd == -1:
                            mk.waitt("vector", exp_toks[kb])
                            mk.op("vector", "tensor_scalar", tk[:],
                                  c["low_strict"][:],
                                  lm01_cols[:, kb:kb + 1], None, op0=ALU.max)
                            mk.op("vector", "tensor_mul", ptk[:, 0:128],
                                  ptk[:, 0:128], tk[:])
                            tok = mk.op("vector", "tensor_scalar",
                                        ptk[:, 128:], ptk[:, 128:],
                                        lm01_cols[:, kb:kb + 1], None,
                                        op0=ALU.mult)
                        fix_toks[kb] = tok
                if kb2 >= 1:
                    pl = (kb2 - 1) % NSL
                    pkbs = [2 * (kb2 - 1), 2 * kb2 - 1]
                    if kb2 == 1 and pend_tail is not None:
                        pend_tail[1]()
                        pend_tail = None
                    if kb2 == 1 and ot_war is not None:
                        mk.waitt("tensor", ot_war)
                    for j, kb in enumerate(pkbs):
                        if fix_toks[kb] is not None:
                            mk.waitt("tensor", fix_toks[kb])
                        else:
                            mk.waitt("tensor", exp_toks[kb])
                        ptk = pt[:, (pl * 2 + j) * TS:(pl * 2 + j + 1) * TS]
                        mk.op("tensor", "matmul", ps_den[0:1, :],
                              c["ones_col"][:], ptk,
                              start=(kb == 0), stop=(kb == nkb - 1),
                              skip_group_check=True)
                        mk.op("tensor", "matmul", ps_ot[:, :],
                              vhh[:, kb * 128:(kb + 1) * 128], ptk,
                              start=(kb == 0), stop=(kb == nkb - 1),
                              skip_group_check=True,
                              inc=(st if j == len(pkbs) - 1 else None))
                    pt_war[pl] = st.n
            pend_tail = make_tail(h, qs, st.n)
        pend_tail[0]()
        pend_tail[1]()
        pend_tail = None
        pe_head_done[h] = st.n
        dve_head_done[h] = last_ot_tok
    mk.end_phase()
    return last_ot_tok


def resid_evac_factory(mk, c, tag, R_in, R_out, bias_col=None, n_eo=16,
                       deps=(), rb_out=None):
    rt = mk.sbuf(f"{tag}_rt", [128, 2 * S], F32, phase=True)
    rn = mk.sbuf(f"{tag}_rn", [128, 2 * S], F32, phase=True)
    rld = [mk.sem(f"sm_rl{j}") for j in range(2)]
    sd = [mk.sem(f"sm_sd{j}") for j in range(2)]
    state = {"ld": [None, None], "st": [None, None], "rb": [None, None],
             "rb_rows": [None] * n_eo}
    srb = [mk.sem("sm_srb0"), mk.sem("sm_srb1")]
    for d in deps:
        if d is not None:
            mk.waitt("sync", d)

    state["ld"][0] = mk.dma("sync", rld[0], rt[:, 0:S], R_in[0:128, :])
    state["ld"][1] = mk.dma("sync", rld[1], rt[:, S:2 * S], R_in[128:256, :])

    def evac(eo, q, ps_list, pe_tok):
        ps = ps_list[0]
        sl = eo % 2
        out = rn[:, sl * S + q * 1024: sl * S + (q + 1) * 1024]
        rts = rt[:, sl * S + q * 1024: sl * S + (q + 1) * 1024]
        mk.waitt("vector", pe_tok)
        mk.waitt("vector", state["ld"][sl])
        if eo >= 2 and q == 0:
            mk.waitt("vector", state["st"][sl])
            mk.waitt("vector", state["rb"][sl])
        tok = mk.op("vector", "scalar_tensor_tensor", out, ps,
                    (bias_col[:, eo:eo + 1] if bias_col is not None else 0.0),
                    rts, op0=ALU.add, op1=ALU.add)
        if q < 3:
            return tok
        mk.waitt("sync", tok)
        state["st"][sl] = mk.dma("sync", sd[sl],
                                 R_out[eo * 128:(eo + 1) * 128, :],
                                 rn[:, sl * S:(sl + 1) * S])
        if rb_out is not None:
            mk.waitt("gpsimd", tok)
            state["rb"][sl] = mk.dma("gpsimd", srb[sl],
                                     rb_out[eo * 128:(eo + 1) * 128, :],
                                     rn[:, sl * S:(sl + 1) * S])
            state["rb_rows"][eo] = state["rb"][sl]
        if eo + 2 < n_eo:
            state["ld"][sl] = mk.dma("sync", rld[sl],
                                     rt[:, sl * S:(sl + 1) * S],
                                     R_in[(eo + 2) * 128:(eo + 3) * 128, :])
        return tok

    def finish():
        mk.wait("sync", sd[0], sd[0].n)
        mk.wait("sync", sd[1], sd[1].n)
        toks = [(sd[0], sd[0].n), (sd[1], sd[1].n)]
        if rb_out is not None:
            toks += [(srb[0], srb[0].n), (srb[1], srb[1].n)]
        return toks
    return evac, finish, state



def ffn1_evac_factory(mk, c, tag, H_s, sgate_col, sup_col, scope=None):
    gs = mk.sbuf(f"{tag}_gs", [128, 2 * 1024], F32, phase=(scope is None),
                 scope=scope)
    hst = mk.sbuf(f"{tag}_hst", [128, S], BF16, phase=(scope is None),
                  scope=scope)
    sd = [mk.sem(f"sm_sd{j}") for j in range(2)]
    state = {"st": [None, None]}

    def evac(eo, q, ps_list, pe_tok):
        ps_g, ps_u = ps_list
        gsl = gs[:, (q % 2) * 1024:(q % 2 + 1) * 1024]
        out = hst[:, q * 1024:(q + 1) * 1024]
        mk.waitt("scalar", pe_tok)
        a_tok = mk.op("scalar", "activation", gsl, ps_g, AF.Silu,
                      scale=sgate_col[:, eo:eo + 1])
        mk.waitt("vector", a_tok)
        if eo >= 1 and q == 0:
            mk.waitt("vector", state["st"][0])
        u_tok = mk.op("vector", "scalar_tensor_tensor", out, ps_u,
                      sup_col[:, eo:eo + 1], gsl, op0=ALU.mult, op1=ALU.mult)
        if q == 3:
            mk.waitt("sync", u_tok)
            state["st"][0] = mk.dma("sync", sd[eo % 2],
                                    H_s[eo * 128:(eo + 1) * 128, :],
                                    hst[:, :])
        return [a_tok, u_tok]

    def finish():
        mk.wait("sync", sd[0], sd[0].n)
        mk.wait("sync", sd[1], sd[1].n)
        return [(sd[0], sd[0].n), (sd[1], sd[1].n)]
    return evac, finish



def ffn2_phase(mk, c, tag, H_s, Wd_ap, R_in, R_out, sdown_col, A, deps=(),
               rb_out=None):
    st = mk.sem("sm_t")
    hld = [mk.sem(f"sm_hl{j}") for j in range(2)]
    wld = [mk.sem(f"sm_wl{j}") for j in range(2)]
    rld = [mk.sem(f"sm_rl{j}") for j in range(2)]
    sd = [mk.sem(f"sm_sd{j}") for j in range(2)]

    wb = mk.sbuf(f"{tag}_wb", [128, 2 * NIO * 128], BF16, phase=True)
    rt = mk.sbuf(f"{tag}_rt", [128, 2 * TS], F32, phase=True)
    rn = mk.sbuf(f"{tag}_rn", [128, 2 * TS], F32, phase=True)
    ps = mk.psum(f"{tag}_ps", [128, 2 * TS], F32)

    for d in deps:
        if d is None:
            continue
        for e in ("tensor", "vector", "sync", "gpsimd"):
            mk.waitt(e, d)

    HHALF = NIO * TS
    st_base = st.n
    pe_ts_done = [0] * (NTS + 2)
    ev_toks = {}
    htoks = [None, None]
    wtoks = [None, None]
    rtoks = [None, None]
    stoks2 = [None, None]
    rbtoks = [None, None]
    pend_rb = []
    rb_row_toks = [None] * NE
    srb = [mk.sem("sm_srb0"), mk.sem("sm_srb1")]

    def h_dma(ts):
        hsl = ts % 2
        if ts >= 2:
            mk.wait("sync", st, pe_ts_done[ts - 2])
        hsrc = H_s[:, ts * TS:(ts + 1) * TS].rearrange("(a p) c -> p a c",
                                                       p=128)
        hdst = A[:, hsl * HHALF:(hsl + 1) * HHALF].rearrange(
            "p (a c) -> p a c", c=TS)
        htoks[hsl] = mk.dma("sync", hld[hsl], hdst, hsrc)

    h_dma(0)
    for ts in range(NTS):
        hsl = ts % 2
        for eo in range(NE):
            if eo == 4 and ts + 1 < NTS:
                h_dma(ts + 1)
            wsl = eo % 2
            gi = ts * NE + eo
            if gi >= 2:
                mk.wait("gpsimd", st, st_base + gi - 1)
            wsrc = Wd_ap[:, eo * 128:(eo + 1) * 128].rearrange(
                "(a p) c -> p a c", p=128)
            wdst = wb[:, wsl * NIO * 128:(wsl + 1) * NIO * 128].rearrange(
                "p (a c) -> p a c", c=128)
            wtoks[wsl] = mk.dma("gpsimd", wld[wsl], wdst, wsrc)
            while pend_rb:
                ptok, psl_, peo, pts = pend_rb.pop(0)
                mk.waitt("gpsimd", ptok)
                rbtoks[psl_] = mk.dma("gpsimd", srb[psl_],
                                      rb_out[peo * 128:(peo + 1) * 128,
                                             pts * TS:(pts + 1) * TS],
                                      rn[:, psl_ * TS:(psl_ + 1) * TS])
                if pts == NTS - 1:
                    rb_row_toks[peo] = rbtoks[psl_]
            if gi >= 2:
                mk.waitt("sync", ev_toks[gi - 2])
            rtoks[wsl] = mk.dma("sync", rld[wsl],
                                rt[:, wsl * TS:(wsl + 1) * TS],
                                R_in[eo * 128:(eo + 1) * 128,
                                     ts * TS:(ts + 1) * TS])
            pass
            mk.waitt("tensor", wtoks[wsl])
            mk.waitt("tensor", htoks[hsl])
            prev = ev_toks.get(gi - 2)
            if prev is not None:
                mk.waitt("tensor", prev)
            psl = ps[:, wsl * TS:(wsl + 1) * TS]
            wbs = wb[:, wsl * NIO * 128:(wsl + 1) * NIO * 128]
            for i in range(NIO):
                mk.op("tensor", "matmul", psl,
                      wbs[:, i * 128:(i + 1) * 128],
                      A[:, hsl * HHALF + i * TS: hsl * HHALF + (i + 1) * TS],
                      start=(i == 0), stop=(i == NIO - 1),
                      inc=(st if i == NIO - 1 else None))
            mk.wait("vector", st, st.n)
            mk.waitt("vector", rtoks[wsl])
            if gi >= 2:
                mk.waitt("vector", stoks2[wsl])
                if rb_out is not None and rbtoks[wsl] is not None:
                    mk.waitt("vector", rbtoks[wsl])
            tok = mk.op("vector", "scalar_tensor_tensor",
                        rn[:, wsl * TS:(wsl + 1) * TS], psl,
                        sdown_col[:, eo:eo + 1], rt[:, wsl * TS:(wsl + 1) * TS],
                        op0=ALU.mult, op1=ALU.add)
            ev_toks[gi] = tok
            mk.waitt("sync", tok)
            stoks2[wsl] = mk.dma("sync", sd[wsl],
                                 R_out[eo * 128:(eo + 1) * 128,
                                       ts * TS:(ts + 1) * TS],
                                 rn[:, wsl * TS:(wsl + 1) * TS])
            if rb_out is not None:
                pend_rb.append((tok, wsl, eo, ts))
        pe_ts_done[ts] = st.n
    for ptok, psl_, peo, pts in pend_rb:
        mk.waitt("gpsimd", ptok)
        rbtoks[psl_] = mk.dma("gpsimd", srb[psl_],
                              rb_out[peo * 128:(peo + 1) * 128,
                                     pts * TS:(pts + 1) * TS],
                              rn[:, psl_ * TS:(psl_ + 1) * TS])
        if pts == NTS - 1:
            rb_row_toks[peo] = rbtoks[psl_]
    fin = [(sd[0], sd[0].n), (sd[1], sd[1].n)]
    if rb_out is not None:
        fin += [(srb[0], srb[0].n), (srb[1], srb[1].n)]
    for s_, v_ in fin:
        mk.wait("sync", s_, v_)
    mk.end_phase()
    half_free = [(st, pe_ts_done[NTS - 2]), (st, pe_ts_done[NTS - 1])]
    return ev_toks[NTS * NE - 1], fin, rb_row_toks, half_free


class MaskPipeline:

    def __init__(self, mk, c, tag, A, var_row, mbar, score_row, scope,
                 xn_tok):
        self.mk, self.c, self.tag = mk, c, tag
        self.A, self.var_row, self.mbar = A, var_row, mbar
        self.score_row = score_row
        self.xn_tok = xn_tok
        self.prodb = mk.sbuf(f"{tag}_prod", [128, S // 2], BF16, scope=scope)
        self.d32 = mk.sbuf(f"{tag}_d32", [128, 32], F32, scope=scope)
        self.dT = mk.sbuf(f"{tag}_dT", [32, 128], F32, scope=scope)
        self.rowin = mk.sbuf(f"{tag}_rin", [1, 2 * TS], F32, scope=scope)
        self.sc = mk.sbuf(f"{tag}_sc", [1, 64], F32, scope=scope)
        self.rowt = mk.sbuf(f"{tag}_rowt", [1, 4 * TS], F32, scope=scope)
        self.msq = mk.sbuf(f"{tag}_msq", [128, NE], F32, scope=scope)
        self.ps_dot = mk.psum(f"{tag}_psd", [128, 32], F32, scope=scope)
        self.ps_t = mk.psum(f"{tag}_pst", [32, 128], F32, scope=scope)
        self.dot_dram = mk.nc.dram_tensor(f"{tag}_dram", [S], F32,
                                          kind="Internal").ap()
        self.st = mk.sem(f"{tag}_t")
        self.sdd = mk.sem(f"{tag}_sdd")
        self.rls = mk.sem(f"{tag}_rls")
        self.i = 0
        self.st0 = None
        self.sc_tok = None
        self.bis = Bisect(mk, c, tag + "b", score_row, scope)

    def _dot(self, i):
        mk, c = self.mk, self.c
        if i == 0:
            self.st0 = self.st.n
            mk.waitt("vector", self.xn_tok)
            mk.waitt("tensor", self.xn_tok)
        for h in range(2):
            if i > 0 or h > 0:
                mk.wait("vector", self.st, self.st0 + 2 * i + h)
            ptok = mk.op("vector", "tensor_scalar", self.prodb[:],
                         self.A[:, i * S + h * (S // 2):
                                i * S + (h + 1) * (S // 2)],
                         self.mbar[:, i:i + 1], None, op0=ALU.mult)
            mk.waitt("tensor", ptok)
            for b in range(16):
                bb = h * 16 + b
                mk.op("tensor", "matmul", self.ps_dot[:, bb:bb + 1],
                      self.prodb[:, b * 128:(b + 1) * 128], c["ones_col"][:],
                      start=(i == 0), stop=(i == NE - 1),
                      inc=(self.st if b == 15 else None),
                      skip_group_check=True)

    def _trans(self):
        mk, c = self.mk, self.c
        mk.wait("vector", self.st, self.st.n)
        t1 = mk.op("vector", "tensor_copy", self.d32[:], self.ps_dot[:, :])
        mk.waitt("tensor", t1)
        mk.op("tensor", "transpose", self.ps_t[:, :], self.d32[:],
              c["ident_f32"][:], inc=self.st)
        mk.wait("vector", self.st, self.st.n)
        t2 = mk.op("vector", "tensor_copy", self.dT[:, :], self.ps_t[:, :])
        mk.waitt("sync", t2)
        mk.dma("sync", self.sdd,
               self.dot_dram.rearrange("(b p) -> b p", p=128), self.dT[:, :])

    def _mnorm(self):
        mk, c, sc = self.mk, self.c, self.sc
        mn, s0, s1 = sc[0:1, 1:2], sc[0:1, 2:3], sc[0:1, 3:4]
        msq_tok = mk.op("vector", "tensor_mul", self.msq[:], self.mbar[:],
                        self.mbar[:])
        mk.waitt("tensor", msq_tok)
        mk.op("tensor", "matmul", self.ps_t[0:1, 0:16], c["ones_col_f32"][:],
              self.msq[:], start=True, stop=True, inc=self.st,
              skip_group_check=True)
        mk.wait("vector", self.st, self.st.n)
        mn_tok = mk.op("vector", "reduce_sum", mn, self.ps_t[0:1, 0:16],
                       axis=AX.X)
        mk.waitt("scalar", mn_tok)
        sq_tok = mk.op("scalar", "activation", s0, mn, AF.Sqrt)
        mk.waitt("vector", sq_tok)
        for _ in range(2):
            mk.op("vector", "reciprocal", s1, s0)
            mk.op("vector", "tensor_mul", s1, s1, mn)
            mk.op("vector", "tensor_add", s0, s0, s1)
            mk.op("vector", "tensor_scalar", s0, s0, 0.5, None, op0=ALU.mult)

    def _score(self, ts):
        mk, c = self.mk, self.c
        rowt, rowin = self.rowt, self.rowin
        s0 = self.sc[0:1, 2:3]
        v_, r_, n_, x_ = (rowt[0:1, 0:TS], rowt[0:1, TS:2 * TS],
                          rowt[0:1, 2 * TS:3 * TS], rowt[0:1, 3 * TS:4 * TS])
        vin, din = rowin[0:1, 0:TS], rowin[0:1, TS:2 * TS]
        tsl = slice(ts * TS, (ts + 1) * TS)
        if ts == 0:
            mk.wait("sync", self.sdd, self.sdd.n)
        if self.sc_tok is not None:
            mk.waitt("sync", self.sc_tok)
        mk.dma("sync", self.rls, vin, self.var_row[0:1, tsl])
        rltok = mk.dma("sync", self.rls, din, self.dot_dram[None, tsl])
        mk.waitt("vector", rltok)
        mk.op("vector", "tensor_scalar", v_, vin, EPS, None, op0=ALU.add)
        mk.op("vector", "reciprocal", x_, v_)
        mk.op("vector", "tensor_mul", x_, x_, vin)
        wtok = mk.op("vector", "tensor_scalar", n_, x_, float(E), None,
                     op0=ALU.mult)
        mk.waitt("scalar", wtok)
        stok = mk.op("scalar", "activation", x_, n_, AF.Sqrt)
        mk.waitt("vector", stok)
        mk.op("vector", "reciprocal", r_, x_)
        for _ in range(2):
            mk.op("vector", "tensor_mul", x_, r_, r_)
            mk.op("vector", "tensor_mul", x_, x_, n_)
            mk.op("vector", "tensor_scalar", x_, x_, -0.5, 1.5,
                  op0=ALU.mult, op1=ALU.add)
            mk.op("vector", "tensor_mul", r_, r_, x_)
        mk.op("vector", "tensor_mul", n_, n_, r_)
        mk.op("vector", "tensor_scalar", v_, n_, s0, 1e-6,
              op0=ALU.mult, op1=ALU.add)
        mk.op("vector", "reciprocal", v_, v_)
        mk.op("vector", "tensor_mul", x_, din, v_)
        mk.op("vector", "tensor_scalar", n_, n_, ALPHA, None, op0=ALU.mult)
        self.sc_tok = mk.op("vector", "scalar_tensor_tensor",
                            self.score_row[0:1, tsl], x_, -(1.0 - ALPHA),
                            n_, op0=ALU.mult, op1=ALU.add)

    def step(self):
        i = self.i
        if i < NE:
            self._dot(i)
        elif i == NE:
            self._trans()
        elif i == NE + 1:
            self._mnorm()
        elif i < NE + 2 + NTS:
            self._score(i - NE - 2)
        elif i == NE + 2 + NTS:
            self.bis.start()
        else:
            self.bis.step()
        self.i += 1

    def finish(self):
        while not (self.i > NE + 2 + NTS and self.bis.done_tok is not None):
            self.step()


class Bisect:

    NITER = 36

    def __init__(self, mk, c, tag, score_row, scope):
        self.mk = mk
        self.c = c
        self.tag = tag
        self.score = score_row
        self.row = mk.sbuf(f"{tag}_brow", [1, S], BF16, scope=scope)
        self.lm = mk.sbuf(f"{tag}_lm", [1, S], BF16, scope=scope)
        self.sc = mk.sbuf(f"{tag}_bsc", [1, 64], F32, scope=scope)
        self.ci = mk.sbuf(f"{tag}_bci", [1, 8], mybir.dt.uint8, scope=scope)
        self.i = 0
        self.done_tok = None

    def start(self):
        mk = self.mk
        mk.op("vector", "tensor_reduce", self.sc[0:1, 0:1],
              self.score[0:1, :], axis=AX.X, op=ALU.min)
        mk.op("vector", "reduce_max", self.sc[0:1, 1:2], self.score[0:1, :],
              axis=AX.X)

    def step(self):
        mk, sc = self.mk, self.sc
        lo, hi, tau, cnt, c01 = (sc[0:1, 0:1], sc[0:1, 1:2], sc[0:1, 2:3],
                                 sc[0:1, 3:4], sc[0:1, 4:5])
        if self.i < self.NITER:
            mk.op("vector", "tensor_add", tau, lo, hi)
            mk.op("vector", "tensor_scalar", tau, tau, 0.5, None, op0=ALU.mult)
            mk.op("vector", "tensor_single_scalar", self.row[0:1, :],
                  self.score[0:1, :], tau, op=ALU.is_ge)
            mk.op("vector", "reduce_sum", cnt, self.row[0:1, :], axis=AX.X)
            mk.op("vector", "tensor_single_scalar", self.ci[0:1, 0:1], cnt,
                  float(KEEP_K), op=ALU.is_ge)
            mk.op("vector", "tensor_single_scalar", self.ci[0:1, 1:2], cnt,
                  float(KEEP_K), op=ALU.is_lt)
            mk.op("vector", "copy_predicated", lo, self.ci[0:1, 0:1], tau)
            mk.op("vector", "copy_predicated", hi, self.ci[0:1, 1:2], tau)
        elif self.i == self.NITER:
            lm0, lmd = self.lm[0:1, 0:S], self.row[0:1, 0:S]
            mk.op("vector", "tensor_single_scalar", lm0, self.score[0:1, :],
                  lo, op=ALU.is_ge)
            for _ in range(HOPS):
                mk.op("vector", "tensor_copy", lmd, lm0)
                mk.op("vector", "tensor_max", lmd[0:1, 1:S], lmd[0:1, 1:S],
                      lm0[0:1, 0:S - 1])
                mk.op("vector", "tensor_max", lmd[0:1, 0:S - 1],
                      lmd[0:1, 0:S - 1], lm0[0:1, 1:S])
                tok = mk.op("vector", "tensor_copy", lm0, lmd)
            self.done_tok = tok
        self.i += 1

    def finish_steps(self):
        while self.done_tok is None:
            self.step()


def mask_cols(mk, c, tag, bis, lm01_cols, lm_bias_cols, deps=()):
    sdd = mk.sem(f"{tag}_sdd")
    st = mk.sem(f"{tag}_t")
    dram = mk.nc.dram_tensor(f"{tag}_dram", [S], BF16, kind="Internal").ap()
    dT = mk.sbuf(f"{tag}_dT", [32, 128], BF16, phase=True)
    ps_t = mk.psum(f"{tag}_pst", [128, 32], F32)
    for d in deps:
        if d is None:
            continue
        for e in ("tensor", "vector", "sync"):
            mk.waitt(e, d)
    bis.finish_steps()
    mk.waitt("sync", bis.done_tok)
    mk.dma("sync", sdd, dram[None, :], bis.lm[0:1, 0:S])
    mk.wait("sync", sdd, sdd.n)
    mk.dma("sync", sdd, dT[:, :], dram.rearrange("(b p) -> b p", p=128))
    mk.wait("tensor", sdd, sdd.n)
    mk.op("tensor", "matmul", ps_t[:, 0:32], dT[:, :],
          c["ident"][0:32, 0:32], start=True, stop=True, inc=st,
          skip_group_check=True)
    mk.wait("vector", st, st.n)
    mk.op("vector", "tensor_copy", lm01_cols[:], ps_t[:, 0:32])
    tok = mk.op("vector", "tensor_scalar", lm_bias_cols[:], lm01_cols[:],
                1e4, -1e4, op0=ALU.mult, op1=ALU.add)
    mk.end_phase()
    return tok



def build_core_program():
    nc = bass.Bass("TRN2", target_bir_lowering=False)
    dr = lambda n, s, d, k="ExternalInput": nc.dram_tensor(n, s, d, kind=k).ap()
    io = {
        "xT": dr("xT", [E, S], F32),
        "Wqa": dr("Wqa", [E, E], F32), "bqa": dr("bqa", [E], F32),
        "Wka": dr("Wka", [E, E], F32), "bka": dr("bka", [E], F32),
        "Wva": dr("Wva", [E, E], F32), "bva": dr("bva", [E], F32),
        "Wca": dr("Wca", [E, E], F32), "bca": dr("bca", [E], F32),
        "Wqb": dr("Wqb", [E, E], F32),
        "Wkb": dr("Wkb", [E, 1024], F32), "Wvb": dr("Wvb", [E, 1024], F32),
        "Wob": dr("Wob", [E, E], F32),
        "wg": dr("wg", [E, INTER], I8), "wu": dr("wu", [E, INTER], I8),
        "wd": dr("wd", [INTER, E], I8),
        "sg": dr("sg", [INTER], F32), "su": dr("su", [INTER], F32),
        "sdn": dr("sdn", [E], F32),
        "yT": dr("yT", [E, S], F32, "ExternalOutput"),
    }
    R = dr("R_scr", [E, S], F32, "Internal")
    Rb_s = dr("Rb_scr", [E, S], BF16, "Internal")
    QT_s = dr("QT_scr", [E, S], BF16, "Internal")
    KT_s = dr("KT_scr", [E, S], BF16, "Internal")
    V_s = dr("V_scr", [S, E], BF16, "Internal")
    H_s = dr("H_scr", [INTER, S], BF16, "Internal")

    mk = MK(nc)
    c = build_consts(mk)
    A = mk.sbuf("A", [128, NE * S], BF16)
    bias = mk.sbuf("bias", [128, 2 * NE], F32)
    scales = mk.sbuf("scales", [128, 2 * NIO + NE], F32)
    sg_col = scales[:, 0:NIO]
    su_col = scales[:, NIO:2 * NIO]
    sdn_col = scales[:, 2 * NIO:2 * NIO + NE]

    for i, w in enumerate(["bva", "bca"]):
        load_vec_cols(mk, c, f"l{w}", io[w], bias[:, i * NE:(i + 1) * NE], NE)
    load_vec_cols(mk, c, "lsg", io["sg"], sg_col, NIO)
    load_vec_cols(mk, c, "lsu", io["su"], su_col, NIO)
    ldt = load_vec_cols(mk, c, "lsdn", io["sdn"], sdn_col, NE)
    mk.end_phase()
    movs = [A[:, i * S:(i + 1) * S] for i in range(NE)]

    def hookify(evac, hook):
        if hook is None:
            return evac

        def e2(eo, q, ps_list, pe_tok):
            r = evac(eo, q, ps_list, pe_tok)
            hook()
            return r
        return e2

    def qkv_block(subtag, Wq, Wk, Wv, nk, dep=None, hook=None, mov_ts=None,
                  psum_dep=None, prealloc=None, wdep=()):
        prev = dep
        stoks = []
        for j, (nm, Wa, n_eo) in enumerate((("q", Wq, NE), ("k", Wk, nk),
                                            ("v", Wv, nk))):
            out = {"q": QT_s, "k": KT_s, "v": V_s}[nm]
            ev, fin = qkv_evac_factory(mk, c, f"{subtag}{nm}", out,
                                       vtrans=(nm == "v"))
            for s_, v_ in stoks:
                mk.wait("vector", s_, v_)
            prev = mm_pass(mk, c, f"{subtag}m{nm}", movs, [{"ap": Wa}], n_eo,
                           hookify(ev, hook), mov_tok=dep, deps=[prev],
                           mov_ts=(mov_ts if j == 0 else None),
                           psum_dep=(psum_dep if j == 0 else None),
                           prealloc=prealloc,
                           wdep=(wdep if j == 0 else (prev,)))
            stoks = fin()
        return prev, stoks

    wsc1 = []
    ev, fin = qkv_evac_factory(mk, c, "s1v", QT_s, bias_col=bias[:, 0:NE],
                               scope=wsc1)
    wb1 = mm_prealloc(mk, "s1mv", NE, 1, wsc1)
    r1 = ln_phase(mk, c, io["xT"], A, "ln1", deps=[ldt], src_bf16=True,
                  cast_load=True)
    p0 = mm_pass(mk, c, "s1mv", movs, [{"ap": io["Wva"]}], NE, ev,
                 mov_ts=r1["xn_ts"], psum_dep=r1["psum_free"], wdep=[ldt],
                 prealloc=wb1)
    stoks = fin()
    mk.release(wsc1)
    ldv = mk.sem("sm_ldv")
    for s_, v_ in stoks:
        mk.wait("sync", s_, v_)
    mk.waitt("sync", p0)
    for i in range(NE):
        at1 = mk.dma("sync", ldv, A[:, i * S:(i + 1) * S],
                     QT_s[i * 128:(i + 1) * 128, :])
    ev, fin, rpst1 = resid_evac_factory(mk, c, "rp1", io["xT"], R,
                                        bias_col=bias[:, NE:2 * NE],
                                        deps=[p0], rb_out=Rb_s)
    p1 = mm_pass(mk, c, "mmc1", movs, [{"ap": io["Wca"]}], NE, ev,
                 mov_tok=at1, deps=[p0])
    rtoks1 = fin()

    wsc2 = []
    ev, fin = ffn1_evac_factory(mk, c, "f1", H_s, sg_col, su_col, scope=wsc2)
    wb2 = mm_prealloc(mk, "mmf1", NE, 2, wsc2)
    r2 = ln_phase(mk, c, Rb_s, A, "ln2", src_bf16=True,
                  deps=[p1] + rtoks1, row_toks=rpst1["rb_rows"],
                  load_dep_half=[p1, p1])
    p2 = mm_pass(mk, c, "mmf1", movs, [{"ap": io["wg"]}, {"ap": io["wu"]}],
                 NIO, ev, mov_ts=r2["xn_ts"], psum_dep=r2["psum_free"],
                 wdep=[p1] + rtoks1, prealloc=wb2)
    htoks = fin()
    mk.release(wsc2)
    p3, rtoks2, rb2_rows, hfree2 = ffn2_phase(mk, c, "f2", H_s, io["wd"],
                                              R, R, sdn_col, A,
                                              deps=[p2] + htoks, rb_out=Rb_s)

    heldlm = []
    lm01_cols = mk.sbuf("lm01c", [128, 32], F32, scope=heldlm)
    lmb_cols = mk.sbuf("lmbc", [128, 32], F32, scope=heldlm)
    heldsc = []
    score_row = mk.sbuf("score3", [1, S], F32, scope=heldsc)
    var_row = dr("var3_scr", [1, S], F32, "Internal")
    mbar = mk.sbuf("mbar3", [128, NE], F32, scope=heldsc)
    wb3 = mm_prealloc(mk, "s3w", NE, 1, heldsc)
    r3 = ln_phase(mk, c, Rb_s, A, "ln3", src_bf16=True, var_row=var_row,
                  mbar=mbar, deps=[p3] + rtoks2, row_toks=rb2_rows,
                  load_dep_half=hfree2)
    msk = MaskPipeline(mk, c, "ms3", A, var_row, mbar, score_row, heldsc,
                       r3["xn_done"])
    prev, stoks = qkv_block("s3", io["Wqb"], io["Wkb"], io["Wvb"], 8,
                            dep=r3["xn_done"], hook=msk.step,
                            mov_ts=r3["xn_ts"], psum_dep=r3["psum_free"],
                            prealloc=wb3, wdep=[p3] + rtoks2)
    msk.finish()
    cm_tok = mask_cols(mk, c, "mc3", msk.bis, lm01_cols, lmb_cols,
                       deps=[prev])
    mk.release(heldsc)
    at3 = attn_core(mk, c, "at3", QT_s, KT_s, V_s, A, kv_group=2,
                    lm01_cols=lm01_cols, lm_bias_cols=lmb_cols,
                    deps=[prev, cm_tok] + stoks)
    ev, fin, rpst3 = resid_evac_factory(mk, c, "rp3", R, R, deps=[at3],
                                        rb_out=Rb_s)
    p4 = mm_pass(mk, c, "mmc3", movs, [{"ap": io["Wob"]}], NE, ev,
                 mov_tok=at3, deps=[at3])
    rtoks3 = fin()
    mk.release(heldlm)

    wsc4 = []
    ev, fin = ffn1_evac_factory(mk, c, "f3", H_s, sg_col, su_col, scope=wsc4)
    wb4 = mm_prealloc(mk, "mmf3", NE, 2, wsc4)
    r4 = ln_phase(mk, c, Rb_s, A, "ln4", src_bf16=True,
                  deps=[p4] + rtoks3, row_toks=rpst3["rb_rows"],
                  load_dep_half=[p4, p4])
    p5 = mm_pass(mk, c, "mmf3", movs, [{"ap": io["wg"]}, {"ap": io["wu"]}],
                 NIO, ev, mov_ts=r4["xn_ts"], psum_dep=r4["psum_free"],
                 wdep=[p4] + rtoks3, prealloc=wb4)
    htoks = fin()
    mk.release(wsc4)
    ffn2_phase(mk, c, "f4", H_s, io["wd"], R, io["yT"], sdn_col, A,
               deps=[p5] + htoks)

    mk.play()
    mk.close()
    return nc


def kernel(**inputs):
    nc = build_core_program()
    B = 8
    x = inputs["L0_attn_a_x"]
    in_maps = []
    for b in range(B):
        m = {
            "xT": np.ascontiguousarray(np.asarray(x[b]).T),
            "Wqa": inputs["L0_attn_a_Wq"], "bqa": inputs["L0_attn_a_bq"],
            "Wka": inputs["L0_attn_a_Wk"], "bka": inputs["L0_attn_a_bk"],
            "Wva": inputs["L0_attn_a_Wv"], "bva": inputs["L0_attn_a_bv"],
            "Wca": inputs["L0_attn_a_Wc"], "bca": inputs["L0_attn_a_bc"],
            "Wqb": inputs["L0_attn_b_Wq"], "Wkb": inputs["L0_attn_b_Wk"],
            "Wvb": inputs["L0_attn_b_Wv"], "Wob": inputs["L0_attn_b_Wo"],
            "wg": inputs["L0_ffn_w_gate"], "wu": inputs["L0_ffn_w_up"],
            "wd": inputs["L0_ffn_w_down"],
            "sg": inputs["L0_ffn_s_gate"], "su": inputs["L0_ffn_s_up"],
            "sdn": inputs["L0_ffn_s_down"],
        }
        in_maps.append({k: np.asarray(v) for k, v in m.items()})
    from concourse.bass_utils import run_bass_kernel_spmd
    res = run_bass_kernel_spmd(nc, in_maps, core_ids=list(range(B)))
    out = np.stack([res.results[b]["yT"].T for b in range(B)], axis=0)
    return np.ascontiguousarray(out.astype(np.float32))


# version 17
# speedup vs baseline: 1.0495x; 1.0495x over previous
import math
import numpy as np
import concourse.bass as bass
import concourse.mybir as mybir

F32 = mybir.dt.float32
BF16 = mybir.dt.bfloat16
I8 = mybir.dt.int8
AF = mybir.ActivationFunctionType
ALU = mybir.AluOpType
AX = mybir.AxisListType

E = 2048
S = 4096
NE = 16
NTS = 8
TS = 512
INTER = 8192
NIO = 64
EPS = 1e-5
WIN = 128
HOPS = 2
ALPHA = 0.7
KEEP_K = int(0.33 * S)
SCALE_D = 1.0 / math.sqrt(128.0)
NEGBIG = -30000.0


class Sem:
    def __init__(self, h):
        self.h = h
        self.n = 0

    def bump(self, k=1):
        self.n += k
        return self.n


class MK:

    ENGINES = ("tensor", "vector", "scalar", "gpsimd", "sync")

    def __init__(self, nc):
        self.nc = nc
        self.q = {e: [] for e in self.ENGINES}
        self.sems = {}
        self._ctx = []
        self._phase_ctx = []

    CHAINED = ("vector", "scalar", "gpsimd")

    def op(self, eng, method, *args, inc=None, **kwargs):
        if inc is not None and not isinstance(inc, tuple):
            inc = (inc, 1)
        if eng in self.CHAINED and method != "dma_start":
            assert inc is None
            ctr = self.sem("ctr_" + eng)
            if ctr.n > 0:
                self.q[eng].append(
                    lambda e, _s=ctr.h, _v=ctr.n: e.wait_ge(_s, _v))
            inc = (ctr, 1)

        def fn(e, _m=method, _a=args, _k=kwargs, _i=inc):
            ins = getattr(e, _m)(*_a, **_k)
            if _i is not None:
                ins.then_inc(_i[0].h, _i[1])
        self.q[eng].append(fn)
        if inc is not None:
            inc[0].bump(inc[1])
            return (inc[0], inc[0].n)
        return None

    def waitt(self, eng, tok):
        if tok is not None:
            self.wait(eng, tok[0], tok[1])

    def wait(self, eng, sem, val=None):
        v = sem.n if val is None else val
        if v > 0:
            self.q[eng].append(lambda e, _s=sem.h, _v=v: e.wait_ge(_s, _v))

    def tick(self, eng, sem, k=1):
        self.op(eng, "nop", inc=(sem, k))

    def dma(self, q, sem, out, in_):
        return self.op(q, "dma_start", out=out, in_=in_, inc=(sem, 16))

    def dma_p(self, q, out, in_):
        if not hasattr(self, "_dma_rr"):
            self._dma_rr = {}
        i = self._dma_rr[q] = (self._dma_rr.get(q, -1) + 1) % 8
        sem = self.sem(f"dpool_{q}{i}")
        if sem.n > 0:
            self.wait(q, sem, sem.n)
        return self.op(q, "dma_start", out=out, in_=in_, inc=(sem, 16))

    def barrier(self, engines=("tensor", "vector", "scalar")):
        bar = self.sem("bar")
        for e in engines:
            self.op(e, "nop", inc=(bar, 1))
        tgt = bar.n
        for e in engines:
            self.wait(e, bar, tgt)

    def play(self):
        nc = self.nc
        with nc.Block() as block:
            @block.tensor
            def _(eng):
                for fn in self.q["tensor"]:
                    fn(eng)

            @block.vector
            def _(eng):
                for fn in self.q["vector"]:
                    fn(eng)

            @block.scalar
            def _(eng):
                for fn in self.q["scalar"]:
                    fn(eng)

            @block.gpsimd
            def _(eng):
                for fn in self.q["gpsimd"]:
                    fn(eng)

            @block.sync
            def _(eng):
                for fn in self.q["sync"]:
                    fn(eng)

    def sem(self, name):
        if name not in self.sems:
            g = self.nc.semaphore(name)
            h = g.__enter__()
            self._ctx.append(g)
            self.sems[name] = Sem(h)
        return self.sems[name]

    def sbuf(self, name, shape, dtype, phase=False, scope=None):
        g = self.nc.sbuf_tensor(name, shape, dtype)
        t = g.__enter__()
        if scope is not None:
            scope.append(g)
        else:
            (self._phase_ctx if phase else self._ctx).append(g)
        return t

    def release(self, scope):
        for g in reversed(scope):
            g.__exit__(None, None, None)
        scope.clear()

    def psum(self, name, shape, dtype=F32, phase=True, scope=None):
        g = self.nc.psum_tensor(name, shape, dtype)
        t = g.__enter__()
        if scope is not None:
            scope.append(g)
        else:
            (self._phase_ctx if phase else self._ctx).append(g)
        return t

    def end_phase(self):
        for g in reversed(self._phase_ctx):
            g.__exit__(None, None, None)
        self._phase_ctx = []

    def close(self):
        self.end_phase()
        for g in reversed(self._ctx):
            g.__exit__(None, None, None)
        self._ctx = []



def build_consts(mk):
    c = {}
    c["ones_col"] = mk.sbuf("c_ones_col", [128, 1], BF16)
    c["ones_col_f32"] = mk.sbuf("c_ones_col32", [128, 1], F32)
    c["ones_row"] = mk.sbuf("c_ones_row", [1, 128], BF16)
    c["ones_row_f32"] = mk.sbuf("c_ones_row32", [1, 128], F32)
    c["ident"] = mk.sbuf("c_ident", [128, 128], BF16)
    c["ident32f"] = mk.sbuf("c_ident32f", [32, 32], F32)
    c["ident_f32"] = mk.sbuf("c_ident_f32", [128, 128], F32)
    c["tri_keep"] = mk.sbuf("c_tri_keep", [128, 128], BF16)
    c["low_strict"] = mk.sbuf("c_low_strict", [128, 128], BF16)

    mk.op("gpsimd", "memset", c["ones_col"][:], 1.0)
    mk.op("gpsimd", "memset", c["ones_col_f32"][:], 1.0)
    mk.op("gpsimd", "memset", c["ones_row"][:], 1.0)
    mk.op("gpsimd", "memset", c["ones_row_f32"][:], 1.0)
    mk.op("gpsimd", "memset", c["ident"][:], 0.0)
    mk.op("gpsimd", "memset", c["ident32f"][:], 0.0)
    mk.op("gpsimd", "memset", c["ident_f32"][:], 0.0)
    mk.op("gpsimd", "memset", c["tri_keep"][:], 0.0)
    mk.op("gpsimd", "memset", c["low_strict"][:], 0.0)
    mk.op("gpsimd", "affine_select", out=c["ident"][:], in_=c["ident"][:],
          compare_op=ALU.not_equal, fill=1.0, base=0,
          pattern=[[-1, 128]], channel_multiplier=1)
    mk.op("gpsimd", "affine_select", out=c["ident32f"][:], in_=c["ident32f"][:],
          compare_op=ALU.not_equal, fill=1.0, base=0,
          pattern=[[-1, 32]], channel_multiplier=1)
    mk.op("gpsimd", "affine_select", out=c["ident_f32"][:], in_=c["ident_f32"][:],
          compare_op=ALU.not_equal, fill=1.0, base=0,
          pattern=[[-1, 128]], channel_multiplier=1)
    mk.op("gpsimd", "affine_select", out=c["tri_keep"][:], in_=c["tri_keep"][:],
          compare_op=ALU.is_gt, fill=1.0, base=0,
          pattern=[[-1, 128]], channel_multiplier=1)
    tok = mk.op("gpsimd", "affine_select", out=c["low_strict"][:],
                in_=c["low_strict"][:], compare_op=ALU.is_ge, fill=1.0, base=0,
                pattern=[[1, 128]], channel_multiplier=-1)
    for e in ("tensor", "vector", "scalar", "sync"):
        mk.waitt(e, tok)
    return c



def ln_phase(mk, c, R_ap, A, tag, var_row=None, mbar=None, src_bf16=False,
             deps=(), row_toks=None, load_dep_half=None,
             cast_load=False):
    st = mk.sem(f"{tag}_t")
    for d in deps:
        if d is None:
            continue
        for e in ("tensor", "vector", "scalar", "gpsimd"):
            mk.waitt(e, d)

    sqb = mk.sbuf(f"{tag}_sqb", [128, S], BF16, phase=True)
    meanh = mk.sbuf(f"{tag}_meanh", [1, S // 2], F32, phase=True)
    scr = mk.sbuf(f"{tag}_scr", [1, 3 * TS], F32, phase=True)
    varh = mk.sbuf(f"{tag}_varh", [1, S // 2], F32, phase=True)
    tmp = mk.sbuf(f"{tag}_tmp", [128, 2 * TS], F32, phase=True)
    bcs = mk.sbuf(f"{tag}_bcs", [128, 2 * TS], F32, phase=True)
    if cast_load:
        rt2 = mk.sbuf(f"{tag}_rt2", [128, S], F32, phase=True)
    ps_sum = mk.psum(f"{tag}_ps_sum", [1, S // 2], F32)
    ps_bc = [mk.psum(f"{tag}_ps_bc{j}", [128, 2 * TS], F32)
             for j in range(2)]
    svr = mk.sem("sm_svr")

    cast_done = [None] * NE
    for i in range(NE):
        q_ = "gpsimd" if (cast_load and i % 2 == 0) else "sync"
        if load_dep_half is not None:
            mk.waitt(q_, load_dep_half[i // 8])
        if row_toks is not None and row_toks[i] is not None:
            mk.waitt(q_, row_toks[i])
        if cast_load and i % 2 == 1:
            if i >= 3:
                mk.waitt("sync", cast_done[i - 2])
            dtok = mk.dma_p("sync", rt2[:, :],
                            R_ap[i * 128:(i + 1) * 128, :])
            mk.waitt("vector", dtok)
            cast_done[i] = mk.op("vector", "tensor_copy",
                                 A[:, i * S:(i + 1) * S], rt2[:, :])
        else:
            cast_done[i] = mk.dma_p(q_, A[:, i * S:(i + 1) * S],
                                    R_ap[i * 128:(i + 1) * 128, :])
    for d in deps:
        if d is not None:
            mk.waitt("sync", d)

    xn_ts = [None] * NTS
    sq_done = {}
    mk.waitt("vector", cast_done[-1])
    for half in range(2):
        hts = range(half * 4, half * 4 + 4)
        if half == 1:
            mk.waitt("tensor", varh_tok)
        for i in range(NE):
            if half == 0:
                mk.waitt("tensor", cast_done[i])
            ch = A[:, i * S:(i + 1) * S]
            for j, t_ in enumerate(hts):
                last = (i == NE - 1 and j == 3)
                mk.op("tensor", "matmul", ps_sum[0:1, j * TS:(j + 1) * TS],
                      c["ones_col"][:], ch[:, t_ * TS:(t_ + 1) * TS],
                      start=(i == 0), stop=(i == NE - 1),
                      inc=(st if last else None))
        mk.wait("vector", st, st.n)
        mean_tok = mk.op("vector", "tensor_scalar", meanh[0:1, :],
                         ps_sum[0:1, :], 1.0 / E, None, op0=ALU.mult)
        S2 = S // 2
        for i in range(NE):
            ch = A[:, i * S + half * S2: i * S + (half + 1) * S2]
            gi = half * NE + i
            slot = sqb[:, (i % 2) * S2:(i % 2 + 1) * S2]
            if gi >= 2:
                mk.wait("vector", st, sq_done[gi - 2])
            sq_tok = mk.op("vector", "tensor_mul", slot, ch, ch)
            mk.waitt("tensor", sq_tok)
            if i == 0:
                mk.waitt("tensor", mean_tok)
            for j, t_ in enumerate(hts):
                mk.op("tensor", "matmul", ps_sum[0:1, j * TS:(j + 1) * TS],
                      c["ones_col"][:], slot[:, j * TS:(j + 1) * TS],
                      start=(i == 0), stop=(i == NE - 1),
                      inc=(st if j == 3 else None))
            sq_done[gi] = st.n
        mk.wait("vector", st, st.n)
        varh_tok = mk.op("vector", "tensor_scalar", varh[0:1, :],
                         ps_sum[0:1, :], 1.0 / E, None, op0=ALU.mult)
        if half == 0:
            bufs_early = varh_tok
        else:
            psum_free = varh_tok

        v_, s_, r_ = (scr[0:1, 0:TS], scr[0:1, TS:2 * TS],
                      scr[0:1, 2 * TS:3 * TS])
        for j, t_ in enumerate(hts):
            pb = ps_bc[t_ % 2]
            cm = bcs[:, 0:TS]
            cr = bcs[:, TS:2 * TS]
            tsl = slice(t_ * TS, (t_ + 1) * TS)
            x_ = tmp[0:1, 0:TS]
            mk.op("vector", "tensor_mul", x_, meanh[0:1, j * TS:(j + 1) * TS],
                  meanh[0:1, j * TS:(j + 1) * TS])
            vtok = mk.op("vector", "tensor_sub", v_,
                         varh[0:1, j * TS:(j + 1) * TS], x_)
            if var_row is not None:
                mk.waitt("sync", vtok)
                mk.dma("sync", svr, var_row[0:1, tsl], v_)
                mk.wait("vector", svr, svr.n)
            ve_tok = mk.op("vector", "tensor_scalar", v_, v_, EPS, None,
                           op0=ALU.add)
            mk.waitt("scalar", ve_tok)
            sq2 = mk.op("scalar", "activation", s_, v_, AF.Sqrt)
            mk.waitt("vector", sq2)
            mk.op("vector", "reciprocal", r_, s_)
            for it in range(2):
                mk.op("vector", "tensor_mul", x_, r_, r_)
                mk.op("vector", "tensor_mul", x_, x_, v_)
                mk.op("vector", "tensor_scalar", x_, x_, -0.5, 1.5,
                      op0=ALU.mult, op1=ALU.add)
                rows_tok = mk.op("vector", "tensor_mul", r_, r_, x_)
            mk.waitt("tensor", rows_tok)
            if t_ >= 2:
                mk.waitt("tensor", xn_ts[t_ - 2])
            mk.op("tensor", "matmul", pb[:, 0:TS], c["ones_row_f32"][:],
                  meanh[0:1, j * TS:(j + 1) * TS], start=True, stop=True,
                  skip_group_check=True)
            mk.op("tensor", "matmul", pb[:, TS:2 * TS], c["ones_row_f32"][:],
                  r_, start=True, stop=True, inc=st, skip_group_check=True)
            mk.wait("vector", st, st.n)
            mk.op("vector", "tensor_copy", bcs[:, :], pb[:, 0:2 * TS])
            for i in range(NE):
                last = (i == NE - 1)
                ach = A[:, i * S + t_ * TS: i * S + (t_ + 1) * TS]
                f32t = tmp[:, 0:TS]
                mk.op("vector", "tensor_sub", f32t, ach, cm)
                if mbar is None:
                    tok = mk.op("vector", "tensor_mul", ach, f32t, cr)
                else:
                    x32 = tmp[:, TS:2 * TS]
                    mk.op("vector", "tensor_mul", x32, f32t, cr)
                    mk.op("vector", "tensor_copy", ach, x32)
                    mk.op("vector", "reduce_sum", tmp[:, 0:1], x32, axis=AX.X)
                    if t_ == 0:
                        tok = mk.op("vector", "tensor_scalar", mbar[:, i:i + 1],
                                    tmp[:, 0:1], 1.0 / S, None, op0=ALU.mult)
                    else:
                        mk.op("vector", "tensor_scalar", tmp[:, 0:1],
                              tmp[:, 0:1], 1.0 / S, None, op0=ALU.mult)
                        tok = mk.op("vector", "tensor_add", mbar[:, i:i + 1],
                                    mbar[:, i:i + 1], tmp[:, 0:1])
            xn_ts[t_] = tok

    mk.end_phase()
    return {"xn_ts": xn_ts, "xn_done": xn_ts[-1], "psum_free": psum_free,
            "bufs_early": bufs_early, "mrow_tok": None}



def mm_prealloc(mk, tag, n_ei, ns, scope):
    return [mk.sbuf(f"{tag}_wb{s}", [128, 2 * n_ei * 128], BF16, scope=scope)
            for s in range(ns)]


def mm_pass(mk, c, tag, movs, wstreams, n_eo, evac, mov_tok=None, deps=(),
            mov_ts=None, psum_dep=None, wdep=(), prealloc=None):
    n_ei = len(movs)
    ns = len(wstreams)
    st = mk.sem("sm_t")
    wld = [[mk.sem(f"sm_w{s}_{j}") for j in range(2)] for s in range(ns)]
    wbufs = prealloc if prealloc is not None else [
        mk.sbuf(f"{tag}_wb{s}", [128, 2 * n_ei * 128], BF16, phase=True)
        for s in range(ns)]
    pss = [mk.psum(f"{tag}_ps{s}", [128, 2 * 1024], F32) for s in range(ns)]

    if mov_tok is not None:
        mk.waitt("tensor", mov_tok)
        if prealloc is None:
            mk.waitt("gpsimd", mov_tok)
    for d in wdep:
        if d is not None:
            mk.waitt("gpsimd", d)
    if psum_dep is not None:
        mk.waitt("tensor", psum_dep)
    for d in deps:
        if d is None:
            continue
        mk.waitt("tensor", d)
        if prealloc is None:
            mk.waitt("gpsimd", d)
    pe_groups_done = [0] * (n_eo + 1)
    evac_toks = {}
    wtoks = [None] * ns
    for eo in range(n_eo):
        for s in range(ns):
            if eo >= 2:
                mk.wait("gpsimd", st, pe_groups_done[eo - 2])
            src = wstreams[s]["ap"][:, eo * 128:(eo + 1) * 128].rearrange(
                "(a p) c -> p a c", p=128)
            dst = wbufs[s][:, (eo % 2) * n_ei * 128:
                           (eo % 2 + 1) * n_ei * 128].rearrange(
                "p (a c) -> p a c", c=128)
            wtoks[s] = mk.dma("gpsimd", wld[s][eo % 2], dst, src)
        for s in range(ns):
            mk.waitt("tensor", wtoks[s])
        for q in range(4):
            sl = q % 2
            if eo == 0 and mov_ts is not None:
                mk.waitt("tensor", mov_ts[2 * q + 1])
            for s in range(ns):
                prev = (eo, q - 2, s) if q >= 2 else (eo - 1, q + 2, s)
                if prev in evac_toks:
                    mk.waitt("tensor", evac_toks[prev])
                wb = wbufs[s][:, (eo % 2) * n_ei * 128:
                              (eo % 2 + 1) * n_ei * 128]
                ps = pss[s][:, sl * 1024:(sl + 1) * 1024]
                for ei in range(n_ei):
                    wc = wb[:, ei * 128:(ei + 1) * 128]
                    for u in range(2):
                        t_ = 2 * q + u
                        last = (ei == n_ei - 1 and u == 1 and s == ns - 1)
                        mk.op("tensor", "matmul",
                              ps[:, u * TS:(u + 1) * TS], wc,
                              movs[ei][:, t_ * TS:(t_ + 1) * TS],
                              start=(ei == 0), stop=(ei == n_ei - 1),
                              inc=(st if last else None))
            pe_groups_done[eo] = st.n
            toks = evac(eo, q, [pss[s][:, sl * 1024:(sl + 1) * 1024]
                                for s in range(ns)], (st, st.n))
            if not isinstance(toks, list):
                toks = [toks] * ns
            for s in range(ns):
                evac_toks[(eo, q, s)] = toks[s]
                last_tok = toks[s]
    mk.end_phase()
    return last_tok


def qkv_evac_factory(mk, c, tag, out_hbm, bias_col=None, vtrans=False,
                     n_eo=16, scope=None):
    stg = mk.sbuf(f"{tag}_stg", [128, S], BF16, phase=(scope is None),
                  scope=scope)
    sd = [mk.sem(f"sm_sd{j}") for j in range(2)]
    state = {"st": None, "tr": None}
    if vtrans:
        pst = [mk.psum(f"{tag}_pt{j}", [128, 128], BF16) for j in range(2)]
        vstg = mk.sbuf(f"{tag}_vstg", [128, S], BF16, phase=True)
        stv = mk.sem("sm_tv")

    def evac(eo, q, ps_list, pe_tok):
        ps = ps_list[0]
        out = stg[:, q * 1024:(q + 1) * 1024]
        mk.waitt("vector", pe_tok)
        if eo >= 1 and q == 0:
            if vtrans:
                mk.waitt("vector", state["tr"])
            else:
                mk.waitt("vector", state["st"])
        if bias_col is None:
            tok = mk.op("vector", "tensor_copy", out, ps)
        else:
            tok = mk.op("vector", "tensor_scalar", out, ps,
                        bias_col[:, eo:eo + 1], None, op0=ALU.add)
        if q < 3:
            return tok
        if not vtrans:
            mk.waitt("sync", tok)
            state["st"] = mk.dma("sync", sd[0],
                                 out_hbm[eo * 128:(eo + 1) * 128, :],
                                 stg[:, :])
            return tok
        mk.waitt("tensor", tok)
        if eo >= 1:
            mk.waitt("vector", state["st"])
        ctoks = [None, None]
        for blk in range(32):
            pj = blk % 2
            if blk >= 2:
                mk.waitt("tensor", ctoks[pj])
            mk.op("tensor", "transpose", pst[pj][:],
                  stg[:, blk * 128:(blk + 1) * 128],
                  c["ident"][:], inc=stv)
            mk.wait("vector", stv, stv.n)
            ctoks[pj] = mk.op("vector", "tensor_copy",
                              vstg[:, blk * 128:(blk + 1) * 128], pst[pj][:])
        mk.waitt("sync", ctoks[1])
        dst = out_hbm[:, eo * 128:(eo + 1) * 128].rearrange(
            "(b p) c -> p b c", p=128)
        state["st"] = mk.dma("sync", sd[1], dst, vstg[:, :].rearrange(
            "p (b c) -> p b c", c=128))
        state["tr"] = ctoks[1]
        return ctoks[1]

    def finish():
        mk.wait("sync", sd[0], sd[0].n)
        mk.wait("sync", sd[1], sd[1].n)
        return [(sd[0], sd[0].n), (sd[1], sd[1].n)]
    return evac, finish


def load_vec_cols(mk, c, tag, vec_ap, out_cols, n, ps_share=None):
    tmp = mk.sbuf(f"{tag}_lvtmp", [n, 128], F32)
    ps = ps_share if ps_share is not None else mk.psum(f"{tag}_lvp",
                                                       [128, n], F32)
    sdl = mk.sem(f"lv_{tag}_d")
    stl = mk.sem("sm_lvt")
    mk.dma("sync", sdl, tmp[:, :], vec_ap.rearrange("(a p) -> a p", p=128))
    mk.wait("tensor", sdl, sdl.n)
    mk.op("tensor", "matmul", ps[:, 0:n], tmp[:, :], c["ident_f32"][0:n, 0:n],
          start=True, stop=True, inc=stl)
    mk.wait("vector", stl, stl.n)
    return mk.op("vector", "tensor_copy", out_cols[:, 0:n], ps[:, 0:n])


def attn_core(mk, c, tag, QT_s, KT_s, V_s, A, nheads=16, kv_group=1,
              lm01_cols=None, lm_bias_cols=None, deps=()):
    st = mk.sem("sm_t")
    qld = [mk.sem(f"sm_ql{j}") for j in range(2)]
    kld = [mk.sem(f"sm_kl{j}") for j in range(2)]
    vld = [mk.sem(f"sm_vl{j}") for j in range(2)]
    NSL = 3

    qt = mk.sbuf(f"{tag}_qt", [128, 2 * S], BF16, phase=True)
    kt = mk.sbuf(f"{tag}_kt", [128, 2 * S], BF16, phase=True)
    vh = mk.sbuf(f"{tag}_vh", [128, 2 * S], BF16, phase=True)
    pt = mk.sbuf(f"{tag}_pt", [128, NSL * 1024], BF16, phase=True)
    sq = mk.sbuf(f"{tag}_sq", [128, S], BF16, phase=True)
    sm = mk.sbuf(f"{tag}_sm", [1, 16], F32, phase=True)
    nbc = mk.sbuf(f"{tag}_nbc", [128, 2], F32, phase=True)
    bh = mk.sbuf(f"{tag}_bh", [128, 32], F32, phase=True)
    tk = mk.sbuf(f"{tag}_tk", [128, 128], BF16, phase=True)
    rr = mk.sbuf(f"{tag}_rr", [1, TS], F32, phase=True)
    rdb = mk.sbuf(f"{tag}_rdb", [128, TS], F32, phase=True)

    ps_st = mk.psum(f"{tag}_ps_st", [128, NSL * 1024], F32)
    ps_ot = mk.psum(f"{tag}_ps_ot", [128, TS], F32)
    ps_den = mk.psum(f"{tag}_ps_den", [1, TS], F32)
    ps_bc = ps_st[:, 2 * 1024: 2 * 1024 + TS]
    ps_n4 = ps_st[0:1, 0:4 * TS]

    for d in deps:
        if d is None:
            continue
        mk.waitt("tensor", d)
        mk.waitt("vector", d)
        mk.waitt("scalar", d)
        mk.waitt("sync", d)

    q2, k2, mm_, tsc = sm[0:1, 0:1], sm[0:1, 1:2], sm[0:1, 2:3], sm[0:1, 3:4]
    pe_head_done = [0] * (nheads + 2)
    dve_head_done = [None] * (nheads + 2)
    qtoks = [None, None]
    ktoks = [None, None]
    vtoks = [None, None]
    last_ot_tok = None
    prev_act_tok = None

    def maxnorm2(dst, src_sb, sq_tok):
        mk.op("vector", "memset", dst[:, :], 0.0)
        tok = mk.op("vector", "tensor_mul", sq[:], src_sb, src_sb)
        for r_ in range(2):
            mk.waitt("tensor", tok)
            for t4 in range(4):
                t_ = r_ * 4 + t4
                mk.op("tensor", "matmul", ps_n4[0:1, t4 * TS:(t4 + 1) * TS],
                      c["ones_col"][:], sq[:, t_ * TS:(t_ + 1) * TS],
                      start=True, stop=True, skip_group_check=True,
                      inc=(st if t4 == 3 else None))
            mk.wait("vector", st, st.n)
            mk.op("vector", "reduce_max", tsc, ps_n4[0:1, :], axis=AX.X)
            tok = mk.op("vector", "tensor_max", dst, dst, tsc)
        return tok

    for h in range(nheads):
        kv = h // kv_group
        hs = h % 2
        ks = kv % 2
        if h >= 2:
            mk.wait("sync", st, pe_head_done[h - 2])
            mk.waitt("sync", dve_head_done[h - 2])
        qtoks[hs] = mk.dma("sync", qld[hs], qt[:, hs * S:(hs + 1) * S],
                           QT_s[h * 128:(h + 1) * 128, :])
        if kv_group == 1 or h % 2 == 0:
            ktoks[ks] = mk.dma("sync", kld[ks], kt[:, ks * S:(ks + 1) * S],
                               KT_s[kv * 128:(kv + 1) * 128, :])
            vtoks[ks] = mk.dma("sync", vld[ks], vh[:, ks * S:(ks + 1) * S],
                               V_s[:, kv * 128:(kv + 1) * 128].rearrange(
                                   "(b p) c -> p b c", p=128))
        qth = qt[:, hs * S:(hs + 1) * S]
        kth = kt[:, ks * S:(ks + 1) * S]
        vhh = vh[:, ks * S:(ks + 1) * S]
        if prev_act_tok is not None:
            mk.waitt("vector", prev_act_tok)
        mk.waitt("vector", qtoks[hs])
        sq_tok = maxnorm2(q2, qth, None)
        if kv_group == 1 or h % 2 == 0:
            mk.waitt("vector", ktoks[ks])
            sq_tok = maxnorm2(k2, kth, None)
        t_mm = mk.op("vector", "tensor_mul", mm_, q2, k2)
        mk.waitt("scalar", t_mm)
        mk.op("scalar", "activation", tsc, mm_, AF.Ln)
        sq2 = mk.op("scalar", "activation", mm_, tsc, AF.Exp, scale=0.5)
        mk.waitt("vector", sq2)
        neg_tok = mk.op("vector", "tensor_scalar", mm_, mm_,
                        -SCALE_D * 1.001, None, op0=ALU.mult)
        mk.waitt("tensor", neg_tok)
        mk.op("tensor", "matmul", ps_bc[:, 0:1], c["ones_row_f32"][:],
              mm_, start=True, stop=True, inc=st, skip_group_check=True)
        mk.wait("vector", st, st.n)
        nb_tok = mk.op("vector", "tensor_copy", nbc[:, 0:1], ps_bc[:, 0:1])
        if lm_bias_cols is not None:
            nb_tok = mk.op("vector", "tensor_scalar", bh[:], lm_bias_cols[:],
                           nbc[:, 0:1], None, op0=ALU.add)
        mk.waitt("scalar", nb_tok)

        mk.waitt("tensor", vtoks[ks])
        st_war = [nb_tok, None, None]
        pt_war = [0] * NSL
        ot_war = last_ot_tok

        def make_tail(h_, qs_, pv_done_val):
            st_ = {"rc": None}

            def stage_rc():
                mk.wait("vector", st, pv_done_val)
                st_["rc"] = mk.op("vector", "tensor_copy", rr[0:1, :],
                                  ps_den[0:1, :])

            def stage_rest():
                nonlocal last_ot_tok, ot_war
                mk.waitt("tensor", st_["rc"])
                mk.op("tensor", "matmul", ps_bc[:, :], c["ones_row_f32"][:],
                      rr[0:1, :], start=True, stop=True, inc=st,
                      skip_group_check=True)
                mk.wait("vector", st, st.n)
                rdb_tok = mk.op("vector", "reciprocal", rdb[:], ps_bc[:, :])
                last_ot_tok = mk.op(
                    "vector", "tensor_mul",
                    A[:, h_ * S + qs_ * TS: h_ * S + (qs_ + 1) * TS],
                    ps_ot[:, :], rdb[:])
                ot_war = last_ot_tok
                st_war[2] = rdb_tok
            return stage_rc, stage_rest

        pend_tail = None
        for qs in range(NTS):
            nkb = 4 * qs + 4
            nslab = (nkb + 1) // 2
            exp_toks = {}
            fix_toks = {}
            if pend_tail is not None:
                pend_tail[0]()
            for kb2 in range(nslab + 1):
                if kb2 < nslab:
                    sl = kb2 % NSL
                    kbs = [2 * kb2, 2 * kb2 + 1]
                    if st_war[sl] is not None:
                        mk.waitt("tensor", st_war[sl])
                    for j, kb in enumerate(kbs):
                        mk.op("tensor", "matmul",
                              ps_st[:, (sl * 2 + j) * TS:(sl * 2 + j + 1) * TS],
                              kth[:, kb * 128:(kb + 1) * 128],
                              qth[:, qs * TS:(qs + 1) * TS],
                              start=True, stop=True, skip_group_check=True,
                              inc=(st if j == len(kbs) - 1 else None))
                    st_done = st.n
                    mk.wait("scalar", st, st_done)
                    mk.wait("scalar", st, pt_war[sl])
                    win_lo = 4 * qs - 1
                    if lm_bias_cols is None or kbs[0] >= win_lo:
                        e_tok = mk.op(
                            "scalar", "activation",
                            pt[:, sl * 1024:(sl + 1) * 1024],
                            ps_st[:, sl * 2 * TS:(sl * 2 + 2) * TS],
                            AF.Exp, bias=nbc[:, 0:1], scale=SCALE_D)
                        for kb in kbs:
                            exp_toks[kb] = e_tok
                    else:
                        for j, kb in enumerate(kbs):
                            bias = (nbc[:, 0:1] if kb >= win_lo
                                    else bh[:, kb:kb + 1])
                            exp_toks[kb] = mk.op(
                                "scalar", "activation",
                                pt[:, (sl * 2 + j) * TS:(sl * 2 + j + 1) * TS],
                                ps_st[:, (sl * 2 + j) * TS:(sl * 2 + j + 1) * TS],
                                AF.Exp, bias=bias, scale=SCALE_D)
                    st_war[sl] = exp_toks[kbs[-1]]
                    prev_act_tok = st_war[sl]
                    for j, kb in enumerate(kbs):
                        ptk = pt[:, (sl * 2 + j) * TS:(sl * 2 + j + 1) * TS]
                        jd = kb - 4 * qs
                        tok = None
                        if jd >= 0:
                            mk.waitt("vector", exp_toks[kb])
                            if jd > 0:
                                tok = mk.op("vector", "memset",
                                            ptk[:, 0:jd * 128], 0.0)
                            tok = mk.op("vector", "tensor_mul",
                                        ptk[:, jd * 128:(jd + 1) * 128],
                                        ptk[:, jd * 128:(jd + 1) * 128],
                                        c["tri_keep"][:])
                            if lm_bias_cols is not None:
                                if jd < 3:
                                    mk.op("vector", "tensor_scalar", tk[:],
                                          c["low_strict"][:],
                                          lm01_cols[:, kb:kb + 1], None,
                                          op0=ALU.max)
                                    tok = mk.op(
                                        "vector", "tensor_mul",
                                        ptk[:, (jd + 1) * 128:(jd + 2) * 128],
                                        ptk[:, (jd + 1) * 128:(jd + 2) * 128],
                                        tk[:])
                                if jd < 2:
                                    tok = mk.op(
                                        "vector", "tensor_scalar",
                                        ptk[:, (jd + 2) * 128:],
                                        ptk[:, (jd + 2) * 128:],
                                        lm01_cols[:, kb:kb + 1], None,
                                        op0=ALU.mult)
                        elif lm_bias_cols is not None and jd == -1:
                            mk.waitt("vector", exp_toks[kb])
                            mk.op("vector", "tensor_scalar", tk[:],
                                  c["low_strict"][:],
                                  lm01_cols[:, kb:kb + 1], None, op0=ALU.max)
                            mk.op("vector", "tensor_mul", ptk[:, 0:128],
                                  ptk[:, 0:128], tk[:])
                            tok = mk.op("vector", "tensor_scalar",
                                        ptk[:, 128:], ptk[:, 128:],
                                        lm01_cols[:, kb:kb + 1], None,
                                        op0=ALU.mult)
                        fix_toks[kb] = tok
                if kb2 >= 1:
                    pl = (kb2 - 1) % NSL
                    pkbs = [2 * (kb2 - 1), 2 * kb2 - 1]
                    if kb2 == 1 and pend_tail is not None:
                        pend_tail[1]()
                        pend_tail = None
                    if kb2 == 1 and ot_war is not None:
                        mk.waitt("tensor", ot_war)
                    for j, kb in enumerate(pkbs):
                        if fix_toks[kb] is not None:
                            mk.waitt("tensor", fix_toks[kb])
                        else:
                            mk.waitt("tensor", exp_toks[kb])
                        ptk = pt[:, (pl * 2 + j) * TS:(pl * 2 + j + 1) * TS]
                        mk.op("tensor", "matmul", ps_den[0:1, :],
                              c["ones_col"][:], ptk,
                              start=(kb == 0), stop=(kb == nkb - 1),
                              skip_group_check=True)
                        mk.op("tensor", "matmul", ps_ot[:, :],
                              vhh[:, kb * 128:(kb + 1) * 128], ptk,
                              start=(kb == 0), stop=(kb == nkb - 1),
                              skip_group_check=True,
                              inc=(st if j == len(pkbs) - 1 else None))
                    pt_war[pl] = st.n
            pend_tail = make_tail(h, qs, st.n)
        pend_tail[0]()
        pend_tail[1]()
        pend_tail = None
        pe_head_done[h] = st.n
        dve_head_done[h] = last_ot_tok
    mk.end_phase()
    return last_ot_tok


def resid_evac_factory(mk, c, tag, R_in, R_out, bias_col=None, n_eo=16,
                       deps=(), rb_out=None):
    rt = mk.sbuf(f"{tag}_rt", [128, 2 * S], F32, phase=True)
    rn = mk.sbuf(f"{tag}_rn", [128, 2 * S], F32, phase=True)
    rld = [mk.sem(f"sm_rl{j}") for j in range(2)]
    sd = [mk.sem(f"sm_sd{j}") for j in range(2)]
    state = {"ld": [None, None], "st": [None, None], "rb": [None, None],
             "rb_rows": [None] * n_eo}
    srb = [mk.sem("sm_srb0"), mk.sem("sm_srb1")]
    for d in deps:
        if d is not None:
            mk.waitt("sync", d)

    state["ld"][0] = mk.dma("sync", rld[0], rt[:, 0:S], R_in[0:128, :])
    state["ld"][1] = mk.dma("sync", rld[1], rt[:, S:2 * S], R_in[128:256, :])

    def evac(eo, q, ps_list, pe_tok):
        ps = ps_list[0]
        sl = eo % 2
        out = rn[:, sl * S + q * 1024: sl * S + (q + 1) * 1024]
        rts = rt[:, sl * S + q * 1024: sl * S + (q + 1) * 1024]
        mk.waitt("vector", pe_tok)
        mk.waitt("vector", state["ld"][sl])
        if eo >= 2 and q == 0:
            mk.waitt("vector", state["st"][sl])
            mk.waitt("vector", state["rb"][sl])
        tok = mk.op("vector", "scalar_tensor_tensor", out, ps,
                    (bias_col[:, eo:eo + 1] if bias_col is not None else 0.0),
                    rts, op0=ALU.add, op1=ALU.add)
        if q < 3:
            return tok
        mk.waitt("sync", tok)
        state["st"][sl] = mk.dma("sync", sd[sl],
                                 R_out[eo * 128:(eo + 1) * 128, :],
                                 rn[:, sl * S:(sl + 1) * S])
        if rb_out is not None:
            mk.waitt("gpsimd", tok)
            state["rb"][sl] = mk.dma("gpsimd", srb[sl],
                                     rb_out[eo * 128:(eo + 1) * 128, :],
                                     rn[:, sl * S:(sl + 1) * S])
            state["rb_rows"][eo] = state["rb"][sl]
        if eo + 2 < n_eo:
            state["ld"][sl] = mk.dma("sync", rld[sl],
                                     rt[:, sl * S:(sl + 1) * S],
                                     R_in[(eo + 2) * 128:(eo + 3) * 128, :])
        return tok

    def finish():
        mk.wait("sync", sd[0], sd[0].n)
        mk.wait("sync", sd[1], sd[1].n)
        toks = [(sd[0], sd[0].n), (sd[1], sd[1].n)]
        if rb_out is not None:
            toks += [(srb[0], srb[0].n), (srb[1], srb[1].n)]
        return toks
    return evac, finish, state



def ffn1_evac_factory(mk, c, tag, H_s, sgate_col, sup_col, scope=None):
    gs = mk.sbuf(f"{tag}_gs", [128, 2 * 1024], F32, phase=(scope is None),
                 scope=scope)
    hst = mk.sbuf(f"{tag}_hst", [128, S], BF16, phase=(scope is None),
                  scope=scope)
    sd = [mk.sem(f"sm_sd{j}") for j in range(2)]
    state = {"st": [None, None]}

    def evac(eo, q, ps_list, pe_tok):
        ps_g, ps_u = ps_list
        gsl = gs[:, (q % 2) * 1024:(q % 2 + 1) * 1024]
        out = hst[:, q * 1024:(q + 1) * 1024]
        mk.waitt("scalar", pe_tok)
        a_tok = mk.op("scalar", "activation", gsl, ps_g, AF.Silu,
                      scale=sgate_col[:, eo:eo + 1])
        mk.waitt("vector", a_tok)
        if eo >= 1 and q == 0:
            mk.waitt("vector", state["st"][0])
        u_tok = mk.op("vector", "scalar_tensor_tensor", out, ps_u,
                      sup_col[:, eo:eo + 1], gsl, op0=ALU.mult, op1=ALU.mult)
        if q == 3:
            mk.waitt("sync", u_tok)
            state["st"][0] = mk.dma("sync", sd[eo % 2],
                                    H_s[eo * 128:(eo + 1) * 128, :],
                                    hst[:, :])
        return [a_tok, u_tok]

    def finish():
        mk.wait("sync", sd[0], sd[0].n)
        mk.wait("sync", sd[1], sd[1].n)
        return [(sd[0], sd[0].n), (sd[1], sd[1].n)]
    return evac, finish



def ffn2_phase(mk, c, tag, H_s, Wd_ap, R_in, R_out, sdown_col, A, deps=(),
               rb_out=None):
    st = mk.sem("sm_t")
    hld = [mk.sem(f"sm_hl{j}") for j in range(2)]
    wld = [mk.sem(f"sm_wl{j}") for j in range(2)]
    rld = [mk.sem(f"sm_rl{j}") for j in range(2)]
    sd = [mk.sem(f"sm_sd{j}") for j in range(2)]

    wb = mk.sbuf(f"{tag}_wb", [128, 2 * NIO * 128], BF16, phase=True)
    rt = mk.sbuf(f"{tag}_rt", [128, 2 * TS], F32, phase=True)
    rn = mk.sbuf(f"{tag}_rn", [128, 2 * TS], F32, phase=True)
    ps = mk.psum(f"{tag}_ps", [128, 2 * TS], F32)

    for d in deps:
        if d is None:
            continue
        for e in ("tensor", "vector", "sync", "gpsimd"):
            mk.waitt(e, d)

    HHALF = NIO * TS
    st_base = st.n
    pe_ts_done = [0] * (NTS + 2)
    ev_toks = {}
    htoks = [None, None]
    wtoks = [None, None]
    rtoks = [None, None]
    stoks2 = [None, None]
    rbtoks = [None, None]
    pend_rb = []
    rb_row_toks = [None] * NE
    srb = [mk.sem("sm_srb0"), mk.sem("sm_srb1")]

    def h_dma(ts):
        hsl = ts % 2
        if ts >= 2:
            mk.wait("sync", st, pe_ts_done[ts - 2])
        hsrc = H_s[:, ts * TS:(ts + 1) * TS].rearrange("(a p) c -> p a c",
                                                       p=128)
        hdst = A[:, hsl * HHALF:(hsl + 1) * HHALF].rearrange(
            "p (a c) -> p a c", c=TS)
        htoks[hsl] = mk.dma("sync", hld[hsl], hdst, hsrc)

    h_dma(0)
    for ts in range(NTS):
        hsl = ts % 2
        for eo in range(NE):
            if eo == 4 and ts + 1 < NTS:
                h_dma(ts + 1)
            wsl = eo % 2
            gi = ts * NE + eo
            if gi >= 2:
                mk.wait("gpsimd", st, st_base + gi - 1)
            wsrc = Wd_ap[:, eo * 128:(eo + 1) * 128].rearrange(
                "(a p) c -> p a c", p=128)
            wdst = wb[:, wsl * NIO * 128:(wsl + 1) * NIO * 128].rearrange(
                "p (a c) -> p a c", c=128)
            wtoks[wsl] = mk.dma("gpsimd", wld[wsl], wdst, wsrc)
            while pend_rb:
                ptok, psl_, peo, pts = pend_rb.pop(0)
                mk.waitt("gpsimd", ptok)
                rbtoks[psl_] = mk.dma("gpsimd", srb[psl_],
                                      rb_out[peo * 128:(peo + 1) * 128,
                                             pts * TS:(pts + 1) * TS],
                                      rn[:, psl_ * TS:(psl_ + 1) * TS])
                if pts == NTS - 1:
                    rb_row_toks[peo] = rbtoks[psl_]
            if gi >= 2:
                mk.waitt("sync", ev_toks[gi - 2])
            rtoks[wsl] = mk.dma("sync", rld[wsl],
                                rt[:, wsl * TS:(wsl + 1) * TS],
                                R_in[eo * 128:(eo + 1) * 128,
                                     ts * TS:(ts + 1) * TS])
            pass
            mk.waitt("tensor", wtoks[wsl])
            mk.waitt("tensor", htoks[hsl])
            prev = ev_toks.get(gi - 2)
            if prev is not None:
                mk.waitt("tensor", prev)
            psl = ps[:, wsl * TS:(wsl + 1) * TS]
            wbs = wb[:, wsl * NIO * 128:(wsl + 1) * NIO * 128]
            for i in range(NIO):
                mk.op("tensor", "matmul", psl,
                      wbs[:, i * 128:(i + 1) * 128],
                      A[:, hsl * HHALF + i * TS: hsl * HHALF + (i + 1) * TS],
                      start=(i == 0), stop=(i == NIO - 1),
                      inc=(st if i == NIO - 1 else None))
            mk.wait("vector", st, st.n)
            mk.waitt("vector", rtoks[wsl])
            if gi >= 2:
                mk.waitt("vector", stoks2[wsl])
                if rb_out is not None and rbtoks[wsl] is not None:
                    mk.waitt("vector", rbtoks[wsl])
            tok = mk.op("vector", "scalar_tensor_tensor",
                        rn[:, wsl * TS:(wsl + 1) * TS], psl,
                        sdown_col[:, eo:eo + 1], rt[:, wsl * TS:(wsl + 1) * TS],
                        op0=ALU.mult, op1=ALU.add)
            ev_toks[gi] = tok
            mk.waitt("sync", tok)
            stoks2[wsl] = mk.dma("sync", sd[wsl],
                                 R_out[eo * 128:(eo + 1) * 128,
                                       ts * TS:(ts + 1) * TS],
                                 rn[:, wsl * TS:(wsl + 1) * TS])
            if rb_out is not None:
                pend_rb.append((tok, wsl, eo, ts))
        pe_ts_done[ts] = st.n
    for ptok, psl_, peo, pts in pend_rb:
        mk.waitt("gpsimd", ptok)
        rbtoks[psl_] = mk.dma("gpsimd", srb[psl_],
                              rb_out[peo * 128:(peo + 1) * 128,
                                     pts * TS:(pts + 1) * TS],
                              rn[:, psl_ * TS:(psl_ + 1) * TS])
        if pts == NTS - 1:
            rb_row_toks[peo] = rbtoks[psl_]
    fin = [(sd[0], sd[0].n), (sd[1], sd[1].n)]
    if rb_out is not None:
        fin += [(srb[0], srb[0].n), (srb[1], srb[1].n)]
    for s_, v_ in fin:
        mk.wait("sync", s_, v_)
    mk.end_phase()
    half_free = [(st, pe_ts_done[NTS - 2]), (st, pe_ts_done[NTS - 1])]
    return ev_toks[NTS * NE - 1], fin, rb_row_toks, half_free


class MaskPipeline:

    def __init__(self, mk, c, tag, A, var_row, mbar, score_row, scope,
                 xn_tok):
        self.mk, self.c, self.tag = mk, c, tag
        self.A, self.var_row, self.mbar = A, var_row, mbar
        self.score_row = score_row
        self.xn_tok = xn_tok
        self.prodb = mk.sbuf(f"{tag}_prod", [128, S // 2], BF16, scope=scope)
        self.d32 = mk.sbuf(f"{tag}_d32", [128, 32], F32, scope=scope)
        self.dT = mk.sbuf(f"{tag}_dT", [32, 128], F32, scope=scope)
        self.rowin = mk.sbuf(f"{tag}_rin", [1, 2 * TS], F32, scope=scope)
        self.sc = mk.sbuf(f"{tag}_sc", [1, 64], F32, scope=scope)
        self.rowt = mk.sbuf(f"{tag}_rowt", [1, 4 * TS], F32, scope=scope)
        self.msq = mk.sbuf(f"{tag}_msq", [128, NE], F32, scope=scope)
        self.ps_dot = mk.psum(f"{tag}_psd", [128, 32], F32, scope=scope)
        self.ps_t = mk.psum(f"{tag}_pst", [32, 128], F32, scope=scope)
        self.dot_dram = mk.nc.dram_tensor(f"{tag}_dram", [S], F32,
                                          kind="Internal").ap()
        self.st = mk.sem(f"{tag}_t")
        self.sdd = mk.sem(f"{tag}_sdd")
        self.rls = mk.sem(f"{tag}_rls")
        self.i = 0
        self.st0 = None
        self.sc_tok = None
        self.bis = Bisect(mk, c, tag + "b", score_row, scope)

    def _dot(self, i):
        mk, c = self.mk, self.c
        if i == 0:
            self.st0 = self.st.n
            mk.waitt("vector", self.xn_tok)
            mk.waitt("tensor", self.xn_tok)
        for h in range(2):
            if i > 0 or h > 0:
                mk.wait("vector", self.st, self.st0 + 2 * i + h)
            ptok = mk.op("vector", "tensor_scalar", self.prodb[:],
                         self.A[:, i * S + h * (S // 2):
                                i * S + (h + 1) * (S // 2)],
                         self.mbar[:, i:i + 1], None, op0=ALU.mult)
            mk.waitt("tensor", ptok)
            for b in range(16):
                bb = h * 16 + b
                mk.op("tensor", "matmul", self.ps_dot[:, bb:bb + 1],
                      self.prodb[:, b * 128:(b + 1) * 128], c["ones_col"][:],
                      start=(i == 0), stop=(i == NE - 1),
                      inc=(self.st if b == 15 else None),
                      skip_group_check=True)

    def _trans(self):
        mk, c = self.mk, self.c
        mk.wait("vector", self.st, self.st.n)
        t1 = mk.op("vector", "tensor_copy", self.d32[:], self.ps_dot[:, :])
        mk.waitt("tensor", t1)
        mk.op("tensor", "transpose", self.ps_t[:, :], self.d32[:],
              c["ident_f32"][:], inc=self.st)
        mk.wait("vector", self.st, self.st.n)
        t2 = mk.op("vector", "tensor_copy", self.dT[:, :], self.ps_t[:, :])
        mk.waitt("sync", t2)
        mk.dma("sync", self.sdd,
               self.dot_dram.rearrange("(b p) -> b p", p=128), self.dT[:, :])

    def _mnorm(self):
        mk, c, sc = self.mk, self.c, self.sc
        mn, s0, s1 = sc[0:1, 1:2], sc[0:1, 2:3], sc[0:1, 3:4]
        msq_tok = mk.op("vector", "tensor_mul", self.msq[:], self.mbar[:],
                        self.mbar[:])
        mk.waitt("tensor", msq_tok)
        mk.op("tensor", "matmul", self.ps_t[0:1, 0:16], c["ones_col_f32"][:],
              self.msq[:], start=True, stop=True, inc=self.st,
              skip_group_check=True)
        mk.wait("vector", self.st, self.st.n)
        mn_tok = mk.op("vector", "reduce_sum", mn, self.ps_t[0:1, 0:16],
                       axis=AX.X)
        mk.waitt("scalar", mn_tok)
        sq_tok = mk.op("scalar", "activation", s0, mn, AF.Sqrt)
        mk.waitt("vector", sq_tok)
        for _ in range(2):
            mk.op("vector", "reciprocal", s1, s0)
            mk.op("vector", "tensor_mul", s1, s1, mn)
            mk.op("vector", "tensor_add", s0, s0, s1)
            mk.op("vector", "tensor_scalar", s0, s0, 0.5, None, op0=ALU.mult)

    def _score(self, ts):
        mk, c = self.mk, self.c
        rowt, rowin = self.rowt, self.rowin
        s0 = self.sc[0:1, 2:3]
        v_, r_, n_, x_ = (rowt[0:1, 0:TS], rowt[0:1, TS:2 * TS],
                          rowt[0:1, 2 * TS:3 * TS], rowt[0:1, 3 * TS:4 * TS])
        vin, din = rowin[0:1, 0:TS], rowin[0:1, TS:2 * TS]
        tsl = slice(ts * TS, (ts + 1) * TS)
        if ts == 0:
            mk.wait("sync", self.sdd, self.sdd.n)
        if self.sc_tok is not None:
            mk.waitt("sync", self.sc_tok)
        mk.dma("sync", self.rls, vin, self.var_row[0:1, tsl])
        rltok = mk.dma("sync", self.rls, din, self.dot_dram[None, tsl])
        mk.waitt("vector", rltok)
        mk.op("vector", "tensor_scalar", v_, vin, EPS, None, op0=ALU.add)
        mk.op("vector", "reciprocal", x_, v_)
        mk.op("vector", "tensor_mul", x_, x_, vin)
        wtok = mk.op("vector", "tensor_scalar", n_, x_, float(E), None,
                     op0=ALU.mult)
        mk.waitt("scalar", wtok)
        stok = mk.op("scalar", "activation", x_, n_, AF.Sqrt)
        mk.waitt("vector", stok)
        mk.op("vector", "reciprocal", r_, x_)
        for _ in range(2):
            mk.op("vector", "tensor_mul", x_, r_, r_)
            mk.op("vector", "tensor_mul", x_, x_, n_)
            mk.op("vector", "tensor_scalar", x_, x_, -0.5, 1.5,
                  op0=ALU.mult, op1=ALU.add)
            mk.op("vector", "tensor_mul", r_, r_, x_)
        mk.op("vector", "tensor_mul", n_, n_, r_)
        mk.op("vector", "tensor_scalar", v_, n_, s0, 1e-6,
              op0=ALU.mult, op1=ALU.add)
        mk.op("vector", "reciprocal", v_, v_)
        mk.op("vector", "tensor_mul", x_, din, v_)
        mk.op("vector", "tensor_scalar", n_, n_, ALPHA, None, op0=ALU.mult)
        self.sc_tok = mk.op("vector", "scalar_tensor_tensor",
                            self.score_row[0:1, tsl], x_, -(1.0 - ALPHA),
                            n_, op0=ALU.mult, op1=ALU.add)

    def step(self):
        i = self.i
        if i < NE:
            self._dot(i)
        elif i == NE:
            self._trans()
        elif i == NE + 1:
            self._mnorm()
        elif i < NE + 2 + NTS:
            self._score(i - NE - 2)
        elif i == NE + 2 + NTS:
            self.bis.start()
        else:
            self.bis.step()
        self.i += 1

    def finish(self):
        while not (self.i > NE + 2 + NTS and self.bis.done_tok is not None):
            self.step()


class Bisect:

    NITER = 36

    def __init__(self, mk, c, tag, score_row, scope):
        self.mk = mk
        self.c = c
        self.tag = tag
        self.score = score_row
        self.row = mk.sbuf(f"{tag}_brow", [1, S], BF16, scope=scope)
        self.lm = mk.sbuf(f"{tag}_lm", [1, S], BF16, scope=scope)
        self.sc = mk.sbuf(f"{tag}_bsc", [1, 64], F32, scope=scope)
        self.ci = mk.sbuf(f"{tag}_bci", [1, 8], mybir.dt.uint8, scope=scope)
        self.i = 0
        self.done_tok = None

    def start(self):
        mk = self.mk
        mk.op("vector", "tensor_reduce", self.sc[0:1, 0:1],
              self.score[0:1, :], axis=AX.X, op=ALU.min)
        mk.op("vector", "reduce_max", self.sc[0:1, 1:2], self.score[0:1, :],
              axis=AX.X)

    def step(self):
        mk, sc = self.mk, self.sc
        lo, hi, tau, cnt, c01 = (sc[0:1, 0:1], sc[0:1, 1:2], sc[0:1, 2:3],
                                 sc[0:1, 3:4], sc[0:1, 4:5])
        if self.i < self.NITER:
            mk.op("vector", "tensor_add", tau, lo, hi)
            mk.op("vector", "tensor_scalar", tau, tau, 0.5, None, op0=ALU.mult)
            mk.op("vector", "tensor_single_scalar", self.row[0:1, :],
                  self.score[0:1, :], tau, op=ALU.is_ge)
            mk.op("vector", "reduce_sum", cnt, self.row[0:1, :], axis=AX.X)
            mk.op("vector", "tensor_single_scalar", self.ci[0:1, 0:1], cnt,
                  float(KEEP_K), op=ALU.is_ge)
            mk.op("vector", "tensor_single_scalar", self.ci[0:1, 1:2], cnt,
                  float(KEEP_K), op=ALU.is_lt)
            mk.op("vector", "copy_predicated", lo, self.ci[0:1, 0:1], tau)
            mk.op("vector", "copy_predicated", hi, self.ci[0:1, 1:2], tau)
        elif self.i == self.NITER:
            lm0, lmd = self.lm[0:1, 0:S], self.row[0:1, 0:S]
            mk.op("vector", "tensor_single_scalar", lm0, self.score[0:1, :],
                  lo, op=ALU.is_ge)
            for _ in range(HOPS):
                mk.op("vector", "tensor_copy", lmd, lm0)
                mk.op("vector", "tensor_max", lmd[0:1, 1:S], lmd[0:1, 1:S],
                      lm0[0:1, 0:S - 1])
                mk.op("vector", "tensor_max", lmd[0:1, 0:S - 1],
                      lmd[0:1, 0:S - 1], lm0[0:1, 1:S])
                tok = mk.op("vector", "tensor_copy", lm0, lmd)
            self.done_tok = tok
        self.i += 1

    def finish_steps(self):
        while self.done_tok is None:
            self.step()


def mask_cols(mk, c, tag, bis, lm01_cols, lm_bias_cols, deps=()):
    sdd = mk.sem(f"{tag}_sdd")
    st = mk.sem(f"{tag}_t")
    dram = mk.nc.dram_tensor(f"{tag}_dram", [S], BF16, kind="Internal").ap()
    dT = mk.sbuf(f"{tag}_dT", [32, 128], BF16, phase=True)
    ps_t = mk.psum(f"{tag}_pst", [128, 32], F32)
    for d in deps:
        if d is None:
            continue
        for e in ("tensor", "vector", "sync"):
            mk.waitt(e, d)
    bis.finish_steps()
    mk.waitt("sync", bis.done_tok)
    mk.dma("sync", sdd, dram[None, :], bis.lm[0:1, 0:S])
    mk.wait("sync", sdd, sdd.n)
    mk.dma("sync", sdd, dT[:, :], dram.rearrange("(b p) -> b p", p=128))
    mk.wait("tensor", sdd, sdd.n)
    mk.op("tensor", "matmul", ps_t[:, 0:32], dT[:, :],
          c["ident"][0:32, 0:32], start=True, stop=True, inc=st,
          skip_group_check=True)
    mk.wait("vector", st, st.n)
    mk.op("vector", "tensor_copy", lm01_cols[:], ps_t[:, 0:32])
    tok = mk.op("vector", "tensor_scalar", lm_bias_cols[:], lm01_cols[:],
                1e4, -1e4, op0=ALU.mult, op1=ALU.add)
    mk.end_phase()
    return tok



def build_core_program():
    nc = bass.Bass("TRN2", target_bir_lowering=False)
    dr = lambda n, s, d, k="ExternalInput": nc.dram_tensor(n, s, d, kind=k).ap()
    io = {
        "xT": dr("xT", [E, S], F32),
        "Wqa": dr("Wqa", [E, E], F32), "bqa": dr("bqa", [E], F32),
        "Wka": dr("Wka", [E, E], F32), "bka": dr("bka", [E], F32),
        "Wva": dr("Wva", [E, E], F32), "bva": dr("bva", [E], F32),
        "Wca": dr("Wca", [E, E], F32), "bca": dr("bca", [E], F32),
        "Wqb": dr("Wqb", [E, E], F32),
        "Wkb": dr("Wkb", [E, 1024], F32), "Wvb": dr("Wvb", [E, 1024], F32),
        "Wob": dr("Wob", [E, E], F32),
        "wg": dr("wg", [E, INTER], I8), "wu": dr("wu", [E, INTER], I8),
        "wd": dr("wd", [INTER, E], I8),
        "sg": dr("sg", [INTER], F32), "su": dr("su", [INTER], F32),
        "sdn": dr("sdn", [E], F32),
        "yT": dr("yT", [E, S], F32, "ExternalOutput"),
    }
    R = dr("R_scr", [E, S], F32, "Internal")
    Rb_s = dr("Rb_scr", [E, S], BF16, "Internal")
    QT_s = dr("QT_scr", [E, S], BF16, "Internal")
    KT_s = dr("KT_scr", [E, S], BF16, "Internal")
    V_s = dr("V_scr", [S, E], BF16, "Internal")
    H_s = dr("H_scr", [INTER, S], BF16, "Internal")

    mk = MK(nc)
    c = build_consts(mk)
    A = mk.sbuf("A", [128, NE * S], BF16)
    bias = mk.sbuf("bias", [128, 2 * NE], F32)
    scales = mk.sbuf("scales", [128, 2 * NIO + NE], F32)
    sg_col = scales[:, 0:NIO]
    su_col = scales[:, NIO:2 * NIO]
    sdn_col = scales[:, 2 * NIO:2 * NIO + NE]

    for i, w in enumerate(["bva", "bca"]):
        load_vec_cols(mk, c, f"l{w}", io[w], bias[:, i * NE:(i + 1) * NE], NE)
    load_vec_cols(mk, c, "lsg", io["sg"], sg_col, NIO)
    load_vec_cols(mk, c, "lsu", io["su"], su_col, NIO)
    ldt = load_vec_cols(mk, c, "lsdn", io["sdn"], sdn_col, NE)
    mk.end_phase()
    movs = [A[:, i * S:(i + 1) * S] for i in range(NE)]

    def hookify(evac, hook):
        if hook is None:
            return evac

        def e2(eo, q, ps_list, pe_tok):
            r = evac(eo, q, ps_list, pe_tok)
            hook()
            return r
        return e2

    def qkv_block(subtag, Wq, Wk, Wv, nk, dep=None, hook=None, mov_ts=None,
                  psum_dep=None, prealloc=None, wdep=()):
        prev = dep
        stoks = []
        for j, (nm, Wa, n_eo) in enumerate((("q", Wq, NE), ("k", Wk, nk),
                                            ("v", Wv, nk))):
            out = {"q": QT_s, "k": KT_s, "v": V_s}[nm]
            ev, fin = qkv_evac_factory(mk, c, f"{subtag}{nm}", out,
                                       vtrans=(nm == "v"))
            for s_, v_ in stoks:
                mk.wait("vector", s_, v_)
            prev = mm_pass(mk, c, f"{subtag}m{nm}", movs, [{"ap": Wa}], n_eo,
                           hookify(ev, hook), mov_tok=dep, deps=[prev],
                           mov_ts=(mov_ts if j == 0 else None),
                           psum_dep=(psum_dep if j == 0 else None),
                           prealloc=prealloc,
                           wdep=(wdep if j == 0 else (prev,)))
            stoks = fin()
        return prev, stoks

    wsc1 = []
    ev, fin = qkv_evac_factory(mk, c, "s1v", QT_s, bias_col=bias[:, 0:NE],
                               scope=wsc1)
    wb1 = mm_prealloc(mk, "s1mv", NE, 1, wsc1)
    r1 = ln_phase(mk, c, io["xT"], A, "ln1", deps=[ldt], src_bf16=True,
                  cast_load=True)
    p0 = mm_pass(mk, c, "s1mv", movs, [{"ap": io["Wva"]}], NE, ev,
                 mov_ts=r1["xn_ts"], psum_dep=r1["psum_free"], wdep=[ldt],
                 prealloc=wb1)
    stoks = fin()
    mk.release(wsc1)
    ldv = mk.sem("sm_ldv")
    for s_, v_ in stoks:
        mk.wait("sync", s_, v_)
    mk.waitt("sync", p0)
    for i in range(NE):
        at1 = mk.dma("sync", ldv, A[:, i * S:(i + 1) * S],
                     QT_s[i * 128:(i + 1) * 128, :])
    ev, fin, rpst1 = resid_evac_factory(mk, c, "rp1", io["xT"], R,
                                        bias_col=bias[:, NE:2 * NE],
                                        deps=[p0], rb_out=Rb_s)
    p1 = mm_pass(mk, c, "mmc1", movs, [{"ap": io["Wca"]}], NE, ev,
                 mov_tok=at1, deps=[p0])
    rtoks1 = fin()

    wsc2 = []
    ev, fin = ffn1_evac_factory(mk, c, "f1", H_s, sg_col, su_col, scope=wsc2)
    wb2 = mm_prealloc(mk, "mmf1", NE, 2, wsc2)
    r2 = ln_phase(mk, c, Rb_s, A, "ln2", src_bf16=True,
                  deps=[p1] + rtoks1, row_toks=rpst1["rb_rows"],
                  load_dep_half=[p1, p1])
    p2 = mm_pass(mk, c, "mmf1", movs, [{"ap": io["wg"]}, {"ap": io["wu"]}],
                 NIO, ev, mov_ts=r2["xn_ts"], psum_dep=r2["psum_free"],
                 wdep=[p1] + rtoks1, prealloc=wb2)
    htoks = fin()
    mk.release(wsc2)
    p3, rtoks2, rb2_rows, hfree2 = ffn2_phase(mk, c, "f2", H_s, io["wd"],
                                              R, R, sdn_col, A,
                                              deps=[p2] + htoks, rb_out=Rb_s)

    heldlm = []
    lm01_cols = mk.sbuf("lm01c", [128, 32], F32, scope=heldlm)
    lmb_cols = mk.sbuf("lmbc", [128, 32], F32, scope=heldlm)
    heldsc = []
    score_row = mk.sbuf("score3", [1, S], F32, scope=heldsc)
    var_row = dr("var3_scr", [1, S], F32, "Internal")
    mbar = mk.sbuf("mbar3", [128, NE], F32, scope=heldsc)
    wb3 = mm_prealloc(mk, "s3w", NE, 1, heldsc)
    r3 = ln_phase(mk, c, Rb_s, A, "ln3", src_bf16=True, var_row=var_row,
                  mbar=mbar, deps=[p3] + rtoks2, row_toks=rb2_rows,
                  load_dep_half=hfree2)
    msk = MaskPipeline(mk, c, "ms3", A, var_row, mbar, score_row, heldsc,
                       r3["xn_done"])
    prev, stoks = qkv_block("s3", io["Wqb"], io["Wkb"], io["Wvb"], 8,
                            dep=r3["xn_done"], hook=msk.step,
                            mov_ts=r3["xn_ts"], psum_dep=r3["psum_free"],
                            prealloc=wb3, wdep=[p3] + rtoks2)
    msk.finish()
    cm_tok = mask_cols(mk, c, "mc3", msk.bis, lm01_cols, lmb_cols,
                       deps=[prev])
    mk.release(heldsc)
    at3 = attn_core(mk, c, "at3", QT_s, KT_s, V_s, A, kv_group=2,
                    lm01_cols=lm01_cols, lm_bias_cols=lmb_cols,
                    deps=[prev, cm_tok] + stoks)
    ev, fin, rpst3 = resid_evac_factory(mk, c, "rp3", R, R, deps=[at3],
                                        rb_out=Rb_s)
    p4 = mm_pass(mk, c, "mmc3", movs, [{"ap": io["Wob"]}], NE, ev,
                 mov_tok=at3, deps=[at3])
    rtoks3 = fin()
    mk.release(heldlm)

    wsc4 = []
    ev, fin = ffn1_evac_factory(mk, c, "f3", H_s, sg_col, su_col, scope=wsc4)
    wb4 = mm_prealloc(mk, "mmf3", NE, 2, wsc4)
    r4 = ln_phase(mk, c, Rb_s, A, "ln4", src_bf16=True,
                  deps=[p4] + rtoks3, row_toks=rpst3["rb_rows"],
                  load_dep_half=[p4, p4])
    p5 = mm_pass(mk, c, "mmf3", movs, [{"ap": io["wg"]}, {"ap": io["wu"]}],
                 NIO, ev, mov_ts=r4["xn_ts"], psum_dep=r4["psum_free"],
                 wdep=[p4] + rtoks3, prealloc=wb4)
    htoks = fin()
    mk.release(wsc4)
    ffn2_phase(mk, c, "f4", H_s, io["wd"], R, io["yT"], sdn_col, A,
               deps=[p5] + htoks)

    mk.play()
    mk.close()
    return nc


def kernel(**inputs):
    nc = build_core_program()
    B = 8
    x = inputs["L0_attn_a_x"]
    in_maps = []
    for b in range(B):
        m = {
            "xT": np.ascontiguousarray(np.asarray(x[b]).T),
            "Wqa": inputs["L0_attn_a_Wq"], "bqa": inputs["L0_attn_a_bq"],
            "Wka": inputs["L0_attn_a_Wk"], "bka": inputs["L0_attn_a_bk"],
            "Wva": inputs["L0_attn_a_Wv"], "bva": inputs["L0_attn_a_bv"],
            "Wca": inputs["L0_attn_a_Wc"], "bca": inputs["L0_attn_a_bc"],
            "Wqb": inputs["L0_attn_b_Wq"], "Wkb": inputs["L0_attn_b_Wk"],
            "Wvb": inputs["L0_attn_b_Wv"], "Wob": inputs["L0_attn_b_Wo"],
            "wg": inputs["L0_ffn_w_gate"], "wu": inputs["L0_ffn_w_up"],
            "wd": inputs["L0_ffn_w_down"],
            "sg": inputs["L0_ffn_s_gate"], "su": inputs["L0_ffn_s_up"],
            "sdn": inputs["L0_ffn_s_down"],
        }
        in_maps.append({k: np.asarray(v) for k, v in m.items()})
    from concourse.bass_utils import run_bass_kernel_spmd
    res = run_bass_kernel_spmd(nc, in_maps, core_ids=list(range(B)))
    out = np.stack([res.results[b]["yT"].T for b in range(B)], axis=0)
    return np.ascontiguousarray(out.astype(np.float32))


# version 18
# speedup vs baseline: 1.0816x; 1.0306x over previous
import math
import numpy as np
import concourse.bass as bass
import concourse.mybir as mybir

F32 = mybir.dt.float32
BF16 = mybir.dt.bfloat16
I8 = mybir.dt.int8
AF = mybir.ActivationFunctionType
ALU = mybir.AluOpType
AX = mybir.AxisListType

E = 2048
S = 4096
NE = 16
NTS = 8
TS = 512
INTER = 8192
NIO = 64
EPS = 1e-5
WIN = 128
HOPS = 2
ALPHA = 0.7
KEEP_K = int(0.33 * S)
SCALE_D = 1.0 / math.sqrt(128.0)
NEGBIG = -30000.0


class Sem:
    def __init__(self, h):
        self.h = h
        self.n = 0

    def bump(self, k=1):
        self.n += k
        return self.n


class MK:

    ENGINES = ("tensor", "vector", "scalar", "gpsimd", "sync")

    def __init__(self, nc):
        self.nc = nc
        self.q = {e: [] for e in self.ENGINES}
        self.sems = {}
        self._ctx = []
        self._phase_ctx = []

    CHAINED = ("vector", "scalar", "gpsimd")

    def op(self, eng, method, *args, inc=None, **kwargs):
        if inc is not None and not isinstance(inc, tuple):
            inc = (inc, 1)
        if eng in self.CHAINED and method != "dma_start":
            assert inc is None
            ctr = self.sem("ctr_" + eng)
            if ctr.n > 0:
                self.q[eng].append(
                    lambda e, _s=ctr.h, _v=ctr.n: e.wait_ge(_s, _v))
            inc = (ctr, 1)

        def fn(e, _m=method, _a=args, _k=kwargs, _i=inc):
            ins = getattr(e, _m)(*_a, **_k)
            if _i is not None:
                ins.then_inc(_i[0].h, _i[1])
        self.q[eng].append(fn)
        if inc is not None:
            inc[0].bump(inc[1])
            return (inc[0], inc[0].n)
        return None

    def waitt(self, eng, tok):
        if tok is not None:
            self.wait(eng, tok[0], tok[1])

    def wait(self, eng, sem, val=None):
        v = sem.n if val is None else val
        if v > 0:
            self.q[eng].append(lambda e, _s=sem.h, _v=v: e.wait_ge(_s, _v))

    def tick(self, eng, sem, k=1):
        self.op(eng, "nop", inc=(sem, k))

    def dma(self, q, sem, out, in_):
        return self.op(q, "dma_start", out=out, in_=in_, inc=(sem, 16))

    def dma_p(self, q, out, in_):
        if not hasattr(self, "_dma_rr"):
            self._dma_rr = {}
        i = self._dma_rr[q] = (self._dma_rr.get(q, -1) + 1) % 8
        sem = self.sem(f"dpool_{q}{i}")
        if sem.n > 0:
            self.wait(q, sem, sem.n)
        return self.op(q, "dma_start", out=out, in_=in_, inc=(sem, 16))

    def barrier(self, engines=("tensor", "vector", "scalar")):
        bar = self.sem("bar")
        for e in engines:
            self.op(e, "nop", inc=(bar, 1))
        tgt = bar.n
        for e in engines:
            self.wait(e, bar, tgt)

    def play(self):
        nc = self.nc
        with nc.Block() as block:
            @block.tensor
            def _(eng):
                for fn in self.q["tensor"]:
                    fn(eng)

            @block.vector
            def _(eng):
                for fn in self.q["vector"]:
                    fn(eng)

            @block.scalar
            def _(eng):
                for fn in self.q["scalar"]:
                    fn(eng)

            @block.gpsimd
            def _(eng):
                for fn in self.q["gpsimd"]:
                    fn(eng)

            @block.sync
            def _(eng):
                for fn in self.q["sync"]:
                    fn(eng)

    def sem(self, name):
        if name not in self.sems:
            g = self.nc.semaphore(name)
            h = g.__enter__()
            self._ctx.append(g)
            self.sems[name] = Sem(h)
        return self.sems[name]

    def sbuf(self, name, shape, dtype, phase=False, scope=None):
        g = self.nc.sbuf_tensor(name, shape, dtype)
        t = g.__enter__()
        if scope is not None:
            scope.append(g)
        else:
            (self._phase_ctx if phase else self._ctx).append(g)
        return t

    def release(self, scope):
        for g in reversed(scope):
            g.__exit__(None, None, None)
        scope.clear()

    def psum(self, name, shape, dtype=F32, phase=True, scope=None):
        g = self.nc.psum_tensor(name, shape, dtype)
        t = g.__enter__()
        if scope is not None:
            scope.append(g)
        else:
            (self._phase_ctx if phase else self._ctx).append(g)
        return t

    def end_phase(self):
        for g in reversed(self._phase_ctx):
            g.__exit__(None, None, None)
        self._phase_ctx = []

    def close(self):
        self.end_phase()
        for g in reversed(self._ctx):
            g.__exit__(None, None, None)
        self._ctx = []



def build_consts(mk):
    c = {}
    c["ones_col"] = mk.sbuf("c_ones_col", [128, 1], BF16)
    c["ones_col2"] = mk.sbuf("c_ones_col2", [128, 2], BF16)
    c["ones_col_f32"] = mk.sbuf("c_ones_col32", [128, 1], F32)
    c["ones_row"] = mk.sbuf("c_ones_row", [1, 128], BF16)
    c["ones_row_f32"] = mk.sbuf("c_ones_row32", [1, 128], F32)
    c["ident"] = mk.sbuf("c_ident", [128, 128], BF16)
    c["ident32f"] = mk.sbuf("c_ident32f", [32, 32], F32)
    c["ident_f32"] = mk.sbuf("c_ident_f32", [128, 128], F32)
    c["tri_keep"] = mk.sbuf("c_tri_keep", [128, 128], BF16)
    c["low_strict"] = mk.sbuf("c_low_strict", [128, 128], BF16)

    mk.op("gpsimd", "memset", c["ones_col"][:], 1.0)
    mk.op("gpsimd", "memset", c["ones_col2"][:], 1.0)
    mk.op("gpsimd", "memset", c["ones_col_f32"][:], 1.0)
    mk.op("gpsimd", "memset", c["ones_row"][:], 1.0)
    mk.op("gpsimd", "memset", c["ones_row_f32"][:], 1.0)
    mk.op("gpsimd", "memset", c["ident"][:], 0.0)
    mk.op("gpsimd", "memset", c["ident32f"][:], 0.0)
    mk.op("gpsimd", "memset", c["ident_f32"][:], 0.0)
    mk.op("gpsimd", "memset", c["tri_keep"][:], 0.0)
    mk.op("gpsimd", "memset", c["low_strict"][:], 0.0)
    mk.op("gpsimd", "affine_select", out=c["ident"][:], in_=c["ident"][:],
          compare_op=ALU.not_equal, fill=1.0, base=0,
          pattern=[[-1, 128]], channel_multiplier=1)
    mk.op("gpsimd", "affine_select", out=c["ident32f"][:], in_=c["ident32f"][:],
          compare_op=ALU.not_equal, fill=1.0, base=0,
          pattern=[[-1, 32]], channel_multiplier=1)
    mk.op("gpsimd", "affine_select", out=c["ident_f32"][:], in_=c["ident_f32"][:],
          compare_op=ALU.not_equal, fill=1.0, base=0,
          pattern=[[-1, 128]], channel_multiplier=1)
    mk.op("gpsimd", "affine_select", out=c["tri_keep"][:], in_=c["tri_keep"][:],
          compare_op=ALU.is_gt, fill=1.0, base=0,
          pattern=[[-1, 128]], channel_multiplier=1)
    tok = mk.op("gpsimd", "affine_select", out=c["low_strict"][:],
                in_=c["low_strict"][:], compare_op=ALU.is_ge, fill=1.0, base=0,
                pattern=[[1, 128]], channel_multiplier=-1)
    for e in ("tensor", "vector", "scalar", "sync"):
        mk.waitt(e, tok)
    return c



def ln_phase(mk, c, R_ap, A, tag, var_row=None, mbar=None, src_bf16=False,
             deps=(), row_toks=None, load_dep_half=None,
             cast_load=False):
    st = mk.sem(f"{tag}_t")
    for d in deps:
        if d is None:
            continue
        for e in ("tensor", "vector", "scalar", "gpsimd"):
            mk.waitt(e, d)

    sqb = mk.sbuf(f"{tag}_sqb", [128, S], BF16, phase=True)
    meanh = mk.sbuf(f"{tag}_meanh", [1, S // 2], F32, phase=True)
    scr = mk.sbuf(f"{tag}_scr", [1, 3 * TS], F32, phase=True)
    varh = mk.sbuf(f"{tag}_varh", [1, S // 2], F32, phase=True)
    tmp = mk.sbuf(f"{tag}_tmp", [128, 2 * TS], F32, phase=True)
    bcs = mk.sbuf(f"{tag}_bcs", [128, 2 * TS], F32, phase=True)
    if cast_load:
        rt2 = mk.sbuf(f"{tag}_rt2", [128, S], F32, phase=True)
    ps_sum = mk.psum(f"{tag}_ps_sum", [1, S // 2], F32)
    ps_bc = [mk.psum(f"{tag}_ps_bc{j}", [128, 2 * TS], F32)
             for j in range(2)]
    svr = mk.sem("sm_svr")

    cast_done = [None] * NE
    for i in range(NE):
        q_ = "gpsimd" if (cast_load and i % 2 == 0) else "sync"
        if load_dep_half is not None:
            mk.waitt(q_, load_dep_half[i // 8])
        if row_toks is not None and row_toks[i] is not None:
            mk.waitt(q_, row_toks[i])
        if cast_load and i % 2 == 1:
            if i >= 3:
                mk.waitt("sync", cast_done[i - 2])
            dtok = mk.dma_p("sync", rt2[:, :],
                            R_ap[i * 128:(i + 1) * 128, :])
            mk.waitt("vector", dtok)
            cast_done[i] = mk.op("vector", "tensor_copy",
                                 A[:, i * S:(i + 1) * S], rt2[:, :])
        else:
            cast_done[i] = mk.dma_p(q_, A[:, i * S:(i + 1) * S],
                                    R_ap[i * 128:(i + 1) * 128, :])
    for d in deps:
        if d is not None:
            mk.waitt("sync", d)

    xn_ts = [None] * NTS
    sq_done = {}
    mk.waitt("vector", cast_done[-1])
    for half in range(2):
        hts = range(half * 4, half * 4 + 4)
        if half == 1:
            mk.waitt("tensor", varh_tok)
        for i in range(NE):
            if half == 0:
                mk.waitt("tensor", cast_done[i])
            ch = A[:, i * S:(i + 1) * S]
            for j, t_ in enumerate(hts):
                last = (i == NE - 1 and j == 3)
                mk.op("tensor", "matmul", ps_sum[0:1, j * TS:(j + 1) * TS],
                      c["ones_col"][:], ch[:, t_ * TS:(t_ + 1) * TS],
                      start=(i == 0), stop=(i == NE - 1),
                      inc=(st if last else None))
        mk.wait("vector", st, st.n)
        mean_tok = mk.op("vector", "tensor_scalar", meanh[0:1, :],
                         ps_sum[0:1, :], 1.0 / E, None, op0=ALU.mult)
        S2 = S // 2
        for i in range(NE):
            ch = A[:, i * S + half * S2: i * S + (half + 1) * S2]
            gi = half * NE + i
            slot = sqb[:, (i % 2) * S2:(i % 2 + 1) * S2]
            if gi >= 2:
                mk.wait("vector", st, sq_done[gi - 2])
            sq_tok = mk.op("vector", "tensor_mul", slot, ch, ch)
            mk.waitt("tensor", sq_tok)
            if i == 0:
                mk.waitt("tensor", mean_tok)
            for j, t_ in enumerate(hts):
                mk.op("tensor", "matmul", ps_sum[0:1, j * TS:(j + 1) * TS],
                      c["ones_col"][:], slot[:, j * TS:(j + 1) * TS],
                      start=(i == 0), stop=(i == NE - 1),
                      inc=(st if j == 3 else None))
            sq_done[gi] = st.n
        mk.wait("vector", st, st.n)
        varh_tok = mk.op("vector", "tensor_scalar", varh[0:1, :],
                         ps_sum[0:1, :], 1.0 / E, None, op0=ALU.mult)
        if half == 0:
            bufs_early = varh_tok
        else:
            psum_free = varh_tok

        v_, s_, r_ = (scr[0:1, 0:TS], scr[0:1, TS:2 * TS],
                      scr[0:1, 2 * TS:3 * TS])
        for j, t_ in enumerate(hts):
            pb = ps_bc[t_ % 2]
            cm = bcs[:, 0:TS]
            cr = bcs[:, TS:2 * TS]
            tsl = slice(t_ * TS, (t_ + 1) * TS)
            x_ = tmp[0:1, 0:TS]
            mk.op("vector", "tensor_mul", x_, meanh[0:1, j * TS:(j + 1) * TS],
                  meanh[0:1, j * TS:(j + 1) * TS])
            vtok = mk.op("vector", "tensor_sub", v_,
                         varh[0:1, j * TS:(j + 1) * TS], x_)
            if var_row is not None:
                mk.waitt("sync", vtok)
                mk.dma("sync", svr, var_row[0:1, tsl], v_)
                mk.wait("vector", svr, svr.n)
            ve_tok = mk.op("vector", "tensor_scalar", v_, v_, EPS, None,
                           op0=ALU.add)
            mk.waitt("scalar", ve_tok)
            sq2 = mk.op("scalar", "activation", s_, v_, AF.Sqrt)
            mk.waitt("vector", sq2)
            mk.op("vector", "reciprocal", r_, s_)
            for it in range(2):
                mk.op("vector", "tensor_mul", x_, r_, r_)
                mk.op("vector", "tensor_mul", x_, x_, v_)
                mk.op("vector", "tensor_scalar", x_, x_, -0.5, 1.5,
                      op0=ALU.mult, op1=ALU.add)
                rows_tok = mk.op("vector", "tensor_mul", r_, r_, x_)
            mk.waitt("tensor", rows_tok)
            if t_ >= 2:
                mk.waitt("tensor", xn_ts[t_ - 2])
            mk.op("tensor", "matmul", pb[:, 0:TS], c["ones_row_f32"][:],
                  meanh[0:1, j * TS:(j + 1) * TS], start=True, stop=True,
                  skip_group_check=True)
            mk.op("tensor", "matmul", pb[:, TS:2 * TS], c["ones_row_f32"][:],
                  r_, start=True, stop=True, inc=st, skip_group_check=True)
            mk.wait("vector", st, st.n)
            mk.op("vector", "tensor_copy", bcs[:, :], pb[:, 0:2 * TS])
            for i in range(NE):
                last = (i == NE - 1)
                ach = A[:, i * S + t_ * TS: i * S + (t_ + 1) * TS]
                f32t = tmp[:, 0:TS]
                mk.op("vector", "tensor_sub", f32t, ach, cm)
                if mbar is None:
                    tok = mk.op("vector", "tensor_mul", ach, f32t, cr)
                else:
                    x32 = tmp[:, TS:2 * TS]
                    mk.op("vector", "tensor_mul", x32, f32t, cr)
                    mk.op("vector", "tensor_copy", ach, x32)
                    mk.op("vector", "reduce_sum", tmp[:, 0:1], x32, axis=AX.X)
                    if t_ == 0:
                        tok = mk.op("vector", "tensor_scalar", mbar[:, i:i + 1],
                                    tmp[:, 0:1], 1.0 / S, None, op0=ALU.mult)
                    else:
                        mk.op("vector", "tensor_scalar", tmp[:, 0:1],
                              tmp[:, 0:1], 1.0 / S, None, op0=ALU.mult)
                        tok = mk.op("vector", "tensor_add", mbar[:, i:i + 1],
                                    mbar[:, i:i + 1], tmp[:, 0:1])
            xn_ts[t_] = tok

    mk.end_phase()
    return {"xn_ts": xn_ts, "xn_done": xn_ts[-1], "psum_free": psum_free,
            "bufs_early": bufs_early, "mrow_tok": None}



def mm_prealloc(mk, tag, n_ei, ns, scope):
    return [mk.sbuf(f"{tag}_wb{s}", [128, 2 * n_ei * 128], BF16, scope=scope)
            for s in range(ns)]


def mm_pass(mk, c, tag, movs, wstreams, n_eo, evac, mov_tok=None, deps=(),
            mov_ts=None, psum_dep=None, wdep=(), prealloc=None):
    n_ei = len(movs)
    ns = len(wstreams)
    st = mk.sem("sm_t")
    wld = [[mk.sem(f"sm_w{s}_{j}") for j in range(2)] for s in range(ns)]
    wbufs = prealloc if prealloc is not None else [
        mk.sbuf(f"{tag}_wb{s}", [128, 2 * n_ei * 128], BF16, phase=True)
        for s in range(ns)]
    pss = [mk.psum(f"{tag}_ps{s}", [128, 2 * 1024], F32) for s in range(ns)]

    if mov_tok is not None:
        mk.waitt("tensor", mov_tok)
        if prealloc is None:
            mk.waitt("gpsimd", mov_tok)
    for d in wdep:
        if d is not None:
            mk.waitt("gpsimd", d)
    if psum_dep is not None:
        mk.waitt("tensor", psum_dep)
    for d in deps:
        if d is None:
            continue
        mk.waitt("tensor", d)
        if prealloc is None:
            mk.waitt("gpsimd", d)
    pe_groups_done = [0] * (n_eo + 1)
    evac_toks = {}
    wtoks = [None] * ns
    for eo in range(n_eo):
        for s in range(ns):
            if eo >= 2:
                mk.wait("gpsimd", st, pe_groups_done[eo - 2])
            src = wstreams[s]["ap"][:, eo * 128:(eo + 1) * 128].rearrange(
                "(a p) c -> p a c", p=128)
            dst = wbufs[s][:, (eo % 2) * n_ei * 128:
                           (eo % 2 + 1) * n_ei * 128].rearrange(
                "p (a c) -> p a c", c=128)
            wtoks[s] = mk.dma("gpsimd", wld[s][eo % 2], dst, src)
        for s in range(ns):
            mk.waitt("tensor", wtoks[s])
        for q in range(4):
            sl = q % 2
            if eo == 0 and mov_ts is not None:
                mk.waitt("tensor", mov_ts[2 * q + 1])
            for s in range(ns):
                prev = (eo, q - 2, s) if q >= 2 else (eo - 1, q + 2, s)
                if prev in evac_toks:
                    mk.waitt("tensor", evac_toks[prev])
                wb = wbufs[s][:, (eo % 2) * n_ei * 128:
                              (eo % 2 + 1) * n_ei * 128]
                ps = pss[s][:, sl * 1024:(sl + 1) * 1024]
                for ei in range(n_ei):
                    wc = wb[:, ei * 128:(ei + 1) * 128]
                    for u in range(2):
                        t_ = 2 * q + u
                        last = (ei == n_ei - 1 and u == 1 and s == ns - 1)
                        mk.op("tensor", "matmul",
                              ps[:, u * TS:(u + 1) * TS], wc,
                              movs[ei][:, t_ * TS:(t_ + 1) * TS],
                              start=(ei == 0), stop=(ei == n_ei - 1),
                              inc=(st if last else None))
            pe_groups_done[eo] = st.n
            toks = evac(eo, q, [pss[s][:, sl * 1024:(sl + 1) * 1024]
                                for s in range(ns)], (st, st.n))
            if not isinstance(toks, list):
                toks = [toks] * ns
            for s in range(ns):
                evac_toks[(eo, q, s)] = toks[s]
                last_tok = toks[s]
    mk.end_phase()
    return last_tok


def qkv_evac_factory(mk, c, tag, out_hbm, bias_col=None, vtrans=False,
                     n_eo=16, scope=None):
    stg = mk.sbuf(f"{tag}_stg", [128, S], BF16, phase=(scope is None),
                  scope=scope)
    sd = [mk.sem(f"sm_sd{j}") for j in range(2)]
    state = {"st": None, "tr": None}
    if vtrans:
        pst = [mk.psum(f"{tag}_pt{j}", [128, 128], BF16) for j in range(2)]
        vstg = mk.sbuf(f"{tag}_vstg", [128, S], BF16, phase=True)
        stv = mk.sem("sm_tv")

    def evac(eo, q, ps_list, pe_tok):
        ps = ps_list[0]
        out = stg[:, q * 1024:(q + 1) * 1024]
        mk.waitt("vector", pe_tok)
        if eo >= 1 and q == 0:
            if vtrans:
                mk.waitt("vector", state["tr"])
            else:
                mk.waitt("vector", state["st"])
        if bias_col is None:
            tok = mk.op("vector", "tensor_copy", out, ps)
        else:
            tok = mk.op("vector", "tensor_scalar", out, ps,
                        bias_col[:, eo:eo + 1], None, op0=ALU.add)
        if q < 3:
            return tok
        if not vtrans:
            mk.waitt("sync", tok)
            state["st"] = mk.dma("sync", sd[0],
                                 out_hbm[eo * 128:(eo + 1) * 128, :],
                                 stg[:, :])
            return tok
        mk.waitt("tensor", tok)
        if eo >= 1:
            mk.waitt("vector", state["st"])
        ctoks = [None, None]
        for blk in range(32):
            pj = blk % 2
            if blk >= 2:
                mk.waitt("tensor", ctoks[pj])
            mk.op("tensor", "transpose", pst[pj][:],
                  stg[:, blk * 128:(blk + 1) * 128],
                  c["ident"][:], inc=stv)
            mk.wait("vector", stv, stv.n)
            ctoks[pj] = mk.op("vector", "tensor_copy",
                              vstg[:, blk * 128:(blk + 1) * 128], pst[pj][:])
        mk.waitt("sync", ctoks[1])
        dst = out_hbm[:, eo * 128:(eo + 1) * 128].rearrange(
            "(b p) c -> p b c", p=128)
        state["st"] = mk.dma("sync", sd[1], dst, vstg[:, :].rearrange(
            "p (b c) -> p b c", c=128))
        state["tr"] = ctoks[1]
        return ctoks[1]

    def finish():
        mk.wait("sync", sd[0], sd[0].n)
        mk.wait("sync", sd[1], sd[1].n)
        return [(sd[0], sd[0].n), (sd[1], sd[1].n)]
    return evac, finish


def load_vec_cols(mk, c, tag, vec_ap, out_cols, n, ps_share=None):
    tmp = mk.sbuf(f"{tag}_lvtmp", [n, 128], F32)
    ps = ps_share if ps_share is not None else mk.psum(f"{tag}_lvp",
                                                       [128, n], F32)
    sdl = mk.sem(f"lv_{tag}_d")
    stl = mk.sem("sm_lvt")
    mk.dma("sync", sdl, tmp[:, :], vec_ap.rearrange("(a p) -> a p", p=128))
    mk.wait("tensor", sdl, sdl.n)
    mk.op("tensor", "matmul", ps[:, 0:n], tmp[:, :], c["ident_f32"][0:n, 0:n],
          start=True, stop=True, inc=stl)
    mk.wait("vector", stl, stl.n)
    return mk.op("vector", "tensor_copy", out_cols[:, 0:n], ps[:, 0:n])


def attn_core(mk, c, tag, QT_s, KT_s, V_s, A, nheads=16, kv_group=1,
              lm01_cols=None, lm_bias_cols=None, deps=()):
    st = mk.sem("sm_t")
    qld = [mk.sem(f"sm_ql{j}") for j in range(2)]
    kld = [mk.sem(f"sm_kl{j}") for j in range(2)]
    vld = [mk.sem(f"sm_vl{j}") for j in range(2)]
    NSL = 3

    qt = mk.sbuf(f"{tag}_qt", [128, 2 * S], BF16, phase=True)
    kt = mk.sbuf(f"{tag}_kt", [128, 2 * S], BF16, phase=True)
    vh = mk.sbuf(f"{tag}_vh", [128, 2 * S], BF16, phase=True)
    pt = mk.sbuf(f"{tag}_pt", [128, NSL * 1024], BF16, phase=True)
    sq = mk.sbuf(f"{tag}_sq", [128, S], BF16, phase=True)
    sm = mk.sbuf(f"{tag}_sm", [1, 16], F32, phase=True)
    nbc = mk.sbuf(f"{tag}_nbc", [128, 2], F32, phase=True)
    bh = mk.sbuf(f"{tag}_bh", [128, 32], F32, phase=True)
    tk = mk.sbuf(f"{tag}_tk", [128, 128], BF16, phase=True)
    rr = mk.sbuf(f"{tag}_rr", [1, TS], F32, phase=True)
    rdb = mk.sbuf(f"{tag}_rdb", [128, TS], F32, phase=True)

    ps_st = mk.psum(f"{tag}_ps_st", [128, NSL * 1024], F32)
    ps_ot = mk.psum(f"{tag}_ps_ot", [128, TS], F32)
    ps_den = mk.psum(f"{tag}_ps_den", [2, TS], F32)
    ps_bc = ps_st[:, 2 * 1024: 2 * 1024 + TS]
    ps_n4 = ps_st[0:1, 0:4 * TS]
    ps_n42 = ps_st[0:2, 0:4 * TS]

    for d in deps:
        if d is None:
            continue
        mk.waitt("tensor", d)
        mk.waitt("vector", d)
        mk.waitt("scalar", d)
        mk.waitt("sync", d)

    q2, k2, mm_, tsc = sm[0:1, 0:1], sm[0:1, 1:2], sm[0:1, 2:3], sm[0:1, 3:4]
    pe_head_done = [0] * (nheads + 2)
    dve_head_done = [None] * (nheads + 2)
    qtoks = [None, None]
    ktoks = [None, None]
    vtoks = [None, None]
    last_ot_tok = None
    prev_act_tok = None

    def maxnorm2(dst, src_sb, sq_tok):
        mk.op("vector", "memset", dst[:, :], 0.0)
        tok = mk.op("vector", "tensor_mul", sq[:], src_sb, src_sb)
        for r_ in range(2):
            mk.waitt("tensor", tok)
            for t4 in range(4):
                t_ = r_ * 4 + t4
                mk.op("tensor", "matmul", ps_n42[0:2, t4 * TS:(t4 + 1) * TS],
                      c["ones_col2"][:], sq[:, t_ * TS:(t_ + 1) * TS],
                      start=True, stop=True, skip_group_check=True,
                      inc=(st if t4 == 3 else None))
            mk.wait("vector", st, st.n)
            mk.op("vector", "reduce_max", tsc, ps_n4[0:1, :], axis=AX.X)
            tok = mk.op("vector", "tensor_max", dst, dst, tsc)
        return tok

    for h in range(nheads):
        kv = h // kv_group
        hs = h % 2
        ks = kv % 2
        if h >= 2:
            mk.wait("sync", st, pe_head_done[h - 2])
            mk.waitt("sync", dve_head_done[h - 2])
        qtoks[hs] = mk.dma("sync", qld[hs], qt[:, hs * S:(hs + 1) * S],
                           QT_s[h * 128:(h + 1) * 128, :])
        if kv_group == 1 or h % 2 == 0:
            ktoks[ks] = mk.dma("sync", kld[ks], kt[:, ks * S:(ks + 1) * S],
                               KT_s[kv * 128:(kv + 1) * 128, :])
            vtoks[ks] = mk.dma("sync", vld[ks], vh[:, ks * S:(ks + 1) * S],
                               V_s[:, kv * 128:(kv + 1) * 128].rearrange(
                                   "(b p) c -> p b c", p=128))
        qth = qt[:, hs * S:(hs + 1) * S]
        kth = kt[:, ks * S:(ks + 1) * S]
        vhh = vh[:, ks * S:(ks + 1) * S]
        if prev_act_tok is not None:
            mk.waitt("vector", prev_act_tok)
        mk.waitt("vector", qtoks[hs])
        sq_tok = maxnorm2(q2, qth, None)
        if kv_group == 1 or h % 2 == 0:
            mk.waitt("vector", ktoks[ks])
            sq_tok = maxnorm2(k2, kth, None)
        t_mm = mk.op("vector", "tensor_mul", mm_, q2, k2)
        mk.waitt("scalar", t_mm)
        mk.op("scalar", "activation", tsc, mm_, AF.Ln)
        sq2 = mk.op("scalar", "activation", mm_, tsc, AF.Exp, scale=0.5)
        mk.waitt("vector", sq2)
        neg_tok = mk.op("vector", "tensor_scalar", mm_, mm_,
                        -SCALE_D * 1.001, None, op0=ALU.mult)
        mk.waitt("tensor", neg_tok)
        mk.op("tensor", "matmul", ps_bc[:, 0:1], c["ones_row_f32"][:],
              mm_, start=True, stop=True, inc=st, skip_group_check=True)
        mk.wait("vector", st, st.n)
        nb_tok = mk.op("vector", "tensor_copy", nbc[:, 0:1], ps_bc[:, 0:1])
        if lm_bias_cols is not None:
            nb_tok = mk.op("vector", "tensor_scalar", bh[:], lm_bias_cols[:],
                           nbc[:, 0:1], None, op0=ALU.add)
        mk.waitt("scalar", nb_tok)

        mk.waitt("tensor", vtoks[ks])
        st_war = [nb_tok, None, None]
        pt_war = [0] * NSL
        ot_war = last_ot_tok

        def make_tail(h_, qs_, pv_done_val):
            st_ = {"rc": None}

            def stage_rc():
                mk.wait("vector", st, pv_done_val)
                st_["rc"] = mk.op("vector", "tensor_copy", rr[0:1, :],
                                  ps_den[0:1, :])

            def stage_rest():
                nonlocal last_ot_tok, ot_war
                mk.waitt("tensor", st_["rc"])
                mk.op("tensor", "matmul", ps_bc[:, :], c["ones_row_f32"][:],
                      rr[0:1, :], start=True, stop=True, inc=st,
                      skip_group_check=True)
                mk.wait("vector", st, st.n)
                rdb_tok = mk.op("vector", "reciprocal", rdb[:], ps_bc[:, :])
                last_ot_tok = mk.op(
                    "vector", "tensor_mul",
                    A[:, h_ * S + qs_ * TS: h_ * S + (qs_ + 1) * TS],
                    ps_ot[:, :], rdb[:])
                ot_war = last_ot_tok
                st_war[2] = rdb_tok
            return stage_rc, stage_rest

        pend_tail = None
        for qs in range(NTS):
            nkb = 4 * qs + 4
            nslab = (nkb + 1) // 2
            exp_toks = {}
            fix_toks = {}
            if pend_tail is not None:
                pend_tail[0]()
            for kb2 in range(nslab + 1):
                if kb2 < nslab:
                    sl = kb2 % NSL
                    kbs = [2 * kb2, 2 * kb2 + 1]
                    if st_war[sl] is not None:
                        mk.waitt("tensor", st_war[sl])
                    for j, kb in enumerate(kbs):
                        mk.op("tensor", "matmul",
                              ps_st[:, (sl * 2 + j) * TS:(sl * 2 + j + 1) * TS],
                              kth[:, kb * 128:(kb + 1) * 128],
                              qth[:, qs * TS:(qs + 1) * TS],
                              start=True, stop=True, skip_group_check=True,
                              inc=(st if j == len(kbs) - 1 else None))
                    st_done = st.n
                    mk.wait("scalar", st, st_done)
                    mk.wait("scalar", st, pt_war[sl])
                    win_lo = 4 * qs - 1
                    if lm_bias_cols is None:
                        e_tok = mk.op(
                            "scalar", "activation",
                            pt[:, sl * 1024:(sl + 1) * 1024],
                            ps_st[:, sl * 2 * TS:(sl * 2 + 2) * TS],
                            AF.Exp, bias=nbc[:, 0:1], scale=SCALE_D)
                        for kb in kbs:
                            exp_toks[kb] = e_tok
                    else:
                        for j, kb in enumerate(kbs):
                            bias = (nbc[:, 0:1] if kb >= win_lo
                                    else bh[:, kb:kb + 1])
                            exp_toks[kb] = mk.op(
                                "scalar", "activation",
                                pt[:, (sl * 2 + j) * TS:(sl * 2 + j + 1) * TS],
                                ps_st[:, (sl * 2 + j) * TS:(sl * 2 + j + 1) * TS],
                                AF.Exp, bias=bias, scale=SCALE_D)
                    st_war[sl] = exp_toks[kbs[-1]]
                    prev_act_tok = st_war[sl]
                    for j, kb in enumerate(kbs):
                        ptk = pt[:, (sl * 2 + j) * TS:(sl * 2 + j + 1) * TS]
                        jd = kb - 4 * qs
                        tok = None
                        if jd >= 0:
                            mk.waitt("vector", exp_toks[kb])
                            if jd > 0:
                                tok = mk.op("vector", "memset",
                                            ptk[:, 0:jd * 128], 0.0)
                            tok = mk.op("vector", "tensor_mul",
                                        ptk[:, jd * 128:(jd + 1) * 128],
                                        ptk[:, jd * 128:(jd + 1) * 128],
                                        c["tri_keep"][:])
                            if lm_bias_cols is not None:
                                if jd < 3:
                                    mk.op("vector", "tensor_scalar", tk[:],
                                          c["low_strict"][:],
                                          lm01_cols[:, kb:kb + 1], None,
                                          op0=ALU.max)
                                    tok = mk.op(
                                        "vector", "tensor_mul",
                                        ptk[:, (jd + 1) * 128:(jd + 2) * 128],
                                        ptk[:, (jd + 1) * 128:(jd + 2) * 128],
                                        tk[:])
                                if jd < 2:
                                    tok = mk.op(
                                        "vector", "tensor_scalar",
                                        ptk[:, (jd + 2) * 128:],
                                        ptk[:, (jd + 2) * 128:],
                                        lm01_cols[:, kb:kb + 1], None,
                                        op0=ALU.mult)
                        elif lm_bias_cols is not None and jd == -1:
                            mk.waitt("vector", exp_toks[kb])
                            mk.op("vector", "tensor_scalar", tk[:],
                                  c["low_strict"][:],
                                  lm01_cols[:, kb:kb + 1], None, op0=ALU.max)
                            mk.op("vector", "tensor_mul", ptk[:, 0:128],
                                  ptk[:, 0:128], tk[:])
                            tok = mk.op("vector", "tensor_scalar",
                                        ptk[:, 128:], ptk[:, 128:],
                                        lm01_cols[:, kb:kb + 1], None,
                                        op0=ALU.mult)
                        fix_toks[kb] = tok
                if kb2 >= 1:
                    pl = (kb2 - 1) % NSL
                    pkbs = [2 * (kb2 - 1), 2 * kb2 - 1]
                    if kb2 == 1 and pend_tail is not None:
                        pend_tail[1]()
                        pend_tail = None
                    if kb2 == 1 and ot_war is not None:
                        mk.waitt("tensor", ot_war)
                    for j, kb in enumerate(pkbs):
                        if fix_toks[kb] is not None:
                            mk.waitt("tensor", fix_toks[kb])
                        else:
                            mk.waitt("tensor", exp_toks[kb])
                        ptk = pt[:, (pl * 2 + j) * TS:(pl * 2 + j + 1) * TS]
                        mk.op("tensor", "matmul", ps_den[0:2, :],
                              c["ones_col2"][:], ptk,
                              start=(kb == 0), stop=(kb == nkb - 1),
                              skip_group_check=True)
                        mk.op("tensor", "matmul", ps_ot[:, :],
                              vhh[:, kb * 128:(kb + 1) * 128], ptk,
                              start=(kb == 0), stop=(kb == nkb - 1),
                              skip_group_check=True,
                              inc=(st if j == len(pkbs) - 1 else None))
                    pt_war[pl] = st.n
            pend_tail = make_tail(h, qs, st.n)
        pend_tail[0]()
        pend_tail[1]()
        pend_tail = None
        pe_head_done[h] = st.n
        dve_head_done[h] = last_ot_tok
    mk.end_phase()
    return last_ot_tok


def resid_evac_factory(mk, c, tag, R_in, R_out, bias_col=None, n_eo=16,
                       deps=(), rb_out=None):
    rt = mk.sbuf(f"{tag}_rt", [128, 2 * S], F32, phase=True)
    rn = mk.sbuf(f"{tag}_rn", [128, 2 * S], F32, phase=True)
    rld = [mk.sem(f"sm_rl{j}") for j in range(2)]
    sd = [mk.sem(f"sm_sd{j}") for j in range(2)]
    state = {"ld": [None, None], "st": [None, None], "rb": [None, None],
             "rb_rows": [None] * n_eo}
    srb = [mk.sem("sm_srb0"), mk.sem("sm_srb1")]
    for d in deps:
        if d is not None:
            mk.waitt("sync", d)

    state["ld"][0] = mk.dma("sync", rld[0], rt[:, 0:S], R_in[0:128, :])
    state["ld"][1] = mk.dma("sync", rld[1], rt[:, S:2 * S], R_in[128:256, :])

    def evac(eo, q, ps_list, pe_tok):
        ps = ps_list[0]
        sl = eo % 2
        out = rn[:, sl * S + q * 1024: sl * S + (q + 1) * 1024]
        rts = rt[:, sl * S + q * 1024: sl * S + (q + 1) * 1024]
        mk.waitt("vector", pe_tok)
        mk.waitt("vector", state["ld"][sl])
        if eo >= 2 and q == 0:
            mk.waitt("vector", state["st"][sl])
            mk.waitt("vector", state["rb"][sl])
        tok = mk.op("vector", "scalar_tensor_tensor", out, ps,
                    (bias_col[:, eo:eo + 1] if bias_col is not None else 0.0),
                    rts, op0=ALU.add, op1=ALU.add)
        if q < 3:
            return tok
        mk.waitt("sync", tok)
        state["st"][sl] = mk.dma("sync", sd[sl],
                                 R_out[eo * 128:(eo + 1) * 128, :],
                                 rn[:, sl * S:(sl + 1) * S])
        if rb_out is not None:
            mk.waitt("gpsimd", tok)
            state["rb"][sl] = mk.dma("gpsimd", srb[sl],
                                     rb_out[eo * 128:(eo + 1) * 128, :],
                                     rn[:, sl * S:(sl + 1) * S])
            state["rb_rows"][eo] = state["rb"][sl]
        if eo + 2 < n_eo:
            state["ld"][sl] = mk.dma("sync", rld[sl],
                                     rt[:, sl * S:(sl + 1) * S],
                                     R_in[(eo + 2) * 128:(eo + 3) * 128, :])
        return tok

    def finish():
        mk.wait("sync", sd[0], sd[0].n)
        mk.wait("sync", sd[1], sd[1].n)
        toks = [(sd[0], sd[0].n), (sd[1], sd[1].n)]
        if rb_out is not None:
            toks += [(srb[0], srb[0].n), (srb[1], srb[1].n)]
        return toks
    return evac, finish, state



def ffn1_evac_factory(mk, c, tag, H_s, sgate_col, sup_col, scope=None):
    gs = mk.sbuf(f"{tag}_gs", [128, 2 * 1024], F32, phase=(scope is None),
                 scope=scope)
    hst = mk.sbuf(f"{tag}_hst", [128, S], BF16, phase=(scope is None),
                  scope=scope)
    sd = [mk.sem(f"sm_sd{j}") for j in range(2)]
    state = {"st": [None, None]}

    def evac(eo, q, ps_list, pe_tok):
        ps_g, ps_u = ps_list
        gsl = gs[:, (q % 2) * 1024:(q % 2 + 1) * 1024]
        out = hst[:, q * 1024:(q + 1) * 1024]
        mk.waitt("scalar", pe_tok)
        a_tok = mk.op("scalar", "activation", gsl, ps_g, AF.Silu,
                      scale=sgate_col[:, eo:eo + 1])
        mk.waitt("vector", a_tok)
        if eo >= 1 and q == 0:
            mk.waitt("vector", state["st"][0])
        u_tok = mk.op("vector", "scalar_tensor_tensor", out, ps_u,
                      sup_col[:, eo:eo + 1], gsl, op0=ALU.mult, op1=ALU.mult)
        if q == 3:
            mk.waitt("sync", u_tok)
            state["st"][0] = mk.dma("sync", sd[eo % 2],
                                    H_s[eo * 128:(eo + 1) * 128, :],
                                    hst[:, :])
        return [a_tok, u_tok]

    def finish():
        mk.wait("sync", sd[0], sd[0].n)
        mk.wait("sync", sd[1], sd[1].n)
        return [(sd[0], sd[0].n), (sd[1], sd[1].n)]
    return evac, finish



def ffn2_phase(mk, c, tag, H_s, Wd_ap, R_in, R_out, sdown_col, A, deps=(),
               rb_out=None):
    st = mk.sem("sm_t")
    hld = [mk.sem(f"sm_hl{j}") for j in range(2)]
    wld = [mk.sem(f"sm_wl{j}") for j in range(2)]
    rld = [mk.sem(f"sm_rl{j}") for j in range(2)]
    sd = [mk.sem(f"sm_sd{j}") for j in range(2)]

    wb = mk.sbuf(f"{tag}_wb", [128, 2 * NIO * 128], BF16, phase=True)
    rt = mk.sbuf(f"{tag}_rt", [128, 2 * TS], F32, phase=True)
    rn = mk.sbuf(f"{tag}_rn", [128, 2 * TS], F32, phase=True)
    ps = mk.psum(f"{tag}_ps", [128, 2 * TS], F32)

    for d in deps:
        if d is None:
            continue
        for e in ("tensor", "vector", "sync", "gpsimd"):
            mk.waitt(e, d)

    HHALF = NIO * TS
    st_base = st.n
    pe_ts_done = [0] * (NTS + 2)
    ev_toks = {}
    htoks = [None, None]
    wtoks = [None, None]
    rtoks = [None, None]
    stoks2 = [None, None]
    rbtoks = [None, None]
    pend_rb = []
    rb_row_toks = [None] * NE
    srb = [mk.sem("sm_srb0"), mk.sem("sm_srb1")]

    def h_dma(ts):
        hsl = ts % 2
        if ts >= 2:
            mk.wait("sync", st, pe_ts_done[ts - 2])
        hsrc = H_s[:, ts * TS:(ts + 1) * TS].rearrange("(a p) c -> p a c",
                                                       p=128)
        hdst = A[:, hsl * HHALF:(hsl + 1) * HHALF].rearrange(
            "p (a c) -> p a c", c=TS)
        htoks[hsl] = mk.dma("sync", hld[hsl], hdst, hsrc)

    h_dma(0)
    for ts in range(NTS):
        hsl = ts % 2
        for eo in range(NE):
            if eo == 4 and ts + 1 < NTS:
                h_dma(ts + 1)
            wsl = eo % 2
            gi = ts * NE + eo
            if gi >= 2:
                mk.wait("gpsimd", st, st_base + gi - 1)
            wsrc = Wd_ap[:, eo * 128:(eo + 1) * 128].rearrange(
                "(a p) c -> p a c", p=128)
            wdst = wb[:, wsl * NIO * 128:(wsl + 1) * NIO * 128].rearrange(
                "p (a c) -> p a c", c=128)
            wtoks[wsl] = mk.dma("gpsimd", wld[wsl], wdst, wsrc)
            while pend_rb:
                ptok, psl_, peo, pts = pend_rb.pop(0)
                mk.waitt("gpsimd", ptok)
                rbtoks[psl_] = mk.dma("gpsimd", srb[psl_],
                                      rb_out[peo * 128:(peo + 1) * 128,
                                             pts * TS:(pts + 1) * TS],
                                      rn[:, psl_ * TS:(psl_ + 1) * TS])
                if pts == NTS - 1:
                    rb_row_toks[peo] = rbtoks[psl_]
            if gi >= 2:
                mk.waitt("sync", ev_toks[gi - 2])
            rtoks[wsl] = mk.dma("sync", rld[wsl],
                                rt[:, wsl * TS:(wsl + 1) * TS],
                                R_in[eo * 128:(eo + 1) * 128,
                                     ts * TS:(ts + 1) * TS])
            pass
            mk.waitt("tensor", wtoks[wsl])
            mk.waitt("tensor", htoks[hsl])
            prev = ev_toks.get(gi - 2)
            if prev is not None:
                mk.waitt("tensor", prev)
            psl = ps[:, wsl * TS:(wsl + 1) * TS]
            wbs = wb[:, wsl * NIO * 128:(wsl + 1) * NIO * 128]
            for i in range(NIO):
                mk.op("tensor", "matmul", psl,
                      wbs[:, i * 128:(i + 1) * 128],
                      A[:, hsl * HHALF + i * TS: hsl * HHALF + (i + 1) * TS],
                      start=(i == 0), stop=(i == NIO - 1),
                      inc=(st if i == NIO - 1 else None))
            mk.wait("vector", st, st.n)
            mk.waitt("vector", rtoks[wsl])
            if gi >= 2:
                mk.waitt("vector", stoks2[wsl])
                if rb_out is not None and rbtoks[wsl] is not None:
                    mk.waitt("vector", rbtoks[wsl])
            tok = mk.op("vector", "scalar_tensor_tensor",
                        rn[:, wsl * TS:(wsl + 1) * TS], psl,
                        sdown_col[:, eo:eo + 1], rt[:, wsl * TS:(wsl + 1) * TS],
                        op0=ALU.mult, op1=ALU.add)
            ev_toks[gi] = tok
            mk.waitt("sync", tok)
            stoks2[wsl] = mk.dma("sync", sd[wsl],
                                 R_out[eo * 128:(eo + 1) * 128,
                                       ts * TS:(ts + 1) * TS],
                                 rn[:, wsl * TS:(wsl + 1) * TS])
            if rb_out is not None:
                pend_rb.append((tok, wsl, eo, ts))
        pe_ts_done[ts] = st.n
    for ptok, psl_, peo, pts in pend_rb:
        mk.waitt("gpsimd", ptok)
        rbtoks[psl_] = mk.dma("gpsimd", srb[psl_],
                              rb_out[peo * 128:(peo + 1) * 128,
                                     pts * TS:(pts + 1) * TS],
                              rn[:, psl_ * TS:(psl_ + 1) * TS])
        if pts == NTS - 1:
            rb_row_toks[peo] = rbtoks[psl_]
    fin = [(sd[0], sd[0].n), (sd[1], sd[1].n)]
    if rb_out is not None:
        fin += [(srb[0], srb[0].n), (srb[1], srb[1].n)]
    for s_, v_ in fin:
        mk.wait("sync", s_, v_)
    mk.end_phase()
    half_free = [(st, pe_ts_done[NTS - 2]), (st, pe_ts_done[NTS - 1])]
    return ev_toks[NTS * NE - 1], fin, rb_row_toks, half_free


class MaskPipeline:

    def __init__(self, mk, c, tag, A, var_row, mbar, score_row, scope,
                 xn_tok):
        self.mk, self.c, self.tag = mk, c, tag
        self.A, self.var_row, self.mbar = A, var_row, mbar
        self.score_row = score_row
        self.xn_tok = xn_tok
        self.prodb = mk.sbuf(f"{tag}_prod", [128, S // 2], BF16, scope=scope)
        self.d32 = mk.sbuf(f"{tag}_d32", [128, 32], F32, scope=scope)
        self.dT = mk.sbuf(f"{tag}_dT", [32, 128], F32, scope=scope)
        self.rowin = mk.sbuf(f"{tag}_rin", [1, 2 * TS], F32, scope=scope)
        self.sc = mk.sbuf(f"{tag}_sc", [1, 64], F32, scope=scope)
        self.rowt = mk.sbuf(f"{tag}_rowt", [1, 4 * TS], F32, scope=scope)
        self.msq = mk.sbuf(f"{tag}_msq", [128, NE], F32, scope=scope)
        self.ps_dot = mk.psum(f"{tag}_psd", [128, 32], F32, scope=scope)
        self.ps_t = mk.psum(f"{tag}_pst", [32, 128], F32, scope=scope)
        self.dot_dram = mk.nc.dram_tensor(f"{tag}_dram", [S], F32,
                                          kind="Internal").ap()
        self.st = mk.sem(f"{tag}_t")
        self.sdd = mk.sem(f"{tag}_sdd")
        self.rls = mk.sem(f"{tag}_rls")
        self.i = 0
        self.st0 = None
        self.sc_tok = None
        self.bis = Bisect(mk, c, tag + "b", score_row, scope)

    def _dot(self, i):
        mk, c = self.mk, self.c
        if i == 0:
            self.st0 = self.st.n
            mk.waitt("vector", self.xn_tok)
            mk.waitt("tensor", self.xn_tok)
        for h in range(2):
            if i > 0 or h > 0:
                mk.wait("vector", self.st, self.st0 + 2 * i + h)
            ptok = mk.op("vector", "tensor_scalar", self.prodb[:],
                         self.A[:, i * S + h * (S // 2):
                                i * S + (h + 1) * (S // 2)],
                         self.mbar[:, i:i + 1], None, op0=ALU.mult)
            mk.waitt("tensor", ptok)
            for b in range(16):
                bb = h * 16 + b
                mk.op("tensor", "matmul", self.ps_dot[:, bb:bb + 1],
                      self.prodb[:, b * 128:(b + 1) * 128], c["ones_col"][:],
                      start=(i == 0), stop=(i == NE - 1),
                      inc=(self.st if b == 15 else None),
                      skip_group_check=True)

    def _trans(self):
        mk, c = self.mk, self.c
        mk.wait("vector", self.st, self.st.n)
        t1 = mk.op("vector", "tensor_copy", self.d32[:], self.ps_dot[:, :])
        mk.waitt("tensor", t1)
        mk.op("tensor", "transpose", self.ps_t[:, :], self.d32[:],
              c["ident_f32"][:], inc=self.st)
        mk.wait("vector", self.st, self.st.n)
        t2 = mk.op("vector", "tensor_copy", self.dT[:, :], self.ps_t[:, :])
        mk.waitt("sync", t2)
        mk.dma("sync", self.sdd,
               self.dot_dram.rearrange("(b p) -> b p", p=128), self.dT[:, :])

    def _mnorm(self):
        mk, c, sc = self.mk, self.c, self.sc
        mn, s0, s1 = sc[0:1, 1:2], sc[0:1, 2:3], sc[0:1, 3:4]
        msq_tok = mk.op("vector", "tensor_mul", self.msq[:], self.mbar[:],
                        self.mbar[:])
        mk.waitt("tensor", msq_tok)
        mk.op("tensor", "matmul", self.ps_t[0:1, 0:16], c["ones_col_f32"][:],
              self.msq[:], start=True, stop=True, inc=self.st,
              skip_group_check=True)
        mk.wait("vector", self.st, self.st.n)
        mn_tok = mk.op("vector", "reduce_sum", mn, self.ps_t[0:1, 0:16],
                       axis=AX.X)
        mk.waitt("scalar", mn_tok)
        sq_tok = mk.op("scalar", "activation", s0, mn, AF.Sqrt)
        mk.waitt("vector", sq_tok)
        for _ in range(2):
            mk.op("vector", "reciprocal", s1, s0)
            mk.op("vector", "tensor_mul", s1, s1, mn)
            mk.op("vector", "tensor_add", s0, s0, s1)
            mk.op("vector", "tensor_scalar", s0, s0, 0.5, None, op0=ALU.mult)

    def _score(self, ts):
        mk, c = self.mk, self.c
        rowt, rowin = self.rowt, self.rowin
        s0 = self.sc[0:1, 2:3]
        v_, r_, n_, x_ = (rowt[0:1, 0:TS], rowt[0:1, TS:2 * TS],
                          rowt[0:1, 2 * TS:3 * TS], rowt[0:1, 3 * TS:4 * TS])
        vin, din = rowin[0:1, 0:TS], rowin[0:1, TS:2 * TS]
        tsl = slice(ts * TS, (ts + 1) * TS)
        if ts == 0:
            mk.wait("sync", self.sdd, self.sdd.n)
        if self.sc_tok is not None:
            mk.waitt("sync", self.sc_tok)
        mk.dma("sync", self.rls, vin, self.var_row[0:1, tsl])
        rltok = mk.dma("sync", self.rls, din, self.dot_dram[None, tsl])
        mk.waitt("vector", rltok)
        mk.op("vector", "tensor_scalar", v_, vin, EPS, None, op0=ALU.add)
        mk.op("vector", "reciprocal", x_, v_)
        mk.op("vector", "tensor_mul", x_, x_, vin)
        wtok = mk.op("vector", "tensor_scalar", n_, x_, float(E), None,
                     op0=ALU.mult)
        mk.waitt("scalar", wtok)
        stok = mk.op("scalar", "activation", x_, n_, AF.Sqrt)
        mk.waitt("vector", stok)
        mk.op("vector", "reciprocal", r_, x_)
        for _ in range(2):
            mk.op("vector", "tensor_mul", x_, r_, r_)
            mk.op("vector", "tensor_mul", x_, x_, n_)
            mk.op("vector", "tensor_scalar", x_, x_, -0.5, 1.5,
                  op0=ALU.mult, op1=ALU.add)
            mk.op("vector", "tensor_mul", r_, r_, x_)
        mk.op("vector", "tensor_mul", n_, n_, r_)
        mk.op("vector", "tensor_scalar", v_, n_, s0, 1e-6,
              op0=ALU.mult, op1=ALU.add)
        mk.op("vector", "reciprocal", v_, v_)
        mk.op("vector", "tensor_mul", x_, din, v_)
        mk.op("vector", "tensor_scalar", n_, n_, ALPHA, None, op0=ALU.mult)
        self.sc_tok = mk.op("vector", "scalar_tensor_tensor",
                            self.score_row[0:1, tsl], x_, -(1.0 - ALPHA),
                            n_, op0=ALU.mult, op1=ALU.add)

    def step(self):
        i = self.i
        if i < NE:
            self._dot(i)
        elif i == NE:
            self._trans()
        elif i == NE + 1:
            self._mnorm()
        elif i < NE + 2 + NTS:
            self._score(i - NE - 2)
        elif i == NE + 2 + NTS:
            self.bis.start()
        else:
            self.bis.step()
        self.i += 1

    def finish(self):
        while not (self.i > NE + 2 + NTS and self.bis.done_tok is not None):
            self.step()


class Bisect:

    NITER = 36

    def __init__(self, mk, c, tag, score_row, scope):
        self.mk = mk
        self.c = c
        self.tag = tag
        self.score = score_row
        self.row = mk.sbuf(f"{tag}_brow", [1, S], BF16, scope=scope)
        self.lm = mk.sbuf(f"{tag}_lm", [1, S], BF16, scope=scope)
        self.sc = mk.sbuf(f"{tag}_bsc", [1, 64], F32, scope=scope)
        self.ci = mk.sbuf(f"{tag}_bci", [1, 8], mybir.dt.uint8, scope=scope)
        self.i = 0
        self.done_tok = None

    def start(self):
        mk = self.mk
        mk.op("vector", "tensor_reduce", self.sc[0:1, 0:1],
              self.score[0:1, :], axis=AX.X, op=ALU.min)
        mk.op("vector", "reduce_max", self.sc[0:1, 1:2], self.score[0:1, :],
              axis=AX.X)

    def step(self):
        mk, sc = self.mk, self.sc
        lo, hi, tau, cnt, c01 = (sc[0:1, 0:1], sc[0:1, 1:2], sc[0:1, 2:3],
                                 sc[0:1, 3:4], sc[0:1, 4:5])
        if self.i < self.NITER:
            mk.op("vector", "tensor_add", tau, lo, hi)
            mk.op("vector", "tensor_scalar", tau, tau, 0.5, None, op0=ALU.mult)
            mk.op("vector", "tensor_single_scalar", self.row[0:1, :],
                  self.score[0:1, :], tau, op=ALU.is_ge)
            mk.op("vector", "reduce_sum", cnt, self.row[0:1, :], axis=AX.X)
            mk.op("vector", "tensor_single_scalar", self.ci[0:1, 0:1], cnt,
                  float(KEEP_K), op=ALU.is_ge)
            mk.op("vector", "tensor_single_scalar", self.ci[0:1, 1:2], cnt,
                  float(KEEP_K), op=ALU.is_lt)
            mk.op("vector", "copy_predicated", lo, self.ci[0:1, 0:1], tau)
            mk.op("vector", "copy_predicated", hi, self.ci[0:1, 1:2], tau)
        elif self.i == self.NITER:
            lm0, lmd = self.lm[0:1, 0:S], self.row[0:1, 0:S]
            mk.op("vector", "tensor_single_scalar", lm0, self.score[0:1, :],
                  lo, op=ALU.is_ge)
            for _ in range(HOPS):
                mk.op("vector", "tensor_copy", lmd, lm0)
                mk.op("vector", "tensor_max", lmd[0:1, 1:S], lmd[0:1, 1:S],
                      lm0[0:1, 0:S - 1])
                mk.op("vector", "tensor_max", lmd[0:1, 0:S - 1],
                      lmd[0:1, 0:S - 1], lm0[0:1, 1:S])
                tok = mk.op("vector", "tensor_copy", lm0, lmd)
            self.done_tok = tok
        self.i += 1

    def finish_steps(self):
        while self.done_tok is None:
            self.step()


def mask_cols(mk, c, tag, bis, lm01_cols, lm_bias_cols, deps=()):
    sdd = mk.sem(f"{tag}_sdd")
    st = mk.sem(f"{tag}_t")
    dram = mk.nc.dram_tensor(f"{tag}_dram", [S], BF16, kind="Internal").ap()
    dT = mk.sbuf(f"{tag}_dT", [32, 128], BF16, phase=True)
    ps_t = mk.psum(f"{tag}_pst", [128, 32], F32)
    for d in deps:
        if d is None:
            continue
        for e in ("tensor", "vector", "sync"):
            mk.waitt(e, d)
    bis.finish_steps()
    mk.waitt("sync", bis.done_tok)
    mk.dma("sync", sdd, dram[None, :], bis.lm[0:1, 0:S])
    mk.wait("sync", sdd, sdd.n)
    mk.dma("sync", sdd, dT[:, :], dram.rearrange("(b p) -> b p", p=128))
    mk.wait("tensor", sdd, sdd.n)
    mk.op("tensor", "matmul", ps_t[:, 0:32], dT[:, :],
          c["ident"][0:32, 0:32], start=True, stop=True, inc=st,
          skip_group_check=True)
    mk.wait("vector", st, st.n)
    mk.op("vector", "tensor_copy", lm01_cols[:], ps_t[:, 0:32])
    tok = mk.op("vector", "tensor_scalar", lm_bias_cols[:], lm01_cols[:],
                1e4, -1e4, op0=ALU.mult, op1=ALU.add)
    mk.end_phase()
    return tok



def build_core_program():
    nc = bass.Bass("TRN2", target_bir_lowering=False)
    dr = lambda n, s, d, k="ExternalInput": nc.dram_tensor(n, s, d, kind=k).ap()
    io = {
        "xT": dr("xT", [E, S], F32),
        "Wqa": dr("Wqa", [E, E], F32), "bqa": dr("bqa", [E], F32),
        "Wka": dr("Wka", [E, E], F32), "bka": dr("bka", [E], F32),
        "Wva": dr("Wva", [E, E], F32), "bva": dr("bva", [E], F32),
        "Wca": dr("Wca", [E, E], F32), "bca": dr("bca", [E], F32),
        "Wqb": dr("Wqb", [E, E], F32),
        "Wkb": dr("Wkb", [E, 1024], F32), "Wvb": dr("Wvb", [E, 1024], F32),
        "Wob": dr("Wob", [E, E], F32),
        "wg": dr("wg", [E, INTER], I8), "wu": dr("wu", [E, INTER], I8),
        "wd": dr("wd", [INTER, E], I8),
        "sg": dr("sg", [INTER], F32), "su": dr("su", [INTER], F32),
        "sdn": dr("sdn", [E], F32),
        "yT": dr("yT", [E, S], F32, "ExternalOutput"),
    }
    R = dr("R_scr", [E, S], F32, "Internal")
    Rb_s = dr("Rb_scr", [E, S], BF16, "Internal")
    QT_s = dr("QT_scr", [E, S], BF16, "Internal")
    KT_s = dr("KT_scr", [E, S], BF16, "Internal")
    V_s = dr("V_scr", [S, E], BF16, "Internal")
    H_s = dr("H_scr", [INTER, S], BF16, "Internal")

    mk = MK(nc)
    c = build_consts(mk)
    A = mk.sbuf("A", [128, NE * S], BF16)
    bias = mk.sbuf("bias", [128, 2 * NE], F32)
    scales = mk.sbuf("scales", [128, 2 * NIO + NE], F32)
    sg_col = scales[:, 0:NIO]
    su_col = scales[:, NIO:2 * NIO]
    sdn_col = scales[:, 2 * NIO:2 * NIO + NE]

    for i, w in enumerate(["bva", "bca"]):
        load_vec_cols(mk, c, f"l{w}", io[w], bias[:, i * NE:(i + 1) * NE], NE)
    load_vec_cols(mk, c, "lsg", io["sg"], sg_col, NIO)
    load_vec_cols(mk, c, "lsu", io["su"], su_col, NIO)
    ldt = load_vec_cols(mk, c, "lsdn", io["sdn"], sdn_col, NE)
    mk.end_phase()
    movs = [A[:, i * S:(i + 1) * S] for i in range(NE)]

    def hookify(evac, hook):
        if hook is None:
            return evac

        def e2(eo, q, ps_list, pe_tok):
            r = evac(eo, q, ps_list, pe_tok)
            hook()
            return r
        return e2

    def qkv_block(subtag, Wq, Wk, Wv, nk, dep=None, hook=None, mov_ts=None,
                  psum_dep=None, prealloc=None, wdep=()):
        prev = dep
        stoks = []
        for j, (nm, Wa, n_eo) in enumerate((("q", Wq, NE), ("k", Wk, nk),
                                            ("v", Wv, nk))):
            out = {"q": QT_s, "k": KT_s, "v": V_s}[nm]
            ev, fin = qkv_evac_factory(mk, c, f"{subtag}{nm}", out,
                                       vtrans=(nm == "v"))
            for s_, v_ in stoks:
                mk.wait("vector", s_, v_)
            prev = mm_pass(mk, c, f"{subtag}m{nm}", movs, [{"ap": Wa}], n_eo,
                           hookify(ev, hook), mov_tok=dep, deps=[prev],
                           mov_ts=(mov_ts if j == 0 else None),
                           psum_dep=(psum_dep if j == 0 else None),
                           prealloc=prealloc,
                           wdep=(wdep if j == 0 else (prev,)))
            stoks = fin()
        return prev, stoks

    wsc1 = []
    ev, fin = qkv_evac_factory(mk, c, "s1v", QT_s, bias_col=bias[:, 0:NE],
                               scope=wsc1)
    wb1 = mm_prealloc(mk, "s1mv", NE, 1, wsc1)
    r1 = ln_phase(mk, c, io["xT"], A, "ln1", deps=[ldt], src_bf16=True,
                  cast_load=True)
    p0 = mm_pass(mk, c, "s1mv", movs, [{"ap": io["Wva"]}], NE, ev,
                 mov_ts=r1["xn_ts"], psum_dep=r1["psum_free"], wdep=[ldt],
                 prealloc=wb1)
    stoks = fin()
    mk.release(wsc1)
    ldv = mk.sem("sm_ldv")
    for s_, v_ in stoks:
        mk.wait("sync", s_, v_)
    mk.waitt("sync", p0)
    for i in range(NE):
        at1 = mk.dma("sync", ldv, A[:, i * S:(i + 1) * S],
                     QT_s[i * 128:(i + 1) * 128, :])
    ev, fin, rpst1 = resid_evac_factory(mk, c, "rp1", io["xT"], R,
                                        bias_col=bias[:, NE:2 * NE],
                                        deps=[p0], rb_out=Rb_s)
    p1 = mm_pass(mk, c, "mmc1", movs, [{"ap": io["Wca"]}], NE, ev,
                 mov_tok=at1, deps=[p0])
    rtoks1 = fin()

    wsc2 = []
    ev, fin = ffn1_evac_factory(mk, c, "f1", H_s, sg_col, su_col, scope=wsc2)
    wb2 = mm_prealloc(mk, "mmf1", NE, 2, wsc2)
    r2 = ln_phase(mk, c, Rb_s, A, "ln2", src_bf16=True,
                  deps=[p1] + rtoks1, row_toks=rpst1["rb_rows"],
                  load_dep_half=[p1, p1])
    p2 = mm_pass(mk, c, "mmf1", movs, [{"ap": io["wg"]}, {"ap": io["wu"]}],
                 NIO, ev, mov_ts=r2["xn_ts"], psum_dep=r2["psum_free"],
                 wdep=[p1] + rtoks1, prealloc=wb2)
    htoks = fin()
    mk.release(wsc2)
    p3, rtoks2, rb2_rows, hfree2 = ffn2_phase(mk, c, "f2", H_s, io["wd"],
                                              R, R, sdn_col, A,
                                              deps=[p2] + htoks, rb_out=Rb_s)

    heldlm = []
    lm01_cols = mk.sbuf("lm01c", [128, 32], F32, scope=heldlm)
    lmb_cols = mk.sbuf("lmbc", [128, 32], F32, scope=heldlm)
    heldsc = []
    score_row = mk.sbuf("score3", [1, S], F32, scope=heldsc)
    var_row = dr("var3_scr", [1, S], F32, "Internal")
    mbar = mk.sbuf("mbar3", [128, NE], F32, scope=heldsc)
    wb3 = mm_prealloc(mk, "s3w", NE, 1, heldsc)
    r3 = ln_phase(mk, c, Rb_s, A, "ln3", src_bf16=True, var_row=var_row,
                  mbar=mbar, deps=[p3] + rtoks2, row_toks=rb2_rows,
                  load_dep_half=hfree2)
    msk = MaskPipeline(mk, c, "ms3", A, var_row, mbar, score_row, heldsc,
                       r3["xn_done"])
    prev, stoks = qkv_block("s3", io["Wqb"], io["Wkb"], io["Wvb"], 8,
                            dep=r3["xn_done"], hook=msk.step,
                            mov_ts=r3["xn_ts"], psum_dep=r3["psum_free"],
                            prealloc=wb3, wdep=[p3] + rtoks2)
    msk.finish()
    cm_tok = mask_cols(mk, c, "mc3", msk.bis, lm01_cols, lmb_cols,
                       deps=[prev])
    mk.release(heldsc)
    at3 = attn_core(mk, c, "at3", QT_s, KT_s, V_s, A, kv_group=2,
                    lm01_cols=lm01_cols, lm_bias_cols=lmb_cols,
                    deps=[prev, cm_tok] + stoks)
    ev, fin, rpst3 = resid_evac_factory(mk, c, "rp3", R, R, deps=[at3],
                                        rb_out=Rb_s)
    p4 = mm_pass(mk, c, "mmc3", movs, [{"ap": io["Wob"]}], NE, ev,
                 mov_tok=at3, deps=[at3])
    rtoks3 = fin()
    mk.release(heldlm)

    wsc4 = []
    ev, fin = ffn1_evac_factory(mk, c, "f3", H_s, sg_col, su_col, scope=wsc4)
    wb4 = mm_prealloc(mk, "mmf3", NE, 2, wsc4)
    r4 = ln_phase(mk, c, Rb_s, A, "ln4", src_bf16=True,
                  deps=[p4] + rtoks3, row_toks=rpst3["rb_rows"],
                  load_dep_half=[p4, p4])
    p5 = mm_pass(mk, c, "mmf3", movs, [{"ap": io["wg"]}, {"ap": io["wu"]}],
                 NIO, ev, mov_ts=r4["xn_ts"], psum_dep=r4["psum_free"],
                 wdep=[p4] + rtoks3, prealloc=wb4)
    htoks = fin()
    mk.release(wsc4)
    ffn2_phase(mk, c, "f4", H_s, io["wd"], R, io["yT"], sdn_col, A,
               deps=[p5] + htoks)

    mk.play()
    mk.close()
    return nc


def kernel(**inputs):
    nc = build_core_program()
    B = 8
    x = inputs["L0_attn_a_x"]
    in_maps = []
    for b in range(B):
        m = {
            "xT": np.ascontiguousarray(np.asarray(x[b]).T),
            "Wqa": inputs["L0_attn_a_Wq"], "bqa": inputs["L0_attn_a_bq"],
            "Wka": inputs["L0_attn_a_Wk"], "bka": inputs["L0_attn_a_bk"],
            "Wva": inputs["L0_attn_a_Wv"], "bva": inputs["L0_attn_a_bv"],
            "Wca": inputs["L0_attn_a_Wc"], "bca": inputs["L0_attn_a_bc"],
            "Wqb": inputs["L0_attn_b_Wq"], "Wkb": inputs["L0_attn_b_Wk"],
            "Wvb": inputs["L0_attn_b_Wv"], "Wob": inputs["L0_attn_b_Wo"],
            "wg": inputs["L0_ffn_w_gate"], "wu": inputs["L0_ffn_w_up"],
            "wd": inputs["L0_ffn_w_down"],
            "sg": inputs["L0_ffn_s_gate"], "su": inputs["L0_ffn_s_up"],
            "sdn": inputs["L0_ffn_s_down"],
        }
        in_maps.append({k: np.asarray(v) for k, v in m.items()})
    from concourse.bass_utils import run_bass_kernel_spmd
    res = run_bass_kernel_spmd(nc, in_maps, core_ids=list(range(B)))
    out = np.stack([res.results[b]["yT"].T for b in range(B)], axis=0)
    return np.ascontiguousarray(out.astype(np.float32))
